# version 40
# baseline (speedup 1.0000x reference)
from contextlib import ExitStack

import numpy as np
import concourse.bass as bass
import concourse.mybir as mybir
from concourse.bass_utils import run_bass_kernel_spmd

F32 = mybir.dt.float32
F32R = mybir.dt.float32r
AF = mybir.ActivationFunctionType
OP = mybir.AluOpType
AX = mybir.AxisListType.X

PART = 128
TT = 512
NT1 = 8
NT2 = 4
KC = 8
HPR = 8
NPE = 12
GSZ = 6
D = 1024
EPS_ATT = 1e-6
EPS_LN = 1e-5

ENGS = ("sync", "scalar", "vector", "tensor", "gpsimd")


class PSem:
    __slots__ = ("name", "n", "h")

    def __init__(self, name):
        self.name, self.n, self.h = name, 0, None


class FutMark:
    __slots__ = ("sv",)

    def __init__(self):
        self.sv = None


class Plan:
    def __init__(self):
        self.steps = {e: [] for e in ENGS}
        self.sems = []
        self.last_compute = {e: None for e in ENGS}

    def sem(self, name):
        s = PSem(name)
        self.sems.append(s)
        return s

    def emit(self, eng, waits, op, incs, is_dma=False):
        waits = list(waits)
        if not is_dma and eng in ("scalar", "vector", "gpsimd"):
            lc = self.last_compute[eng]
            if lc is not None:
                waits.append(lc)
        w = tuple(x for x in waits
                  if isinstance(x, FutMark) or x[1] > 0)
        for s, k in incs:
            s.n += k
        if not is_dma and incs:
            s0, _ = incs[0]
            self.last_compute[eng] = (s0, s0.n)
        self.steps[eng].append((w, op, tuple(incs)))


class Pool:

    def __init__(self, banks):
        self.free = list(banks)
        self.rel = {}

    def acquire(self):
        b = self.free.pop(0)
        return b, list(self.rel.get(id(b), ()))

    def release(self, b, waits):
        self.rel[id(b)] = tuple(waits)
        self.free.append(b)


_LAST_PLAN = None


def build(debug=None):
    global _LAST_PLAN
    nc = bass.Bass("TRN2", target_bir_lowering=False)

    x_full = nc.dram_tensor("x_full", [D, 4096], F32, kind="ExternalInput").ap()
    x_own = nc.dram_tensor("x_own", [D, 2048], F32, kind="ExternalInput").ap()
    wq = nc.dram_tensor("wq", [D, D], F32, kind="ExternalInput").ap()
    wk = nc.dram_tensor("wk", [D, D], F32, kind="ExternalInput").ap()
    wv = nc.dram_tensor("wv", [D, D], F32, kind="ExternalInput").ap()
    wo = nc.dram_tensor("wo", [D, D], F32, kind="ExternalInput").ap()
    proj_h = nc.dram_tensor("proj", [64, 64], F32, kind="ExternalInput").ap()
    bo_h = nc.dram_tensor("bo", [D, 1], F32, kind="ExternalInput").ap()
    up_all = nc.dram_tensor("up_all", [NPE, D, 512], F32, kind="ExternalInput").ap()
    dn_all = nc.dram_tensor("dn_all", [NPE, 512, D], F32, kind="ExternalInput").ap()
    gwt_h = nc.dram_tensor("gwt", [D, 8], F32, kind="ExternalInput").ap()
    gb_h = nc.dram_tensor("gb", [1, 8], F32, kind="ExternalInput").ap()
    ident_h = nc.dram_tensor("ident", [128, 128], F32, kind="ExternalInput").ap()
    sel16_h = nc.dram_tensor("sel16", [16, 1024], F32, kind="ExternalInput").ap()
    selcw_h = nc.dram_tensor("selcw", [8, 1024], F32, kind="ExternalInput").ap()
    y_h = nc.dram_tensor("y", [D, 2048], F32, kind="ExternalOutput").ap()
    scr_d = nc.dram_tensor("scr", [16, 512], F32, kind="Internal").ap()
    wkh_d = nc.dram_tensor("wkh", [D, D], F32R, kind="Internal").ap()
    wkl_d = nc.dram_tensor("wkl", [D, D], F32R, kind="Internal").ap()
    wvh_d = nc.dram_tensor("wvh", [D, D], F32R, kind="Internal").ap()
    wvl_d = nc.dram_tensor("wvl", [D, D], F32R, kind="Internal").ap()
    uph_d = nc.dram_tensor("uph", [NPE, D, 512], F32R, kind="Internal").ap()
    upl_d = nc.dram_tensor("upl", [NPE, D, 512], F32R, kind="Internal").ap()
    wqh_d = nc.dram_tensor("wqh", [D, D], F32R, kind="Internal").ap()
    wql_d = nc.dram_tensor("wql", [D, D], F32R, kind="Internal").ap()
    woh_d = nc.dram_tensor("woh", [D, D], F32R, kind="Internal").ap()
    wol_d = nc.dram_tensor("wol", [D, D], F32R, kind="Internal").ap()
    dbg_h = None
    if debug == "kv":
        dbg_h = nc.dram_tensor("dbg", [PART, 1040], F32, kind="ExternalOutput").ap()
    elif debug in ("x1", "o", "t1", "t2", "ln", "cwb", "ffn", "upa", "dng0"):
        dbg_h = nc.dram_tensor("dbg", [PART, 4096], F32, kind="ExternalOutput").ap()

    plan = Plan()
    _LAST_PLAN = plan
    s_x = [plan.sem("s_x0"), plan.sem("s_x1")]
    s_wa = [plan.sem("s_wa0"), plan.sem("s_wa1")]
    s_wb = [plan.sem("s_wb0"), plan.sem("s_wb1")]
    s_cb = plan.sem("s_cb")
    s_wq = plan.sem("s_wq")
    s_wo = plan.sem("s_wo")
    s_spl = plan.sem("s_spl")
    s_gv = [plan.sem("s_gv0"), plan.sem("s_gv1")]
    s_gv2 = [plan.sem("s_gv2a"), plan.sem("s_gv2b")]
    s_up = [plan.sem("s_up0"), plan.sem("s_up1")]
    s_ul = [plan.sem("s_ul0"), plan.sem("s_ul1")]
    s_sp2 = plan.sem("s_sp2")
    s_bc = [plan.sem("s_bc0"), plan.sem("s_bc1"),
            plan.sem("s_bc2"), plan.sem("s_bc3")]
    s_mr = plan.sem("s_mr")
    s_sm = plan.sem("s_sm")
    s_out = plan.sem("s_out")
    s_scw = plan.sem("s_scw")
    s_mm = plan.sem("s_mm")
    s_dve = plan.sem("s_dve")
    s_act = plan.sem("s_act")
    s_gp = plan.sem("s_gp")

    with ExitStack() as stack:
        ec = stack.enter_context
        xbuf_t = ec(nc.sbuf_tensor("xbuf", [PART, 4096], F32))
        abuf_t = ec(nc.sbuf_tensor("abuf", [PART, 4096], F32))
        bigA_t = ec(nc.sbuf_tensor("bigA", [PART, 12288], F32))
        xlb_t = ec(nc.sbuf_tensor("xlb", [PART, 2048], F32))
        selcw_t = ec(nc.sbuf_tensor("selcwsb", [8, 1024], F32))
        offA = nc.sbuf_base
        wbufA_t = ec(nc.sbuf_tensor("wbufA", [PART, 8192], F32))
        wbufB_t = ec(nc.sbuf_tensor("wbufB", [PART, 8192], F32))
        cwb_t = ec(nc.sbuf_tensor("cwb", [PART, 4096], F32))
        rdenb_t = ec(nc.sbuf_tensor("rdenb", [PART, 512], F32))
        meanb_t = ec(nc.sbuf_tensor("meanb", [PART, 512], F32))
        kvacc_t = ec(nc.sbuf_tensor("kvacc", [PART, 1040], F32))
        sel2_t = ec(nc.sbuf_tensor("sel2", [2, 128], F32))
        ones_row_t = ec(nc.sbuf_tensor("onesrow", [1, 128], F32))
        sqbuf_t = ec(nc.sbuf_tensor("sqbuf", [PART, 1024], F32))
        rbuf_t = ec(nc.sbuf_tensor("rbuf", [PART, 1024], F32))
        proj2_t = ec(nc.sbuf_tensor("proj2", [PART, 128], F32))
        ident_t = ec(nc.sbuf_tensor("identsb", [PART, 128], F32))
        kvbd_t = ec(nc.sbuf_tensor("kvbd", [PART, 1024], F32))
        ksum2_t = ec(nc.sbuf_tensor("ksum2", [PART, 16], F32))
        bocol_t = ec(nc.sbuf_tensor("bocol", [PART, 8], F32))
        gbrep_t = ec(nc.sbuf_tensor("gbrep", [PART, 32], F32))
        gwts_t = ec(nc.sbuf_tensor("gwts", [PART, 64], F32))
        ones1_t = ec(nc.sbuf_tensor("ones1", [PART, 1], F32))
        cwt_t = ec(nc.sbuf_tensor("cwt", [PART, 32], F32))
        scores_t = ec(nc.sbuf_tensor("scoresb", [PART, 32], F32))
        rsc_t = ec(nc.sbuf_tensor("rsc", [PART, 272], F32))
        cwTsb_t = ec(nc.sbuf_tensor("cwTsb", [8, 512], F32))
        dsb_t = ec(nc.sbuf_tensor("dsb", [16, 512], F32))
        rd16_t = ec(nc.sbuf_tensor("rd16", [16, 512], F32))
        sel16_t = ec(nc.sbuf_tensor("sel16sb", [16, 1024], F32))
        ksum16_t = ec(nc.sbuf_tensor("ksum16", [PART, 128], F32))
        psts = [ec(nc.psum_tensor("ps%d" % i, [PART, 512], F32)) for i in range(8)]
        hs = [ec(nc.semaphore("m%d" % i)) for i in range(len(plan.sems))]
        block = ec(nc.Block())
        for s, h in zip(plan.sems, hs):
            s.h = h

        xbuf = xbuf_t.ap()
        abuf = abuf_t.ap()
        bigA = bigA_t.ap()
        wbufA = wbufA_t.ap()
        wbufB = wbufB_t.ap()
        cwb = cwb_t.ap()
        rdenb = rdenb_t.ap()
        meanb = meanb_t.ap()
        kvacc = kvacc_t.ap()
        xhv = cwb_t.ap().bitcast(F32R)
        wAr = nc.alloc_sbuf_tensor_at(
            "wbufAr", [PART, 8192], F32R, offset=offA).ap()
        wBr = nc.alloc_sbuf_tensor_at(
            "wbufBr", [PART, 8192], F32R, offset=offA + 32768).ap()
        xlr = xlb_t.ap().bitcast(F32R)
        xlf = xlb_t.ap()
        selcw = selcw_t.ap()
        splh2v = [(bigA_t.ap()[:, 8256:9280],
                   bigA_t.ap().bitcast(F32R)[:, 8256:9280]),
                  (bigA_t.ap()[:, 0:1024],
                   bigA_t.ap().bitcast(F32R)[:, 0:1024])]
        splh, splh_r = splh2v[0]
        sel2 = sel2_t.ap()
        onesrow = ones_row_t.ap()
        sqbuf = sqbuf_t.ap()
        rbuf = rbuf_t.ap()
        proj2 = proj2_t.ap()
        identsb = ident_t.ap()
        kvbd = kvbd_t.ap()
        ksum2 = ksum2_t.ap()
        bocol = bocol_t.ap()
        gbrep = gbrep_t.ap()
        gwts = gwts_t.ap()
        kvsb = kvacc
        ones1 = ones1_t.ap()
        cwt = cwt_t.ap()
        scoresb = scores_t.ap()
        rsc = rsc_t.ap()
        cwTsb = cwTsb_t.ap()
        dsb = dsb_t.ap()
        rd16 = rd16_t.ap()
        sel16 = sel16_t.ap()
        ksum16 = ksum16_t.ap()
        lnst = rbuf_t.ap()[0:1, 0:1024]
        PS = [p.ap() for p in psts]

        def mm_group(pairs, out_slices, starts, stops, waits, inc_group=True,
                     per_mm_waits=None):
            seq = list(zip(pairs, out_slices, starts, stops,
                           per_mm_waits or [None] * len(pairs)))

            def op(t):
                ins = None
                last = {}
                for (lhsT, rhs), out, st, sp, pw in seq:
                    if pw is not None:
                        s_, v_ = pw.sv if isinstance(pw, FutMark) else pw
                        if v_ > 0 and last.get(s_.name, -1) < v_:
                            t.wait_ge(s_.h, v_)
                            last[s_.name] = v_
                    ins = t.matmul(out, lhsT, rhs, start=st, stop=sp)
                return ins
            plan.emit("tensor", waits, op, [(s_mm, 1)])
            return (s_mm, s_mm.n)

        def dma(eng, sem, out, in_, waits=()):
            def op(e):
                return e.dma_start(out=out, in_=in_)
            plan.emit(eng, waits, op, [(sem, 16)], is_dma=True)
            return (sem, sem.n)

        def dve(fn, waits=(), sem=s_dve):
            plan.emit("vector", waits, fn, [(sem, 1)])
            return (sem, sem.n)

        def act(fn, waits=()):
            plan.emit("scalar", waits, fn, [(s_act, 1)])
            return (s_act, s_act.n)

        def gps(fn, waits=()):
            plan.emit("gpsimd", waits, fn, [(s_gp, 1)])
            return (s_gp, s_gp.n)

        pool = Pool([PS[i] for i in range(8)])

        w_pj_ms = dve(lambda v: v.memset(proj2[:, :], 0.0))
        dma("scalar", s_sm, proj2[0:64, 0:64], proj_h[:, :], waits=[w_pj_ms])
        dma("scalar", s_sm, proj2[64:128, 64:128], proj_h[:, :], waits=[w_pj_ms])
        dma("scalar", s_sm, identsb[:, :], ident_h[:, :])
        dma("scalar", s_sm, sel16[:, :], sel16_h[:, :])
        dma("scalar", s_sm, selcw[:, :], selcw_h[:, :])
        for k in range(KC):
            dma("scalar", s_sm, gwts[:, k * 8:(k + 1) * 8],
                gwt_h[k * 128:(k + 1) * 128, :])
        for r4 in range(4):
            dma("scalar", s_sm, gbrep[:, r4 * 8:(r4 + 1) * 8],
                gb_h[0:1, :].broadcast_to([PART, 8]))
        for j in range(KC):
            dma("scalar", s_sm, bocol[:, j:j + 1], bo_h[j * 128:(j + 1) * 128, 0:1])
        setup_done = (s_sm, s_sm.n)
        dve(lambda v: v.memset(ones1[:, :], 1.0))
        dve(lambda v: v.memset(kvbd[:, :], 0.0))
        dve(lambda v: v.memset(sel2[:, :], 0.0))
        dve(lambda v: v.memset(sel2[0:1, 0:64], 1.0))
        dve(lambda v: v.memset(sel2[0:2, 64:128], 1.0))
        dve(lambda v: v.memset(sel2[0:1, 64:128], 0.0))
        dve(lambda v: v.memset(ksum16[:, :], 0.0))
        kvacc_ms = dve(lambda v: v.memset(kvacc[:, :], 0.0))
        dve(lambda v: v.memset(bigA[:, 4096:8256], 1.0))
        w_setup_dve = dve(lambda v: v.memset(onesrow[:, :], 1.0))

        stg = [bigA[:, 9280 + i * 1024:9280 + (i + 1) * 1024] for i in range(2)]
        stgr = [bigA_t.ap().bitcast(F32R)[:, 9280 + i * 1024:
                                          9280 + (i + 1) * 1024]
                for i in range(2)]
        wkh_ok = [None] * KC
        wvh_ok = [None] * KC

        xbuf_free = [[], []]
        x_ready = [None]
        x_half = [None, None]

        def load_x_tile(ti):
            done = None
            hmarks = [None, None]
            for k in range(KC):
                src = (x_full[k * 128:(k + 1) * 128, ti * TT:ti * TT + TT]
                       if ti < NT1 else
                       x_own[k * 128:(k + 1) * 128, (ti - NT1) * TT:
                             (ti - NT1) * TT + TT])
                done = dma("scalar", s_x[k // 4],
                           xbuf[:, k * 512:(k + 1) * 512],
                           src, waits=list(xbuf_free[k // 4])
                           if k % 4 == 0 else ())
                hmarks[k // 4] = done
            x_ready[0] = done
            x_half[0], x_half[1] = hmarks
            return done

        load_x_tile(0)

        p1_sxh = [dve(lambda v, d=xhv[:, k * 512:(k + 1) * 512],
                      s_=xbuf[:, k * 512:(k + 1) * 512]: v.tensor_copy(
                          out=d, in_=s_), waits=[x_half[k // 4]])
                  for k in range(KC)]
        p1_fgAB = [FutMark() for _ in range(KC)]
        p1_fvAB = [FutMark() for _ in range(4)]
        p1_kxl = [None] * KC
        p1_vxl = [None] * KC

        def emit_p1_kxl(k):
            w2 = [p1_sxh[k]] + ([p1_fgAB[k - 4]] if k >= 4 else [])
            p1_kxl[k] = dve(
                lambda v, d=xlr[:, (k % 4) * 512:(k % 4 + 1) * 512],
                a=xbuf[:, k * 512:(k + 1) * 512],
                b=cwb[:, k * 512:(k + 1) * 512]: v.tensor_tensor(
                    out=d, in0=a, in1=b, op=OP.subtract), waits=w2)

        def emit_p1_vxl(k):
            w2 = [p1_sxh[k],
                  p1_fgAB[k + 4] if k < 4 else p1_fvAB[k - 4]]
            p1_vxl[k] = dve(
                lambda v, d=xlr[:, (k % 4) * 512:(k % 4 + 1) * 512],
                a=xbuf[:, k * 512:(k + 1) * 512],
                b=cwb[:, k * 512:(k + 1) * 512]: v.tensor_tensor(
                    out=d, in0=a, in1=b, op=OP.subtract), waits=w2)

        for k in range(4):
            emit_p1_kxl(k)

        splh_free = []
        stg_free = [[], []]
        splh_free2 = [[], []]
        for wsel in range(2):
            wsrc, wbr_, hd, ld_, oks = (
                (wk, wAr, wkh_d, wkl_d, wkh_ok) if wsel == 0 else
                (wv, wBr, wvh_d, wvl_d, wvh_ok))
            for k in range(KC):
                c0, c1 = k * 1024, (k + 1) * 1024
                sg = k % 2
                sp, sp_r = splh2v[sg]
                ld0 = dma("sync", s_bc[sg], stg[sg],
                          wsrc[k * 128:(k + 1) * 128, :],
                          waits=list(stg_free[sg]))
                s1 = dve(lambda v, sg=sg, sp_r=sp_r: v.tensor_copy(
                    out=sp_r[:, 0:1024], in_=stg[sg]),
                    waits=[ld0] + list(splh_free2[sg]))
                s2 = dve(lambda v, c0=c0, c1=c1, sg=sg, wbr_=wbr_, sp=sp:
                         v.tensor_tensor(out=wbr_[:, c0:c1], in0=stg[sg],
                                         in1=sp[:, 0:1024], op=OP.subtract),
                         waits=[s1])
                stg_free[sg] = [s2]
                d3 = dma("scalar", s_spl, ld_[k * 128:(k + 1) * 128, :],
                         wbr_[:, c0:c1], waits=[s2])
                d5 = dma("scalar", s_spl, hd[k * 128:(k + 1) * 128, :],
                         sp_r[:, 0:1024], waits=())
                s4 = dve(lambda v, c0=c0, c1=c1, wbr_=wbr_, sp_r=sp_r:
                         v.tensor_copy(
                             out=wbr_[:, c0:c1], in_=sp_r[:, 0:1024]),
                         waits=[d5])
                oks[k] = s4
                splh_free2[sg] = [s4]
            if wsel == 0:
                for k2 in range(4, KC):
                    emit_p1_kxl(k2)
            else:
                for k2 in range(KC):
                    emit_p1_vxl(k2)
        splh_free = sum(splh_free2, [])
        wqo_pend = [(w_[k * 128:(k + 1) * 128, hf * 512:(hf + 1) * 512],
                     hd[k * 128:(k + 1) * 128, hf * 512:(hf + 1) * 512],
                     ld_[k * 128:(k + 1) * 128, hf * 512:(hf + 1) * 512])
                    for w_, hd, ld_ in ((wq, wqh_d, wql_d),
                                        (wo, woh_d, wol_d))
                    for k in range(KC) for hf in range(2)]
        wqo_pend += [(up_all[pe, k * 128:(k + 1) * 128, :],
                      uph_d[pe, k * 128:(k + 1) * 128, :],
                      upl_d[pe, k * 128:(k + 1) * 128, :])
                     for pe in range(NPE) for k in range(KC)]
        wqo_done_m = FutMark()
        bigr = bigA_t.ap().bitcast(F32R)
        w_in = [bigA[:, 9280 + s * 512:9280 + (s + 1) * 512] for s in range(4)]
        w_inr = [bigr[:, 9280 + s * 512:9280 + (s + 1) * 512] for s in range(4)]
        w_hi = [splh[:, hh * 512:(hh + 1) * 512] for hh in range(2)]
        w_hir = [splh_r[:, hh * 512:(hh + 1) * 512] for hh in range(2)]
        w_in_free = [list(stg_free[0]), list(stg_free[0]),
                     list(stg_free[1]), list(stg_free[1])]
        w_hi_free = [list(splh_free), list(splh_free)]
        wqo_c = [0]
        w_d5_prev = [[]]

        def emit_wqo_chunks(n):
            for _ in range(n):
                if not wqo_pend:
                    return
                src_ap, hd_ap, ld_ap = wqo_pend.pop(0)
                c = wqo_c[0]
                wqo_c[0] += 1
                sl = c % 4
                hh = c % 2
                ld0 = dma("scalar", s_bc[sl], w_in[sl], src_ap,
                          waits=w_in_free[sl])
                s1 = gps(lambda g_, sl=sl, hh=hh: g_.tensor_copy(
                    out=w_hir[hh], in_=w_in[sl]),
                    waits=[ld0] + w_hi_free[hh])
                s2 = gps(lambda g_, sl=sl, hh=hh: g_.tensor_tensor(
                    out=w_inr[sl], in0=w_in[sl], in1=w_hi[hh],
                    op=OP.subtract), waits=[s1])
                d3 = dma("gpsimd", s_sp2, ld_ap, w_inr[sl],
                         waits=[s2] + w_d5_prev[0])
                d5 = dma("gpsimd", s_sp2, hd_ap, w_hir[hh], waits=())
                w_d5_prev[0] = [d5]
                w_in_free[sl] = [d5]
                w_hi_free[hh] = [d5]
            if not wqo_pend and wqo_done_m.sv is None:
                wqo_done_m.sv = (s_sp2, s_sp2.n)
        wqo_split_done = wqo_done_m



        last_kv = None
        kf_free = (s_mm, 0)
        abuf_free = (s_mm, 0)
        vpad_free = (s_mm, 0)
        xh_free_k = [(s_mm, 0)] * KC
        xl_busy = [(s_mm, 0)] * 4
        kvacc_busy = kvacc_ms

        for s in range(NT1):
            xk = [xbuf[:, k * 512:(k + 1) * 512] for k in range(KC)]
            xh_k = [xhv[:, k * 512:(k + 1) * 512] for k in range(KC)]
            xhf_k = [cwb[:, k * 512:(k + 1) * 512] for k in range(KC)]
            xl_k = [xlr[:, (k % 4) * 512:(k % 4 + 1) * 512] for k in range(KC)]
            xrdy = x_ready[0]

            sxh = p1_sxh if s == 0 else pend_sxh

            def xl_split(k, slot_wait):
                return dve(lambda v, d=xl_k[k], a=xk[k], b=xhf_k[k]:
                           v.tensor_tensor(out=d, in0=a, in1=b,
                                           op=OP.subtract),
                           waits=[sxh[k], slot_wait])

            kbanks = []
            kbw = []
            for j in range(KC):
                b, w = pool.acquire()
                kbanks.append(b)
                kbw.append(w)
            wkl_ld = [None] * KC
            gAB = [None] * KC
            for k in range(KC):
                sxl = (p1_kxl[k] if s == 0
                       else xl_split(k, xl_busy[k % 4]))
                pairs = ([(wAr[:, k * 1024 + j * 128:k * 1024 + (j + 1) * 128],
                           xh_k[k]) for j in range(KC)] +
                         [(wAr[:, k * 1024 + j * 128:k * 1024 + (j + 1) * 128],
                           xl_k[k]) for j in range(KC)])
                outs = [kbanks[j][:, :] for j in range(KC)] * 2
                w0 = [sxl, wkh_ok[k]] + (sum(kbw, []) if k == 0 else [])
                if k == 0:
                    w0.append(abuf_free)
                gAB[k] = mm_group(
                    pairs=pairs, out_slices=outs,
                    starts=[k == 0] * KC + [False] * KC,
                    stops=[False] * (2 * KC), waits=w0)
                xl_busy[k % 4] = gAB[k]
                if s == 0:
                    p1_fgAB[k].sv = gAB[k]
                wkl_ld[k] = dma("sync", s_wa[k // 4],
                                wAr[:, k * 1024:(k + 1) * 1024],
                                wkl_d[k * 128:(k + 1) * 128, :], waits=[gAB[k]])
            gC = [None] * KC
            gC7 = [None] * KC
            for k in range(KC):
                if k < KC - 1:
                    gC[k] = mm_group(
                        pairs=[(wAr[:, k * 1024 + j * 128:
                                    k * 1024 + (j + 1) * 128],
                                xh_k[k]) for j in range(KC)],
                        out_slices=[kbanks[j][:, :] for j in range(KC)],
                        starts=[False] * KC, stops=[False] * KC,
                        waits=[wkl_ld[3 if k < 4 else 7]])
                else:
                    for j in range(KC):
                        gC7[j] = mm_group(
                            pairs=[(wAr[:, k * 1024 + j * 128:
                                        k * 1024 + (j + 1) * 128],
                                    xh_k[k])],
                            out_slices=[kbanks[j][:, :]],
                            starts=[False], stops=[True],
                            waits=[wkl_ld[7]] if j == 0 else [])
                    gC[k] = gC7[KC - 1]
            if s == NT1 - 1:
                kA_last = gC[KC - 1]
            kevac = []
            for j in range(KC):
                dst = abuf[:, j * 512:(j + 1) * 512]
                ev = act(lambda sc, b=kbanks[j], d=dst: sc.activation(
                    out=d, in_=b[:, :], func=AF.Identity),
                    waits=[gC7[j], abuf_free])
                kevac.append(ev)
                pool.release(kbanks[j], [ev])
            emit_wqo_chunks(4)
            if s < NT1 - 1:
                wkh_ld = [None] * KC
                for k in range(KC):
                    wkh_ld[k] = dma("sync", s_wa[k // 4],
                                    wAr[:, k * 1024:(k + 1) * 1024],
                                    wkh_d[k * 128:(k + 1) * 128, :],
                                    waits=[gC[k]])
                for k in range(KC):
                    wkh_ok[k] = wkh_ld[3 if k < 4 else 7]

            vbanks = []
            vbw = []
            for vi in range(KC):
                b, w = pool.acquire()
                vbanks.append(b)
                vbw.append(w)
            wvl_ld = [None] * KC
            vAB = [None] * KC
            for k in range(KC):
                sxl = (p1_vxl[k] if s == 0
                       else xl_split(k, xl_busy[k % 4]))
                if k == 3:
                    vsplit3 = sxl
                if k == KC - 1:
                    last_vsplit = sxl
                pairs = []
                outs = []
                for xp in (xh_k, xl_k):
                    for tc in range(4):
                        for jh in range(2):
                            pairs.append((
                                xp[k][:, tc * 128:(tc + 1) * 128],
                                wBr[:, k * 1024 + jh * 512:
                                    k * 1024 + jh * 512 + 512]))
                            outs.append(vbanks[tc * 2 + jh][:, :])
                w0 = [sxl, wvh_ok[k]] + (sum(vbw, []) if k == 0 else [])
                if k == 0:
                    w0.append(vpad_free)
                vAB[k] = mm_group(
                    pairs=pairs, out_slices=outs,
                    starts=[k == 0] * KC + [False] * KC,
                    stops=[False] * (2 * KC), waits=w0)
                xl_busy[k % 4] = vAB[k]
                if s == 0 and k < 4:
                    p1_fvAB[k].sv = vAB[k]
                wvl_ld[k] = dma("sync", s_gv2[k // 4],
                                wBr[:, k * 1024:(k + 1) * 1024],
                                wvl_d[k * 128:(k + 1) * 128, :], waits=[vAB[k]])
            vC = [None] * KC
            vC7 = [None] * KC
            for k in range(KC):
                if k < KC - 1:
                    pairs = []
                    outs = []
                    for tc in range(4):
                        for jh in range(2):
                            pairs.append((
                                xh_k[k][:, tc * 128:(tc + 1) * 128],
                                wBr[:, k * 1024 + jh * 512:
                                    k * 1024 + jh * 512 + 512]))
                            outs.append(vbanks[tc * 2 + jh][:, :])
                    vC[k] = mm_group(
                        pairs=pairs, out_slices=outs,
                        starts=[False] * KC, stops=[False] * KC,
                        waits=[wvl_ld[3 if k < 4 else 7]])
                    xh_free_k[k] = vC[k]
                else:
                    for vi in range(KC):
                        tc, jh = vi // 2, vi % 2
                        vC7[vi] = mm_group(
                            pairs=[(xh_k[k][:, tc * 128:(tc + 1) * 128],
                                    wBr[:, k * 1024 + jh * 512:
                                        k * 1024 + jh * 512 + 512])],
                            out_slices=[vbanks[vi][:, :]],
                            starts=[False], stops=[True],
                            waits=[wvl_ld[7]] if vi == 0 else [])
                    vC[k] = vC7[KC - 1]
                    xh_free_k[k] = vC7[KC - 1]
            if s == NT1 - 1:
                vB_last = vC[KC - 1]
            vevac = []
            for tc in range(4):
                vp = bigA[:, 4096 + tc * 1040:4096 + (tc + 1) * 1040]
                vp3 = vp.rearrange("p (a b) -> p a b", a=16)
                for jh in range(2):
                    bank = vbanks[tc * 2 + jh]
                    src3 = bank[:, :].rearrange("p (a b) -> p a b", a=8)
                    dst3 = vp3[:, jh * 8:(jh + 1) * 8, 0:64]
                    ev = dve(lambda v, s3=src3, d3=dst3: v.tensor_copy(
                        out=d3, in_=s3), waits=[vC7[tc * 2 + jh]])
                    vevac.append(ev)
                    pool.release(bank, [ev])
            emit_wqo_chunks(4)
            xbuf_free = [[vsplit3], [last_vsplit]]
            load_x_tile(s + 1)
            if s < NT1 - 1:
                wvh_ld = [None] * KC
                for k in range(KC):
                    wvh_ld[k] = dma("sync", s_gv2[k // 4],
                                    wBr[:, k * 1024:(k + 1) * 1024],
                                    wvh_d[k * 128:(k + 1) * 128, :],
                                    waits=[vC[k]])
                for k in range(KC):
                    wvh_ok[k] = wvh_ld[3 if k < 4 else 7]

            kfevac = []
            for hp in range(HPR):
                bank, bw = pool.acquire()
                g = mm_group(
                    pairs=[(abuf[:, hp * 512 + c * 128:hp * 512 + (c + 1) * 128],
                            proj2[:, :]) for c in range(4)],
                    out_slices=[bank[:, c * 128:(c + 1) * 128] for c in range(4)],
                    starts=[c == 0 for c in range(4)],
                    stops=[c == 3 for c in range(4)],
                    waits=[kevac[hp], setup_done, w_pj_ms] + bw +
                          ([kf_free] if hp == 0 else []))
                if s == NT1 - 1 and hp == HPR - 1:
                    kf_last = g
                dst = bigA[:, hp * 512:(hp + 1) * 512]
                ev = dve(lambda v, b=bank, d=dst: v.tensor_scalar(
                    out=d, in0=b[:, :], scalar1=0.0, scalar2=EPS_ATT,
                    op0=OP.max, op1=OP.add), waits=[g])
                kfevac.append(ev)
                pool.release(bank, [ev])
            emit_wqo_chunks(4)

            kvb = []
            kvw = []
            for b in range(3):
                bk, w = pool.acquire()
                kvb.append(bk)
                kvw.append(w)
            kv_last_b = [None] * 3
            for tc in range(4):
                for hp in range(HPR):
                    bidx = hp // 3
                    slot = hp % 3
                    g = mm_group(
                        pairs=[(bigA[:, hp * 512 + tc * 128:hp * 512 + (tc + 1) * 128],
                                bigA[:, 4096 + tc * 1040 + hp * 130:
                                     4096 + tc * 1040 + (hp + 1) * 130])],
                        out_slices=[kvb[bidx][:, slot * 130:(slot + 1) * 130]],
                        starts=[tc == 0 and slot == 0],
                        stops=[tc == 3 and hp in (2, 5, 7)],
                        waits=[kfevac[hp], vevac[tc * 2 + hp // 4]] +
                              (kvw[bidx] if (tc == 0 and slot == 0) else []))
                    kv_last_b[bidx] = g
            if s + 1 < NT1:
                pend_sxh = [dve(lambda v, d=xh_k[k], s_=xk[k]: v.tensor_copy(
                    out=d, in_=s_), waits=[x_half[k // 4], xh_free_k[k]])
                    for k in range(KC)]
            for b in range(3):
                off = b * 390
                w_ = 390 if b < 2 else 260
                ev = dve(lambda v, bk=kvb[b], off=off, w_=w_: v.tensor_tensor(
                    out=kvacc[:, off:off + w_], in0=bk[:, 0:w_],
                    in1=kvacc[:, off:off + w_], op=OP.add),
                    waits=[kv_last_b[b], kvacc_busy])
                pool.release(kvb[b], [ev])
                kvacc_busy = ev
                last_kv = ev
            emit_wqo_chunks(4)
            kf_free = kv_last_b[2]
            vpad_free = kv_last_b[2]
            abuf_free = kv_last_b[2]

        def emit_pend_splits_a(free_list):
            sh = [dve(lambda v, d=xhv[:, k * 512:(k + 1) * 512],
                      s_=xbuf[:, k * 512:(k + 1) * 512]: v.tensor_copy(
                          out=d, in_=s_),
                      waits=[x_half[k // 4], free_list[k]])
                  for k in range(KC)]
            sl = []
            for k in range(4):
                e_ = dve(lambda v, d=xlr[:, k * 512:(k + 1) * 512],
                         a=xbuf[:, k * 512:(k + 1) * 512],
                         b=cwb[:, k * 512:(k + 1) * 512]: v.tensor_tensor(
                             out=d, in0=a, in1=b, op=OP.subtract),
                         waits=[sh[k], xl_busy[k]])
                sl.append(e_)
                xl_busy[k] = e_
            return sh, sl

        def emit_pend_splits_b(sh, sl):
            fq = [FutMark() for _ in range(4)]
            for k in range(4, KC):
                e_ = dve(lambda v, d=xlr[:, (k % 4) * 512:(k % 4 + 1) * 512],
                         a=xbuf[:, k * 512:(k + 1) * 512],
                         b=cwb[:, k * 512:(k + 1) * 512]: v.tensor_tensor(
                             out=d, in0=a, in1=b, op=OP.subtract),
                         waits=[sh[k], fq[k - 4]])
                sl.append(e_)
                xl_busy[k % 4] = e_
            return sl, fq

        pend_sxh2, _sl = emit_pend_splits_a(xh_free_k)
        pend_sxl2, pend_fq = emit_pend_splits_b(pend_sxh2, _sl)

        ev_kv = [last_kv] * HPR
        w_ks_ms = dve(lambda v: v.memset(ksum2[:, :], 0.0))
        kvbd_done = None
        for hp in range(HPR):
            base = hp * 130
            kvbd_done = dve(lambda v, hp=hp, base=base: v.tensor_copy(
                out=kvbd[0:64, hp * 128:hp * 128 + 64],
                in_=kvsb[0:64, base:base + 64]), waits=[ev_kv[hp], w_setup_dve])
            kvbd_done = dve(lambda v, hp=hp, base=base: v.tensor_copy(
                out=kvbd[64:128, hp * 128 + 64:hp * 128 + 128],
                in_=kvsb[64:128, base + 65:base + 129]), waits=[ev_kv[hp]])
            kvbd_done = dve(lambda v, hp=hp, base=base: v.tensor_copy(
                out=ksum2[0:64, hp * 2:hp * 2 + 1],
                in_=kvsb[0:64, base + 64:base + 65]), waits=[ev_kv[hp], w_ks_ms])
            kvbd_done = dve(lambda v, hp=hp, base=base: v.tensor_copy(
                out=ksum2[64:128, hp * 2 + 1:hp * 2 + 2],
                in_=kvsb[64:128, base + 64:base + 65]), waits=[ev_kv[hp]])
        for hp in range(HPR):
            kvbd_done = dve(lambda v, hp=hp: v.tensor_copy(
                out=ksum16[:, hp * 16 + 2 * hp:hp * 16 + 2 * hp + 2],
                in_=ksum2[:, 2 * hp:2 * hp + 2]), waits=[kvbd_done])

        if debug == "kv":
            st = dma("scalar", s_out, dbg_h[:, :], kvsb[:, :], waits=[ev_kv[-1]])

            def fin(sc):
                return sc.nop()
            plan.emit("scalar", [st], fin, [])
            _run(block, plan)
            return nc

        waH_busy = [kA_last, kA_last]
        wbH_busy = [vB_last, vB_last]
        abuf_busy_j = [kf_last] * KC
        cwb_free_k = list(xh_free_k)
        scores_busy = (s_dve, 0)
        cwTsb_busy = (s_cb, 0)
        rdenb_busy = (s_dve, 0)
        meanb_busy = (s_dve, 0)
        sq_busy = [(s_mm, 0), (s_mm, 0)]
        rbuf_busy = [(s_gp, 0), (s_gp, 0), (s_gp, 0), (s_gp, 0)]
        r2s_free = kvbd_done
        dsb_busy = (s_dve, 0)
        out_done = []

        LN = {"mean": lnst[0:1, 0:512], "tmp": lnst[0:1, 512:1024]}

        for t in range(NT2):
            ti = NT1 + t
            xk = [xbuf[:, k * 512:(k + 1) * 512] for k in range(KC)]
            xrdy = x_ready[0]

            wq_done = []
            for k in range(KC):
                wq_done.append(dma(
                    "gpsimd", s_wq if k < 4 else s_cb,
                    wAr[:, k * 1024:(k + 1) * 1024],
                    wqh_d[k * 128:(k + 1) * 128, :],
                    waits=([waH_busy[k // 4]] +
                           ([wqo_split_done] if t == 0 else []))
                    if k % 4 == 0 else ()))
            wA_wh = [wq_done[3], wq_done[7]]

            wo_done = []
            for k in range(KC):
                wo_done.append(dma(
                    "gpsimd", s_wo if k < 4 else s_mr,
                    wBr[:, k * 1024:(k + 1) * 1024],
                    woh_d[k * 128:(k + 1) * 128, :],
                    waits=([wbH_busy[k // 4]] +
                           ([wqo_split_done] if t == 0 else []))
                    if k % 4 == 0 else ()))
            wB_wh = [wo_done[3], wo_done[7]]

            sxh = pend_sxh2

            def xl_split2(k, slot_wait, src_k):
                return dve(lambda v, d=xl_k[k], a=src_k[k], b=xhf_k[k]:
                           v.tensor_tensor(out=d, in0=a, in1=b,
                                           op=OP.subtract),
                           waits=[sxh[k], slot_wait])

            qbanks = []
            qbw = []
            for j in range(KC):
                b, w = pool.acquire()
                qbanks.append(b)
                qbw.append(w)
            wql_ld = [None] * KC
            qAB = [None] * KC
            for k in range(KC):
                sxl = pend_sxl2[k]
                pairs = ([(wAr[:, k * 1024 + j * 128:k * 1024 + (j + 1) * 128],
                           xh_k[k]) for j in range(KC)] +
                         [(wAr[:, k * 1024 + j * 128:k * 1024 + (j + 1) * 128],
                           xl_k[k]) for j in range(KC)])
                qAB[k] = mm_group(
                    pairs=pairs,
                    out_slices=[qbanks[j][:, :] for j in range(KC)] * 2,
                    starts=[k == 0] * KC + [False] * KC,
                    stops=[False] * (2 * KC),
                    waits=[sxl, wA_wh[k // 4]] +
                          (sum(qbw, []) if k == 0 else []))
                xl_busy[k % 4] = qAB[k]
                if k < 4:
                    pend_fq[k].sv = qAB[k]
                wql_ld[k] = dma("gpsimd", s_wq if k < 4 else s_cb,
                                wAr[:, k * 1024:(k + 1) * 1024],
                                wql_d[k * 128:(k + 1) * 128, :],
                                waits=[qAB[k]] +
                                      ([wqo_split_done] if t == 0 else []))
            qg7 = [None] * KC
            qp3 = [None] * KC
            for k in range(KC):
                if k < KC - 1:
                    qp3[k] = mm_group(
                        pairs=[(wAr[:, k * 1024 + j * 128:
                                    k * 1024 + (j + 1) * 128],
                                xh_k[k]) for j in range(KC)],
                        out_slices=[qbanks[j][:, :] for j in range(KC)],
                        starts=[False] * KC, stops=[False] * KC,
                        waits=[wql_ld[3 if k < 4 else 7]])
                else:
                    for j in range(KC):
                        qg7[j] = mm_group(
                            pairs=[(wAr[:, k * 1024 + j * 128:
                                        k * 1024 + (j + 1) * 128],
                                    xh_k[k])],
                            out_slices=[qbanks[j][:, :]],
                            starts=[False], stops=[True],
                            waits=[wql_ld[7]] if j == 0 else [])
                    qp3[k] = qg7[KC - 1]
            q_groups = []
            for j in range(KC):
                dst = abuf[:, j * 512:(j + 1) * 512]
                ev = act(lambda sc, b=qbanks[j], d=dst: sc.activation(
                    out=d, in_=b[:, :], func=AF.Identity),
                    waits=[qg7[j], abuf_busy_j[j]])
                pool.release(qbanks[j], [ev])
                q_groups.append((qg7[j], ev))
            waH_busy = [qp3[3], qg7[KC - 1]]

            uph_ld = {}

            def emit_uph(pr, gates, extra0=()):
                m = None
                for k in range(KC):
                    for h in range(2):
                        pe_ = 2 * pr + h
                        m = dma("sync", s_up[k // 4],
                                wAr[:, h * 4096 + k * 512:
                                    h * 4096 + (k + 1) * 512],
                                uph_d[pe_, k * 128:(k + 1) * 128, :],
                                waits=([gates[k]] + list(extra0 if k == 0
                                                         else ()))
                                if h == 0 else ())
                    if k == 3:
                        uph_ld[(pr, 0)] = m
                uph_ld[(pr, 1)] = m

            emit_uph(0, [qp3[4 + k // 2] for k in range(KC)],
                     extra0=([wqo_split_done] if t == 0 else ()))

            qf_ev = []
            for hp in range(HPR):
                bank, bw = pool.acquire()
                g = mm_group(
                    pairs=[(proj2[:, :], abuf[:, hp * 512:(hp + 1) * 512])],
                    out_slices=[bank[:, :]], starts=[True], stops=[True],
                    waits=[q_groups[hp][1]] + bw)
                dst = abuf[:, hp * 512:(hp + 1) * 512]
                ev = dve(lambda v, b=bank, d=dst: v.tensor_scalar(
                    out=d, in0=b[:, :], scalar1=0.0, scalar2=EPS_ATT,
                    op0=OP.max, op1=OP.add), waits=[g])
                qf_ev.append(ev)
                pool.release(bank, [ev])

            o_ev = []
            bank_den, bwd = pool.acquire()
            g_den = None
            for hp in range(HPR):
                qf_ap = abuf[:, hp * 512:(hp + 1) * 512]
                g_den = mm_group(
                    pairs=[(ksum16[:, hp * 16:(hp + 1) * 16], qf_ap)],
                    out_slices=[bank_den[0:16, :]],
                    starts=[hp == 0], stops=[hp == HPR - 1],
                    waits=[qf_ev[hp], kvbd_done] + (bwd if hp == 0 else []))
            e1 = dve(lambda v, b=bank_den: v.tensor_scalar(
                out=dsb[:, :], in0=b[0:16, :], scalar1=EPS_ATT, scalar2=None,
                op0=OP.add), waits=[g_den, dsb_busy])
            pool.release(bank_den, [e1])
            e2 = dve(lambda v: v.reciprocal(out=rd16[:, :], in_=dsb[:, :]),
                     waits=[e1])
            dsb_busy = e2
            num_banks = {}
            for hp in range(HPR):
                qf_ap = abuf[:, hp * 512:(hp + 1) * 512]
                bank_n, bwn = pool.acquire()
                g_num = mm_group(
                    pairs=[(kvbd[:, hp * 128:(hp + 1) * 128], qf_ap)],
                    out_slices=[bank_n[:, :]], starts=[True], stops=[True],
                    waits=[qf_ev[hp], kvbd_done] + bwn)
                bank_b, bwb = pool.acquire()
                g_bc = mm_group(
                    pairs=[(sel16[:, hp * 128:(hp + 1) * 128], rd16[:, :])],
                    out_slices=[bank_b[:, :]], starts=[True], stops=[True],
                    waits=[e2, setup_done] + bwb)
                rb = rdenb[:, 0:512]
                e3 = dve(lambda v, bb=bank_b, rb=rb: v.tensor_copy(
                    out=rb, in_=bb[:, :]), waits=[g_bc, rdenb_busy])
                pool.release(bank_b, [e3])
                ev = dve(lambda v, b=bank_n, rb=rb, d=qf_ap:
                         v.tensor_tensor(out=d, in0=b[:, :], in1=rb,
                                         op=OP.mult), waits=[g_num, e3])
                o_ev.append(ev)
                rdenb_busy = ev
                pool.release(bank_n, [ev])

            if debug == "o" and t == 0:
                st = None
                for j in range(KC):
                    st = dma("scalar", s_out, dbg_h[:, j * 512:(j + 1) * 512],
                             abuf[:, j * 512:(j + 1) * 512], waits=[o_ev[j]])
                plan.emit("scalar", [st], lambda sc: sc.nop(), [])
                _run(block, plan)
                return nc

            oh_k = xh_k
            sxo = []
            for k in range(KC):
                s1 = dve(lambda v, d=oh_k[k], s_=abuf[:, k * 512:(k + 1) * 512]:
                         v.tensor_copy(out=d, in_=s_),
                         waits=[o_ev[k], qp3[k]])
                sxo.append(s1)

            def ol_split(k, slot_wait):
                return dve(lambda v, d=xl_k[k],
                           a=abuf[:, k * 512:(k + 1) * 512], b=xhf_k[k]:
                           v.tensor_tensor(out=d, in0=a, in1=b,
                                           op=OP.subtract),
                           waits=[sxo[k], slot_wait])

            abanks = []
            abw = []
            for j in range(KC):
                b, w = pool.acquire()
                abanks.append(b)
                abw.append(w)
            wol_ld = [None] * KC
            aAB = [None] * KC
            for k in range(KC):
                sol = ol_split(k, xl_busy[k % 4])
                pairs = ([(wBr[:, k * 1024 + j * 128:k * 1024 + (j + 1) * 128],
                           oh_k[k]) for j in range(KC)] +
                         [(wBr[:, k * 1024 + j * 128:k * 1024 + (j + 1) * 128],
                           xl_k[k]) for j in range(KC)])
                aAB[k] = mm_group(
                    pairs=pairs,
                    out_slices=[abanks[j][:, :] for j in range(KC)] * 2,
                    starts=[k == 0] * KC, stops=[False] * (2 * KC),
                    waits=[sol, wB_wh[k // 4]] +
                          (sum(abw, []) if k == 0 else []))
                xl_busy[k % 4] = aAB[k]
                wol_ld[k] = dma("gpsimd", s_wo if k < 4 else s_mr,
                                wBr[:, k * 1024:(k + 1) * 1024],
                                wol_d[k * 128:(k + 1) * 128, :],
                                waits=[aAB[k]] +
                                      ([wqo_split_done] if t == 0 else []))
            ag7 = [None] * KC
            ap3 = [None] * KC
            for k in range(KC):
                if k < KC - 1:
                    ap3[k] = mm_group(
                        pairs=[(wBr[:, k * 1024 + j * 128:
                                    k * 1024 + (j + 1) * 128],
                                oh_k[k]) for j in range(KC)],
                        out_slices=[abanks[j][:, :] for j in range(KC)],
                        starts=[False] * KC, stops=[False] * KC,
                        waits=[wol_ld[3 if k < 4 else 7]])
                else:
                    for j in range(KC):
                        ag7[j] = mm_group(
                            pairs=[(wBr[:, k * 1024 + j * 128:
                                        k * 1024 + (j + 1) * 128],
                                    oh_k[k])],
                            out_slices=[abanks[j][:, :]],
                            starts=[False], stops=[True],
                            waits=[wol_ld[7]] if j == 0 else [])
                    ap3[k] = ag7[KC - 1]
            cwb_free_k = list(ap3)
            t1_ev = []
            for j in range(KC):
                dst = xk[j]
                ev = dve(lambda v, b=abanks[j], d=dst, j=j:
                         v.scalar_tensor_tensor(
                             out=d, in0=b[:, :], scalar=bocol[:, j:j + 1],
                             in1=d, op0=OP.add, op1=OP.add),
                         waits=[ag7[j], setup_done])
                t1_ev.append(ev)
                pool.release(abanks[j], [ev])
            wbH_busy = [ap3[3], ag7[KC - 1]]

            if debug == "t1" and t == 0:
                st = None
                for j in range(KC):
                    st = dma("scalar", s_out, dbg_h[:, j * 512:(j + 1) * 512],
                             xk[j], waits=[t1_ev[j]])
                plan.emit("scalar", [st], lambda sc: sc.nop(), [])
                _run(block, plan)
                return nc

            def layer_norm(src_chunks, t_ev_list, dst_chunks=None,
                           norm_eng=None):
                nonlocal sq_busy
                dst_chunks = dst_chunks or src_chunks
                bank1, bw1 = pool.acquire()
                bank2, bw2 = pool.acquire()
                sq_ev = [None] * KC
                g1 = g2 = None
                for j in range(KC):
                    sq = sqbuf[:, (j % 2) * 512:(j % 2 + 1) * 512]
                    sq_ev[j] = act(lambda sc, s_=src_chunks[j], q=sq: sc.activation(
                        out=q, in_=s_, func=AF.Square),
                        waits=[t_ev_list[j], sq_busy[j % 2]])
                    g1 = mm_group(
                        pairs=[(ones1[:, :], src_chunks[j])],
                        out_slices=[bank1[0:1, :]],
                        starts=[j == 0], stops=[j == KC - 1],
                        waits=[t_ev_list[j]] + (bw1 if j == 0 else []),
                        inc_group=(j == KC - 1))
                    g2 = mm_group(
                        pairs=[(ones1[:, :], sq)],
                        out_slices=[bank2[0:1, :]],
                        starts=[j == 0], stops=[j == KC - 1],
                        waits=[sq_ev[j]] + (bw2 if j == 0 else []),
                        inc_group=(j == KC - 1))
                    sq_busy[j % 2] = g2
                c = 1.0 / 1024.0
                e1 = dve(lambda v, b=bank1: v.tensor_scalar(
                    out=LN["mean"], in0=b[0:1, :], scalar1=c, scalar2=None,
                    op0=OP.mult), waits=[g1])
                e2 = dve(lambda v, b=bank2: v.tensor_scalar(
                    out=LN["tmp"], in0=b[0:1, :], scalar1=c, scalar2=None,
                    op0=OP.mult), waits=[g2])
                pool.release(bank1, [e1])
                pool.release(bank2, [e2])
                nonlocal rdenb_busy
                bank_m, bwm = pool.acquire()
                bank_r, bwr = pool.acquire()
                g_m = mm_group(
                    pairs=[(onesrow[:, :], LN["mean"])],
                    out_slices=[bank_m[:, :]], starts=[True], stops=[True],
                    waits=[e1, w_setup_dve] + bwm)
                e3 = dve(lambda v: v.tensor_tensor(
                    out=LN["mean"], in0=LN["mean"], in1=LN["mean"], op=OP.mult),
                    waits=[e1, g_m])
                e4 = dve(lambda v: v.tensor_tensor(
                    out=LN["tmp"], in0=LN["tmp"], in1=LN["mean"], op=OP.subtract),
                    waits=[e2, e3])
                e5 = dve(lambda v: v.tensor_scalar(
                    out=LN["tmp"], in0=LN["tmp"], scalar1=EPS_LN, scalar2=None,
                    op0=OP.add), waits=[e4])
                e6 = act(lambda sc: sc.activation(
                    out=LN["tmp"], in_=LN["tmp"], func=AF.Sqrt), waits=[e5])
                g_r = mm_group(
                    pairs=[(onesrow[:, :], LN["tmp"])],
                    out_slices=[bank_r[:, :]], starts=[True], stops=[True],
                    waits=[e6] + bwr)
                nonlocal meanb_busy
                e_mb = dve(lambda v, b=bank_m: v.tensor_copy(
                    out=meanb[:, :], in_=b[:, :]), waits=[g_m, meanb_busy])
                pool.release(bank_m, [e_mb])
                rsb = rdenb[:, 0:512]
                e7 = dve(lambda v, b=bank_r, rsb=rsb: v.reciprocal(
                    out=rsb, in_=b[:, :]), waits=[g_r, rdenb_busy])
                pool.release(bank_r, [e7])
                norm_ev = []
                n1_ev = []
                nfn = gps if norm_eng == "gpsimd" else dve
                for j in range(KC):
                    n1 = nfn(lambda v, s_=src_chunks[j], d=dst_chunks[j]:
                             v.tensor_tensor(
                                 out=d, in0=s_, in1=meanb[:, :],
                                 op=OP.subtract),
                             waits=[e_mb, t_ev_list[j]])
                    n2 = nfn(lambda v, d=dst_chunks[j], rsb=rsb:
                             v.tensor_tensor(out=d, in0=d, in1=rsb,
                                             op=OP.mult), waits=[e7, n1])
                    norm_ev.append(n2)
                    n1_ev.append(n1)
                meanb_busy = n1_ev[-1]
                rdenb_busy = norm_ev[-1]
                return norm_ev, n1_ev

            x1_ev, _ = layer_norm(xk, t1_ev)

            if debug == "ln" and t == 0:
                st = dma("scalar", s_out, dbg_h[0:1, 0:1024], lnst,
                         waits=[x1_ev[-1]])
                plan.emit("scalar", [st], lambda sc: sc.nop(), [])
                _run(block, plan)
                return nc

            if debug == "x1" and t == 0:
                st = None
                for j in range(KC):
                    st = dma("scalar", s_out, dbg_h[:, j * 512:(j + 1) * 512],
                             xk[j], waits=[x1_ev[j]])

                def fin(sc):
                    return sc.nop()
                plan.emit("scalar", [st], fin, [])
                _run(block, plan)
                return nc

            bank_lg, bwl = pool.acquire()
            g_lg = None
            for k in range(KC):
                g_lg = mm_group(
                    pairs=[(xk[k][:, tc * 128:(tc + 1) * 128],
                            gwts[:, k * 8:(k + 1) * 8]) for tc in range(4)],
                    out_slices=[bank_lg[:, tc * 8:(tc + 1) * 8]
                                for tc in range(4)],
                    starts=[k == 0 and tc == 0 for tc in range(4)],
                    stops=[k == KC - 1 and tc == 3 for tc in range(4)],
                    waits=[x1_ev[k], setup_done] + (bwl if k == 0 else []))
            sig = act(lambda sc, b=bank_lg: sc.activation(
                out=scoresb[:, :], in_=b[:, 0:32], func=AF.Sigmoid),
                waits=[g_lg, scores_busy])
            pool.release(bank_lg, [sig])

            P3 = lambda c0, c1, w: rsc[:, c0:c1].rearrange(
                "p (t c) -> p t c", t=4)
            sc3 = scoresb[:, :].rearrange("p (t c) -> p t c", t=4)
            e = dve(lambda v: v.tensor_tensor(
                out=rsc[:, 0:32], in0=scoresb[:, :], in1=gbrep[:, :],
                op=OP.add), waits=[sig, setup_done, scores_busy])
            sfc3 = P3(0, 32, 8)
            ps3 = P3(32, 80, 12)
            pr = [(0, 1), (2, 3), (0, 2), (1, 3), (0, 3), (1, 2)]
            for gidx in range(2):
                for p_, (a_, b_) in enumerate(pr):
                    c0, c1, co = 4 * gidx + a_, 4 * gidx + b_, 6 * gidx + p_
                    e = dve(lambda v, c0=c0, c1=c1, co=co: v.tensor_tensor(
                        out=ps3[:, :, co:co + 1], in0=sfc3[:, :, c0:c0 + 1],
                        in1=sfc3[:, :, c1:c1 + 1], op=OP.add), waits=[e])
            e = dve(lambda v: v.reduce_max(
                out=rsc[:, 80:88],
                in_=rsc[:, 32:80].rearrange("p (t g k) -> p (t g) k", t=4,
                                            g=2), axis=AX), waits=[e])
            gs3 = P3(80, 88, 2)
            e = dve(lambda v: v.tensor_tensor(
                out=rsc[:, 88:92], in0=gs3[:, :, 0:1].rearrange("p t c -> p (t c)"),
                in1=gs3[:, :, 1:2].rearrange("p t c -> p (t c)"),
                op=OP.is_ge), waits=[e])
            e = dve(lambda v: v.tensor_scalar(
                out=rsc[:, 92:96], in0=rsc[:, 88:92], scalar1=-1.0,
                scalar2=1.0, op0=OP.mult, op1=OP.add), waits=[e])
            m0b = P3(88, 92, 1)
            m1b = P3(92, 96, 1)
            msk3 = P3(96, 128, 8)
            e = dve(lambda v: v.tensor_tensor(
                out=msk3[:, :, 0:4], in0=sfc3[:, :, 0:4],
                in1=m0b.broadcast_to([PART, 4, 4]), op=OP.mult), waits=[e])
            e = dve(lambda v: v.tensor_tensor(
                out=msk3[:, :, 4:8], in0=sfc3[:, :, 4:8],
                in1=m1b.broadcast_to([PART, 4, 4]), op=OP.mult), waits=[e])
            e = dve(lambda v: v.reduce_max(
                out=rsc[:, 128:132], in_=msk3, axis=AX), waits=[e])
            mx1b = P3(128, 132, 1)
            e = dve(lambda v: v.tensor_tensor(
                out=P3(132, 164, 8), in0=msk3,
                in1=mx1b.broadcast_to([PART, 4, 8]), op=OP.is_equal),
                waits=[e])
            e = dve(lambda v: v.scalar_tensor_tensor(
                out=rsc[:, 164:196], in0=rsc[:, 132:164], scalar=-1e30,
                in1=rsc[:, 96:128], op0=OP.mult, op1=OP.add), waits=[e])
            e = dve(lambda v: v.reduce_max(
                out=rsc[:, 196:200], in_=P3(164, 196, 8), axis=AX), waits=[e])
            mx2b = P3(196, 200, 1)
            e = dve(lambda v: v.tensor_tensor(
                out=P3(200, 232, 8), in0=msk3,
                in1=mx2b.broadcast_to([PART, 4, 8]), op=OP.is_ge), waits=[e])
            e = dve(lambda v: v.tensor_tensor(
                out=rsc[:, 232:264], in0=rsc[:, 200:232], in1=scoresb[:, :],
                op=OP.mult), waits=[e])
            e = dve(lambda v: v.reduce_sum(
                out=rsc[:, 264:268], in_=P3(232, 264, 8), axis=AX), waits=[e])
            e = dve(lambda v: v.tensor_scalar(
                out=rsc[:, 264:268], in0=rsc[:, 264:268], scalar1=1e-20,
                scalar2=None, op0=OP.add), waits=[e])
            e = dve(lambda v: v.reciprocal(
                out=rsc[:, 268:272], in_=rsc[:, 264:268]), waits=[e])
            rv3 = P3(268, 272, 1)
            e = dve(lambda v: v.scalar_tensor_tensor(
                out=cwt[:, 0:32].rearrange("p (t c) -> p t c", t=4),
                in0=P3(232, 264, 8), scalar=2.5,
                in1=rv3.broadcast_to([PART, 4, 8]),
                op0=OP.mult, op1=OP.mult), waits=[e])
            cw_ev = [e] * 4
            scores_busy = cw_ev[-1]

            def emit_cwT():
                nonlocal cwTsb_busy
                bank_cw, bwc = pool.acquire()
                g_cw = None
                for tc in range(4):
                    g_cw = mm_group(
                        pairs=[(cwt[:, tc * 8:(tc + 1) * 8], identsb[:, :])],
                        out_slices=[bank_cw[0:8, tc * 128:(tc + 1) * 128]],
                        starts=[tc == 0], stops=[tc == 3],
                        waits=[cw_ev[tc], setup_done] +
                              (bwc if tc == 0 else []),
                        inc_group=(tc == 3))
                e_cwT = dve(lambda v, b=bank_cw: v.tensor_copy(
                    out=cwTsb[:, :], in_=b[0:8, :]), waits=[g_cw, cwTsb_busy])
                pool.release(bank_cw, [e_cwT])
                return e_cwT

            cwslots = [rdenb[:, 0:512], meanb[:, 0:512]]
            cw_busy = [rdenb_busy, meanb_busy]
            rbslots = [rbuf[:, 0:512], rbuf[:, 512:1024],
                       sqbuf[:, 0:512], sqbuf[:, 512:1024]]
            sxh1 = []
            for k in range(KC):
                sxh1.append(gps(lambda g_, d=xh_k[k], s_=xk[k]:
                                g_.tensor_copy(out=d, in_=s_),
                                waits=[x1_ev[k]]))
            stat = lambda h, k, i: wAr[:, h * 4096 + k * 512 + i * 128:
                                       h * 4096 + k * 512 + (i + 1) * 128]
            dn_evac = []
            g0_evac_last = None
            last_upP3 = None
            for g_i in range(2):
                pes = list(range(g_i * GSZ, (g_i + 1) * GSZ))
                rfree = r2s_free
                r2s_ready = {}
                for pl in range(3):
                    pr = g_i * 3 + pl
                    qA, qB = 2 * pl, 2 * pl + 1
                    peA, peB = pes[qA], pes[qB]
                    prev_rsp = rsp_next if pr > 0 else None
                    rsp_next = None
                    rsp = [None] * KC
                    for k in range(4):
                        rsp[k] = (prev_rsp[k] if prev_rsp is not None else
                                  gps(lambda g_, d=xl_k[k], a=xk[k],
                                      b=xhf_k[k]: g_.tensor_tensor(
                                          out=d, in0=a, in1=b,
                                          op=OP.subtract),
                                      waits=[sxh1[k], xl_busy[k % 4]]))
                    banks8 = []
                    bw8 = []
                    for b in range(8):
                        bk, w = pool.acquire()
                        banks8.append(bk)
                        bw8.append(w)
                    upAB = [None] * KC
                    upl_m03 = upl_m47 = None
                    for k in range(KC):
                        pairs = ([(stat(h, k, i), xh_k[k])
                                  for h in range(2) for i in range(4)] +
                                 [(stat(h, k, i), xl_k[k])
                                  for h in range(2) for i in range(4)])
                        upAB[k] = mm_group(
                            pairs=pairs,
                            out_slices=[banks8[h * 4 + i][:, :]
                                        for h in range(2) for i in range(4)] * 2,
                            starts=[k == 0] * 8 + [False] * 8,
                            stops=[False] * 16,
                            waits=[rsp[k], sxh1[k], uph_ld[(pr, k // 4)]] +
                                  (sum(bw8, []) if k == 0 else []))
                        xl_busy[k % 4] = upAB[k]
                        ul = None
                        for h, pe_ in enumerate((peA, peB)):
                            ul = dma("gpsimd", s_ul[k // 4],
                                     wAr[:, h * 4096 + k * 512:
                                         h * 4096 + (k + 1) * 512],
                                     upl_d[pe_, k * 128:(k + 1) * 128, :],
                                     waits=[upAB[k]] if h == 0 else ())
                        if k == 3:
                            upl_m03 = ul
                        if k == KC - 1:
                            upl_m47 = ul
                        if k < 4:
                            rsp[k + 4] = gps(
                                lambda g_, d=xl_k[k + 4], a=xk[k + 4],
                                b=xhf_k[k + 4]: g_.tensor_tensor(
                                    out=d, in0=a, in1=b, op=OP.subtract),
                                waits=[sxh1[k + 4], upAB[k]])
                    if pr + 1 < 6:
                        rsp_next = [gps(lambda g_, d=xl_k[k], a=xk[k],
                                        b=xhf_k[k]: g_.tensor_tensor(
                                            out=d, in0=a, in1=b,
                                            op=OP.subtract),
                                        waits=[sxh1[k], xl_busy[k % 4]])
                                    for k in range(4)]
                    upP3 = [None] * KC
                    g7 = [None] * 8
                    for k in range(4):
                        upP3[k] = mm_group(
                            pairs=[(stat(h, k, i), xh_k[k])
                                   for h in range(2) for i in range(4)],
                            out_slices=[banks8[h * 4 + i][:, :]
                                        for h in range(2) for i in range(4)],
                            starts=[False] * 8, stops=[False] * 8,
                            waits=[upl_m03] if k == 0 else [])
                    for b in range(8):
                        g7[b] = mm_group(
                            pairs=[(stat(b // 4, k, b % 4), xh_k[k])
                                   for k in range(4, KC)],
                            out_slices=[banks8[b][:, :]] * 4,
                            starts=[False] * 4,
                            stops=[False, False, False, True],
                            waits=[upl_m47] if b == 0 else [])
                    for k in range(4, KC):
                        upP3[k] = g7[7]
                    last_upP3 = upP3
                    a1s = [None] * 8
                    slots8 = []
                    for b in range(8):
                        q_ = (qA, qB)[b // 4]
                        slots8.append(bigA[:, (q_ * 4 + b % 4) * 512:
                                           (q_ * 4 + b % 4 + 1) * 512])
                    for b in range(8):
                        a1 = dve(lambda v, bk=banks8[b], sl=slots8[b]:
                                 v.tensor_scalar(out=sl, in0=bk[:, :],
                                                 scalar1=0.0, scalar2=None,
                                                 op0=OP.max),
                                 waits=[g7[b], rfree])
                        a1s[b] = a1
                        pool.release(banks8[b], [a1])
                    if g_i == 0 and pl == 0:
                        e_cwT = emit_cwT()
                    ec_m = {}
                    for h, pe_ in enumerate((peA, peB)):
                        if pe_ < 8:
                            bank_cw2, bwc2 = pool.acquire()
                            gg = mm_group(
                                pairs=[(selcw[:, pe_ * 128:(pe_ + 1) * 128],
                                        cwTsb[:, :])],
                                out_slices=[bank_cw2[:, :]], starts=[True],
                                stops=[True],
                                waits=[e_cwT, setup_done, cwTsb_busy] + bwc2)
                            cwTsb_busy = gg
                            ecv = dve(lambda v, b=bank_cw2, d=cwslots[h]:
                                      v.tensor_copy(out=d, in_=b[:, :]),
                                      waits=[gg, cw_busy[h]])
                            pool.release(bank_cw2, [ecv])
                            ec_m[h] = ecv
                    for b in range(8):
                        q_ = (qA, qB)[b // 4]
                        pe_ = (peA, peB)[b // 4]
                        slot = slots8[b]
                        r = rbslots[b % 4]
                        if pe_ < 8:
                            d1 = dve(lambda v, sl=slot, r=r,
                                     c=cwslots[b // 4]: v.tensor_tensor(
                                         out=r, in0=sl, in1=c, op=OP.mult),
                                     waits=[a1s[b], ec_m[b // 4],
                                            rbuf_busy[b % 4]])
                            cw_busy[b // 4] = d1
                            g1 = gps(lambda g_, sl=slot, r=r: g_.tensor_tensor(
                                out=sl, in0=sl, in1=r, op=OP.mult),
                                waits=[d1])
                            rbuf_busy[b % 4] = g1
                        else:
                            g1 = gps(lambda g_, sl=slot: g_.tensor_tensor(
                                out=sl, in0=sl, in1=sl, op=OP.mult),
                                waits=[a1s[b]])
                        r2s_ready[(q_, b % 4)] = g1
                    if pr + 1 < 6:
                        emit_uph(pr + 1, upP3)
                waH_busy = [last_upP3[KC - 1], last_upP3[KC - 1]]

                if debug == "upa" and t == 0 and g_i == 0:
                    st = None
                    for sl8 in range(KC):
                        st = dma("scalar", s_out,
                                 dbg_h[:, sl8 * 512:(sl8 + 1) * 512],
                                 bigA[:, sl8 * 512:(sl8 + 1) * 512],
                                 waits=[r2s_ready[(sl8 // 4, sl8 % 4)]])
                    plan.emit("scalar", [st], lambda sc: sc.nop(), [])
                    _run(block, plan)
                    return nc

                banks_j = []
                bws_j = []
                for j in range(KC):
                    bank, bw = pool.acquire()
                    banks_j.append(bank)
                    bws_j.append(bw)
                dn_loaded = {}
                dn_last_mm = {}

                def emit_dn_load(qi):
                    pe = pes[qi]
                    h = qi % 2
                    w = [wbH_busy[h]] if qi < 2 else [dn_last_mm[qi - 2]]
                    ld = None
                    for i in range(4):
                        ld = dma("scalar", s_wb[h],
                                 wbufB[:, h * 4096 + i * 1024:h * 4096 + (i + 1) * 1024],
                                 dn_all[pe, i * 128:(i + 1) * 128, :],
                                 waits=w if i == 0 else ())
                    dn_loaded[qi] = ld

                emit_dn_load(0)
                emit_dn_load(1)
                dn_last = {}
                for qi, pe in enumerate(pes):
                    h = qi % 2
                    for j in range(KC):
                        g = mm_group(
                            pairs=[(wbufB[:, h * 4096 + i * 1024 + j * 128:
                                          h * 4096 + i * 1024 + (j + 1) * 128],
                                    bigA[:, (qi * 4 + i) * 512:
                                         (qi * 4 + i + 1) * 512])
                                   for i in range(4)],
                            out_slices=[banks_j[j][:, :]] * 4,
                            starts=[(qi == 0 and i == 0) for i in range(4)],
                            stops=[(qi == GSZ - 1 and i == 3) for i in range(4)],
                            waits=[dn_loaded[qi], r2s_ready[(qi, 3)]] +
                                  [r2s_ready[(qi, i)] for i in range(3)] +
                                  (bws_j[j] if qi == 0 else []))
                        if qi == GSZ - 1:
                            dn_last[j] = g
                    dn_last_mm[qi] = (s_mm, s_mm.n)
                    if qi + 2 < GSZ:
                        emit_dn_load(qi + 2)
                wbH_busy = [dn_last_mm[GSZ - 2], dn_last_mm[GSZ - 1]]
                evs = []
                for j in range(KC):
                    if g_i == 0:
                        ev = dve(lambda v, b=banks_j[j], j=j, xkj=xk[j]:
                                 v.tensor_tensor(
                                     out=abuf[:, j * 512:(j + 1) * 512],
                                     in0=b[:, :], in1=xkj, op=OP.add),
                                 waits=[dn_last[j]])
                    else:
                        ev = dve(lambda v, b=banks_j[j], j=j: v.tensor_tensor(
                            out=abuf[:, j * 512:(j + 1) * 512], in0=b[:, :],
                            in1=abuf[:, j * 512:(j + 1) * 512], op=OP.add),
                            waits=[dn_last[j]])
                    evs.append(ev)
                    pool.release(banks_j[j], [ev])
                if g_i == 0:
                    g0_evac_last = evs[-1]
                dn_evac = evs
                r2s_free = (s_mm, s_mm.n)

                if debug == "dng0" and t == 0 and g_i == 0:
                    st = None
                    for j8 in range(KC):
                        st = dma("scalar", s_out,
                                 dbg_h[:, j8 * 512:(j8 + 1) * 512],
                                 abuf[:, j8 * 512:(j8 + 1) * 512],
                                 waits=[evs[j8]])
                    plan.emit("scalar", [st], lambda sc: sc.nop(), [])
                    _run(block, plan)
                    return nc

            cwb_free_k = list(last_upP3)
            rdenb_busy = cw_busy[0]
            meanb_busy = cw_busy[1]
            sq_busy = [rbuf_busy[2], rbuf_busy[3]]

            xbuf_free = [[(s_mm, s_mm.n), g0_evac_last]] * 2
            if ti + 1 < NT1 + NT2:
                load_x_tile(ti + 1)

            if debug == "ffn" and t == 0:
                st = None
                for j in range(KC):
                    st = dma("scalar", s_out, dbg_h[:, j * 512:(j + 1) * 512],
                             abuf[:, j * 512:(j + 1) * 512], waits=[dn_evac[j]])
                plan.emit("scalar", [st], lambda sc: sc.nop(), [])
                _run(block, plan)
                return nc

            if t + 1 < NT2:
                pend_sxh2, _sl = emit_pend_splits_a(cwb_free_k)

            t2_chunks = [abuf[:, j * 512:(j + 1) * 512] for j in range(KC)]
            x2_ev, n1s = layer_norm(t2_chunks, dn_evac)

            st = None
            for j in range(KC):
                st = dma("scalar", s_out, y_h[j * 128:(j + 1) * 128,
                                              t * 512:(t + 1) * 512],
                         abuf[:, j * 512:(j + 1) * 512], waits=[x2_ev[j]])
            out_done.append(st)
            abuf_busy_j = [st] * KC
            if t + 1 < NT2:
                pend_sxl2, pend_fq = emit_pend_splits_b(pend_sxh2, _sl)

        def fin(sc):
            return sc.nop()
        plan.emit("scalar", [out_done[-1]], fin, [])

        _run(block, plan)
    return nc


def _run(block, plan):
    def runner(name):
        def go(eng):
            last = {}
            for waits, op, incs in plan.steps[name]:
                for w in waits:
                    s, v = w.sv if isinstance(w, FutMark) else w
                    if v > 0 and last.get(s.name, -1) < v:
                        eng.wait_ge(s.h, v)
                        last[s.name] = v
                ins = op(eng)
                for s, k in incs:
                    ins.then_inc(s.h, k)
        return go

    block.sync(runner("sync"))
    block.scalar(runner("scalar"))
    block.vector(runner("vector"))
    block.tensor(runner("tensor"))
    block.gpsimd(runner("gpsimd"))


def _prep_inputs(L0_attn_x, L0_attn_Wq, L0_attn_Wk, L0_attn_Wv, L0_attn_proj,
                 L0_attn_Wo, L0_attn_bo, L0_ffn_gate_weight, L0_ffn_gate_bias,
                 L0_ffn_up_weights, L0_ffn_down_weights, L0_ffn_shared_up_weight,
                 L0_ffn_shared_down_weight):
    f = lambda a: np.ascontiguousarray(np.asarray(a, dtype=np.float32))
    x = f(L0_attn_x)
    wq, wk, wv, wo = f(L0_attn_Wq), f(L0_attn_Wk), f(L0_attn_Wv), f(L0_attn_Wo)
    proj = f(L0_attn_proj)
    bo = f(L0_attn_bo).reshape(1024, 1)
    gw = f(L0_ffn_gate_weight)
    gb = f(L0_ffn_gate_bias).reshape(1, 8)
    upw = f(L0_ffn_up_weights)
    dnw = f(L0_ffn_down_weights)
    sup = f(L0_ffn_shared_up_weight)
    sdn = f(L0_ffn_shared_down_weight)

    upT = np.transpose(upw, (0, 2, 1))
    supT = np.ascontiguousarray(sup.T)
    up_all = np.stack([*upT] + [supT[:, p * 512:(p + 1) * 512] for p in range(4)])
    dnT = np.transpose(dnw, (0, 2, 1))
    sdnT = np.ascontiguousarray(sdn.T)
    dn_all = np.stack([*dnT] + [sdnT[p * 512:(p + 1) * 512, :] for p in range(4)])
    gwT = np.ascontiguousarray(gw.T)
    ident = np.eye(128, dtype=np.float32)
    sel16 = np.zeros((16, 1024), dtype=np.float32)
    for hp in range(8):
        sel16[2 * hp, hp * 128:hp * 128 + 64] = 1.0
        sel16[2 * hp + 1, hp * 128 + 64:hp * 128 + 128] = 1.0
    selcw = np.zeros((8, 1024), dtype=np.float32)
    for e in range(8):
        selcw[e, e * 128:(e + 1) * 128] = 1.0

    in_maps = []
    for core in range(8):
        b, h = core // 2, core % 2
        xT = np.ascontiguousarray(x[b].T)
        in_maps.append({
            "x_full": xT,
            "x_own": np.ascontiguousarray(xT[:, h * 2048:(h + 1) * 2048]),
            "wq": wq, "wk": wk, "wv": wv, "wo": wo,
            "proj": proj, "bo": bo,
            "up_all": np.ascontiguousarray(up_all),
            "dn_all": np.ascontiguousarray(dn_all),
            "gwt": gwT, "gb": gb, "ident": ident, "sel16": sel16,
            "selcw": selcw,
        })
    return in_maps


def kernel(**inputs):
    in_maps = _prep_inputs(**inputs)
    nc = build()
    res = run_bass_kernel_spmd(nc, in_maps, core_ids=list(range(8)))
    ys = [res.results[c]["y"] for c in range(8)]
    out = np.stack([
        np.concatenate([ys[2 * b], ys[2 * b + 1]], axis=1).T for b in range(4)
    ]).astype(np.float32)
    return np.ascontiguousarray(out)


# version 41
# speedup vs baseline: 1.1185x; 1.1185x over previous
from contextlib import ExitStack

import numpy as np
import concourse.bass as bass
import concourse.mybir as mybir
from concourse.bass_utils import run_bass_kernel_spmd

F32 = mybir.dt.float32
F32R = mybir.dt.float32r
AF = mybir.ActivationFunctionType
OP = mybir.AluOpType
AX = mybir.AxisListType.X

PART = 128
TT = 512
NT1 = 8
NT2 = 4
KC = 8
HPR = 8
NPE = 12
GSZ = 6
D = 1024
EPS_ATT = 1e-6
EPS_LN = 1e-5

ENGS = ("sync", "scalar", "vector", "tensor", "gpsimd")


class PSem:
    __slots__ = ("name", "n", "h")

    def __init__(self, name):
        self.name, self.n, self.h = name, 0, None


class FutMark:
    __slots__ = ("sv",)

    def __init__(self):
        self.sv = None


class Plan:
    def __init__(self):
        self.steps = {e: [] for e in ENGS}
        self.sems = []
        self.last_compute = {e: None for e in ENGS}

    def sem(self, name):
        s = PSem(name)
        self.sems.append(s)
        return s

    def emit(self, eng, waits, op, incs, is_dma=False):
        waits = list(waits)
        if not is_dma and eng in ("scalar", "vector", "gpsimd"):
            lc = self.last_compute[eng]
            if lc is not None:
                waits.append(lc)
        w = tuple(x for x in waits
                  if isinstance(x, FutMark) or x[1] > 0)
        for s, k in incs:
            s.n += k
        if not is_dma and incs:
            s0, _ = incs[0]
            self.last_compute[eng] = (s0, s0.n)
        self.steps[eng].append((w, op, tuple(incs)))


class Pool:

    def __init__(self, banks):
        self.free = list(banks)
        self.rel = {}

    def acquire(self):
        b = self.free.pop(0)
        return b, list(self.rel.get(id(b), ()))

    def release(self, b, waits):
        self.rel[id(b)] = tuple(waits)
        self.free.append(b)


_LAST_PLAN = None


def build(debug=None):
    global _LAST_PLAN
    nc = bass.Bass("TRN2", target_bir_lowering=False)

    x_full = nc.dram_tensor("x_full", [D, 4096], F32, kind="ExternalInput").ap()
    x_own = nc.dram_tensor("x_own", [D, 2048], F32, kind="ExternalInput").ap()
    wq = nc.dram_tensor("wq", [D, D], F32, kind="ExternalInput").ap()
    wk = nc.dram_tensor("wk", [D, D], F32, kind="ExternalInput").ap()
    wv = nc.dram_tensor("wv", [D, D], F32, kind="ExternalInput").ap()
    wo = nc.dram_tensor("wo", [D, D], F32, kind="ExternalInput").ap()
    proj_h = nc.dram_tensor("proj", [64, 64], F32, kind="ExternalInput").ap()
    bo_h = nc.dram_tensor("bo", [D, 1], F32, kind="ExternalInput").ap()
    up_all = nc.dram_tensor("up_all", [NPE, D, 512], F32, kind="ExternalInput").ap()
    dn_all = nc.dram_tensor("dn_all", [NPE, 512, D], F32, kind="ExternalInput").ap()
    gwt_h = nc.dram_tensor("gwt", [D, 8], F32, kind="ExternalInput").ap()
    gb_h = nc.dram_tensor("gb", [1, 8], F32, kind="ExternalInput").ap()
    ident_h = nc.dram_tensor("ident", [128, 128], F32, kind="ExternalInput").ap()
    sel16_h = nc.dram_tensor("sel16", [16, 1024], F32, kind="ExternalInput").ap()
    selcw_h = nc.dram_tensor("selcw", [8, 1024], F32, kind="ExternalInput").ap()
    y_h = nc.dram_tensor("y", [D, 2048], F32, kind="ExternalOutput").ap()
    scr_d = nc.dram_tensor("scr", [16, 512], F32, kind="Internal").ap()
    wkh_d = nc.dram_tensor("wkh", [D, D], F32R, kind="Internal").ap()
    wkl_d = nc.dram_tensor("wkl", [D, D], F32R, kind="Internal").ap()
    wvh_d = nc.dram_tensor("wvh", [D, D], F32R, kind="Internal").ap()
    wvl_d = nc.dram_tensor("wvl", [D, D], F32R, kind="Internal").ap()
    uph_d = nc.dram_tensor("uph", [NPE, D, 512], F32R, kind="Internal").ap()
    upl_d = nc.dram_tensor("upl", [NPE, D, 512], F32R, kind="Internal").ap()
    wqh_d = nc.dram_tensor("wqh", [D, D], F32R, kind="Internal").ap()
    wql_d = nc.dram_tensor("wql", [D, D], F32R, kind="Internal").ap()
    woh_d = nc.dram_tensor("woh", [D, D], F32R, kind="Internal").ap()
    wol_d = nc.dram_tensor("wol", [D, D], F32R, kind="Internal").ap()
    dbg_h = None
    if debug == "kv":
        dbg_h = nc.dram_tensor("dbg", [PART, 1040], F32, kind="ExternalOutput").ap()
    elif debug in ("x1", "o", "t1", "t2", "ln", "cwb", "ffn", "upa", "dng0"):
        dbg_h = nc.dram_tensor("dbg", [PART, 4096], F32, kind="ExternalOutput").ap()

    plan = Plan()
    _LAST_PLAN = plan
    s_x = [plan.sem("s_x0"), plan.sem("s_x1")]
    s_wa = [plan.sem("s_wa0"), plan.sem("s_wa1")]
    s_wb = [plan.sem("s_wb0"), plan.sem("s_wb1")]
    s_cb = plan.sem("s_cb")
    s_wq = plan.sem("s_wq")
    s_wo = plan.sem("s_wo")
    s_spl = plan.sem("s_spl")
    s_gv = [plan.sem("s_gv0"), plan.sem("s_gv1")]
    s_gv2 = [plan.sem("s_gv2a"), plan.sem("s_gv2b")]
    s_up = [plan.sem("s_up0"), plan.sem("s_up1")]
    s_ul = [plan.sem("s_ul0"), plan.sem("s_ul1")]
    s_sp2 = plan.sem("s_sp2")
    s_bc = [plan.sem("s_bc0"), plan.sem("s_bc1"),
            plan.sem("s_bc2"), plan.sem("s_bc3")]
    s_mr = plan.sem("s_mr")
    s_sm = plan.sem("s_sm")
    s_out = plan.sem("s_out")
    s_scw = plan.sem("s_scw")
    s_mm = plan.sem("s_mm")
    s_dve = plan.sem("s_dve")
    s_act = plan.sem("s_act")
    s_gp = plan.sem("s_gp")

    with ExitStack() as stack:
        ec = stack.enter_context
        xbuf_t = ec(nc.sbuf_tensor("xbuf", [PART, 4096], F32))
        abuf_t = ec(nc.sbuf_tensor("abuf", [PART, 4096], F32))
        bigA_t = ec(nc.sbuf_tensor("bigA", [PART, 12288], F32))
        xlb_t = ec(nc.sbuf_tensor("xlb", [PART, 2048], F32))
        selcw_t = ec(nc.sbuf_tensor("selcwsb", [8, 1024], F32))
        offA = nc.sbuf_base
        wbufA_t = ec(nc.sbuf_tensor("wbufA", [PART, 8192], F32))
        wbufB_t = ec(nc.sbuf_tensor("wbufB", [PART, 8192], F32))
        cwb_t = ec(nc.sbuf_tensor("cwb", [PART, 4096], F32))
        rdenb_t = ec(nc.sbuf_tensor("rdenb", [PART, 512], F32))
        meanb_t = ec(nc.sbuf_tensor("meanb", [PART, 512], F32))
        kvacc_t = ec(nc.sbuf_tensor("kvacc", [PART, 1040], F32))
        sel2_t = ec(nc.sbuf_tensor("sel2", [2, 128], F32))
        ones_row_t = ec(nc.sbuf_tensor("onesrow", [1, 128], F32))
        sqbuf_t = ec(nc.sbuf_tensor("sqbuf", [PART, 1024], F32))
        rbuf_t = ec(nc.sbuf_tensor("rbuf", [PART, 1024], F32))
        proj2_t = ec(nc.sbuf_tensor("proj2", [PART, 128], F32))
        ident_t = ec(nc.sbuf_tensor("identsb", [PART, 128], F32))
        kvbd_t = ec(nc.sbuf_tensor("kvbd", [PART, 1024], F32))
        ksum2_t = ec(nc.sbuf_tensor("ksum2", [PART, 16], F32))
        bocol_t = ec(nc.sbuf_tensor("bocol", [PART, 8], F32))
        gbrep_t = ec(nc.sbuf_tensor("gbrep", [PART, 32], F32))
        gwts_t = ec(nc.sbuf_tensor("gwts", [PART, 64], F32))
        ones1_t = ec(nc.sbuf_tensor("ones1", [PART, 1], F32))
        cwt_t = ec(nc.sbuf_tensor("cwt", [PART, 32], F32))
        scores_t = ec(nc.sbuf_tensor("scoresb", [PART, 32], F32))
        rsc_t = ec(nc.sbuf_tensor("rsc", [PART, 272], F32))
        cwTsb_t = ec(nc.sbuf_tensor("cwTsb", [8, 512], F32))
        dsb_t = ec(nc.sbuf_tensor("dsb", [16, 512], F32))
        rd16_t = ec(nc.sbuf_tensor("rd16", [16, 512], F32))
        sel16_t = ec(nc.sbuf_tensor("sel16sb", [16, 1024], F32))
        ksum16_t = ec(nc.sbuf_tensor("ksum16", [PART, 128], F32))
        psts = [ec(nc.psum_tensor("ps%d" % i, [PART, 512], F32)) for i in range(8)]
        hs = [ec(nc.semaphore("m%d" % i)) for i in range(len(plan.sems))]
        block = ec(nc.Block())
        for s, h in zip(plan.sems, hs):
            s.h = h

        xbuf = xbuf_t.ap()
        abuf = abuf_t.ap()
        bigA = bigA_t.ap()
        wbufA = wbufA_t.ap()
        wbufB = wbufB_t.ap()
        cwb = cwb_t.ap()
        rdenb = rdenb_t.ap()
        meanb = meanb_t.ap()
        kvacc = kvacc_t.ap()
        xhv = cwb_t.ap().bitcast(F32R)
        wAr = nc.alloc_sbuf_tensor_at(
            "wbufAr", [PART, 8192], F32R, offset=offA).ap()
        wBr = nc.alloc_sbuf_tensor_at(
            "wbufBr", [PART, 8192], F32R, offset=offA + 32768).ap()
        xlr = xlb_t.ap().bitcast(F32R)
        xlf = xlb_t.ap()
        selcw = selcw_t.ap()
        splh2v = [(bigA_t.ap()[:, 8256:9280],
                   bigA_t.ap().bitcast(F32R)[:, 8256:9280]),
                  (bigA_t.ap()[:, 0:1024],
                   bigA_t.ap().bitcast(F32R)[:, 0:1024])]
        splh, splh_r = splh2v[0]
        sel2 = sel2_t.ap()
        onesrow = ones_row_t.ap()
        sqbuf = sqbuf_t.ap()
        rbuf = rbuf_t.ap()
        proj2 = proj2_t.ap()
        identsb = ident_t.ap()
        kvbd = kvbd_t.ap()
        ksum2 = ksum2_t.ap()
        bocol = bocol_t.ap()
        gbrep = gbrep_t.ap()
        gwts = gwts_t.ap()
        kvsb = kvacc
        ones1 = ones1_t.ap()
        cwt = cwt_t.ap()
        scoresb = scores_t.ap()
        rsc = rsc_t.ap()
        cwTsb = cwTsb_t.ap()
        dsb = dsb_t.ap()
        rd16 = rd16_t.ap()
        sel16 = sel16_t.ap()
        ksum16 = ksum16_t.ap()
        lnst = rbuf_t.ap()[0:1, 0:1024]
        PS = [p.ap() for p in psts]

        def mm_group(pairs, out_slices, starts, stops, waits, inc_group=True,
                     per_mm_waits=None):
            seq = list(zip(pairs, out_slices, starts, stops,
                           per_mm_waits or [None] * len(pairs)))

            def op(t):
                ins = None
                last = {}
                for (lhsT, rhs), out, st, sp, pw in seq:
                    if pw is not None:
                        s_, v_ = pw.sv if isinstance(pw, FutMark) else pw
                        if v_ > 0 and last.get(s_.name, -1) < v_:
                            t.wait_ge(s_.h, v_)
                            last[s_.name] = v_
                    ins = t.matmul(out, lhsT, rhs, start=st, stop=sp)
                return ins
            plan.emit("tensor", waits, op, [(s_mm, 1)])
            return (s_mm, s_mm.n)

        def dma(eng, sem, out, in_, waits=()):
            def op(e):
                return e.dma_start(out=out, in_=in_)
            plan.emit(eng, waits, op, [(sem, 16)], is_dma=True)
            return (sem, sem.n)

        def dve(fn, waits=(), sem=s_dve):
            plan.emit("vector", waits, fn, [(sem, 1)])
            return (sem, sem.n)

        def act(fn, waits=()):
            plan.emit("scalar", waits, fn, [(s_act, 1)])
            return (s_act, s_act.n)

        def gps(fn, waits=()):
            plan.emit("gpsimd", waits, fn, [(s_gp, 1)])
            return (s_gp, s_gp.n)

        pool = Pool([PS[i] for i in range(8)])

        w_pj_ms = dve(lambda v: v.memset(proj2[:, :], 0.0))
        dma("scalar", s_sm, proj2[0:64, 0:64], proj_h[:, :], waits=[w_pj_ms])
        dma("scalar", s_sm, proj2[64:128, 64:128], proj_h[:, :], waits=[w_pj_ms])
        dma("scalar", s_sm, identsb[:, :], ident_h[:, :])
        dma("scalar", s_sm, sel16[:, :], sel16_h[:, :])
        dma("scalar", s_sm, selcw[:, :], selcw_h[:, :])
        for k in range(KC):
            dma("scalar", s_sm, gwts[:, k * 8:(k + 1) * 8],
                gwt_h[k * 128:(k + 1) * 128, :])
        for r4 in range(4):
            dma("scalar", s_sm, gbrep[:, r4 * 8:(r4 + 1) * 8],
                gb_h[0:1, :].broadcast_to([PART, 8]))
        for j in range(KC):
            dma("scalar", s_sm, bocol[:, j:j + 1], bo_h[j * 128:(j + 1) * 128, 0:1])
        setup_done = (s_sm, s_sm.n)
        dve(lambda v: v.memset(ones1[:, :], 1.0))
        dve(lambda v: v.memset(kvbd[:, :], 0.0))
        dve(lambda v: v.memset(sel2[:, :], 0.0))
        dve(lambda v: v.memset(sel2[0:1, 0:64], 1.0))
        dve(lambda v: v.memset(sel2[0:2, 64:128], 1.0))
        dve(lambda v: v.memset(sel2[0:1, 64:128], 0.0))
        dve(lambda v: v.memset(ksum16[:, :], 0.0))
        kvacc_ms = dve(lambda v: v.memset(kvacc[:, :], 0.0))
        dve(lambda v: v.memset(bigA[:, 4096:8256], 1.0))
        w_setup_dve = dve(lambda v: v.memset(onesrow[:, :], 1.0))

        stg = [bigA[:, 9280 + i * 1024:9280 + (i + 1) * 1024] for i in range(2)]
        stgr = [bigA_t.ap().bitcast(F32R)[:, 9280 + i * 1024:
                                          9280 + (i + 1) * 1024]
                for i in range(2)]
        wkh_ok = [None] * KC
        wvh_ok = [None] * KC

        xbuf_free = [[], []]
        x_ready = [None]
        x_half = [None, None]

        def load_x_tile(ti):
            done = None
            hmarks = [None, None]
            for k in range(KC):
                src = (x_full[k * 128:(k + 1) * 128, ti * TT:ti * TT + TT]
                       if ti < NT1 else
                       x_own[k * 128:(k + 1) * 128, (ti - NT1) * TT:
                             (ti - NT1) * TT + TT])
                done = dma("scalar", s_x[k // 4],
                           xbuf[:, k * 512:(k + 1) * 512],
                           src, waits=list(xbuf_free[k // 4])
                           if k % 4 == 0 else ())
                hmarks[k // 4] = done
            x_ready[0] = done
            x_half[0], x_half[1] = hmarks
            return done

        load_x_tile(0)

        p1_sxh = [dve(lambda v, d=xhv[:, k * 512:(k + 1) * 512],
                      s_=xbuf[:, k * 512:(k + 1) * 512]: v.tensor_copy(
                          out=d, in_=s_), waits=[x_half[k // 4]])
                  for k in range(KC)]
        p1_fgAB = [FutMark() for _ in range(KC)]
        p1_fvAB = [FutMark() for _ in range(4)]
        p1_kxl = [None] * KC
        p1_vxl = [None] * KC

        def emit_p1_kxl(k):
            w2 = [p1_sxh[k]] + ([p1_fgAB[k - 4]] if k >= 4 else [])
            p1_kxl[k] = dve(
                lambda v, d=xlr[:, (k % 4) * 512:(k % 4 + 1) * 512],
                a=xbuf[:, k * 512:(k + 1) * 512],
                b=cwb[:, k * 512:(k + 1) * 512]: v.tensor_tensor(
                    out=d, in0=a, in1=b, op=OP.subtract), waits=w2)

        def emit_p1_vxl(k):
            w2 = [p1_sxh[k],
                  p1_fgAB[k + 4] if k < 4 else p1_fvAB[k - 4]]
            p1_vxl[k] = dve(
                lambda v, d=xlr[:, (k % 4) * 512:(k % 4 + 1) * 512],
                a=xbuf[:, k * 512:(k + 1) * 512],
                b=cwb[:, k * 512:(k + 1) * 512]: v.tensor_tensor(
                    out=d, in0=a, in1=b, op=OP.subtract), waits=w2)

        for k in range(4):
            emit_p1_kxl(k)

        splh_free = []
        stg_free = [[], []]
        splh_free2 = [[], []]
        for wsel in range(2):
            wsrc, wbr_, hd, ld_, oks = (
                (wk, wAr, wkh_d, wkl_d, wkh_ok) if wsel == 0 else
                (wv, wBr, wvh_d, wvl_d, wvh_ok))
            for k in range(KC):
                c0, c1 = k * 1024, (k + 1) * 1024
                sg = k % 2
                sp, sp_r = splh2v[sg]
                ld0 = dma("sync", s_bc[sg], stg[sg],
                          wsrc[k * 128:(k + 1) * 128, :],
                          waits=list(stg_free[sg]))
                s1 = dve(lambda v, sg=sg, sp_r=sp_r: v.tensor_copy(
                    out=sp_r[:, 0:1024], in_=stg[sg]),
                    waits=[ld0] + list(splh_free2[sg]))
                s2 = dve(lambda v, c0=c0, c1=c1, sg=sg, wbr_=wbr_, sp=sp:
                         v.tensor_tensor(out=wbr_[:, c0:c1], in0=stg[sg],
                                         in1=sp[:, 0:1024], op=OP.subtract),
                         waits=[s1])
                stg_free[sg] = [s2]
                d3 = dma("scalar", s_spl, ld_[k * 128:(k + 1) * 128, :],
                         wbr_[:, c0:c1], waits=[s2])
                d5 = dma("scalar", s_spl, hd[k * 128:(k + 1) * 128, :],
                         sp_r[:, 0:1024], waits=())
                s4 = dve(lambda v, c0=c0, c1=c1, wbr_=wbr_, sp_r=sp_r:
                         v.tensor_copy(
                             out=wbr_[:, c0:c1], in_=sp_r[:, 0:1024]),
                         waits=[d5])
                oks[k] = s4
                splh_free2[sg] = [s4]
            if wsel == 0:
                for k2 in range(4, KC):
                    emit_p1_kxl(k2)
            else:
                for k2 in range(KC):
                    emit_p1_vxl(k2)
        splh_free = sum(splh_free2, [])
        wqo_pend = [(w_[k * 128:(k + 1) * 128, hf * 512:(hf + 1) * 512],
                     hd[k * 128:(k + 1) * 128, hf * 512:(hf + 1) * 512],
                     ld_[k * 128:(k + 1) * 128, hf * 512:(hf + 1) * 512])
                    for w_, hd, ld_ in ((wq, wqh_d, wql_d),
                                        (wo, woh_d, wol_d))
                    for k in range(KC) for hf in range(2)]
        wqo_pend += [(up_all[pe, k * 128:(k + 1) * 128, :],
                      uph_d[pe, k * 128:(k + 1) * 128, :],
                      upl_d[pe, k * 128:(k + 1) * 128, :])
                     for pe in range(NPE) for k in range(KC)]
        wqo_done_m = FutMark()
        wq32_done_m = FutMark()
        bigr = bigA_t.ap().bitcast(F32R)
        w_in = [bigA[:, 9280 + s * 512:9280 + (s + 1) * 512] for s in range(4)]
        w_inr = [bigr[:, 9280 + s * 512:9280 + (s + 1) * 512] for s in range(4)]
        w_hi = [splh[:, hh * 512:(hh + 1) * 512] for hh in range(2)]
        w_hir = [splh_r[:, hh * 512:(hh + 1) * 512] for hh in range(2)]
        w_in_free = [list(stg_free[0]), list(stg_free[0]),
                     list(stg_free[1]), list(stg_free[1])]
        w_hi_free = [list(splh_free), list(splh_free)]
        wqo_c = [0]
        w_d5_prev = [[]]

        def emit_wqo_chunks(n):
            for _ in range(n):
                if not wqo_pend:
                    return
                src_ap, hd_ap, ld_ap = wqo_pend.pop(0)
                c = wqo_c[0]
                wqo_c[0] += 1
                sl = c % 4
                hh = c % 2
                ld0 = dma("scalar", s_bc[sl], w_in[sl], src_ap,
                          waits=w_in_free[sl])
                s1 = gps(lambda g_, sl=sl, hh=hh: g_.tensor_copy(
                    out=w_hir[hh], in_=w_in[sl]),
                    waits=[ld0] + w_hi_free[hh])
                s2 = gps(lambda g_, sl=sl, hh=hh: g_.tensor_tensor(
                    out=w_inr[sl], in0=w_in[sl], in1=w_hi[hh],
                    op=OP.subtract), waits=[s1])
                d3 = dma("gpsimd", s_sp2, ld_ap, w_inr[sl],
                         waits=[s2] + w_d5_prev[0])
                d5 = dma("gpsimd", s_sp2, hd_ap, w_hir[hh], waits=())
                w_d5_prev[0] = [d5]
                w_in_free[sl] = [d5]
                w_hi_free[hh] = [d5]
                if wqo_c[0] == 32 and wq32_done_m.sv is None:
                    wq32_done_m.sv = (s_sp2, s_sp2.n)
            if not wqo_pend and wqo_done_m.sv is None:
                wqo_done_m.sv = (s_sp2, s_sp2.n)
        wqo_split_done = wqo_done_m
        wq32_done = wq32_done_m



        last_kv = None
        kf_free = (s_mm, 0)
        abuf_free = (s_mm, 0)
        vpad_free = (s_mm, 0)
        xh_free_k = [(s_mm, 0)] * KC
        xl_busy = [(s_mm, 0)] * 4
        kvacc_busy = kvacc_ms

        for s in range(NT1):
            xk = [xbuf[:, k * 512:(k + 1) * 512] for k in range(KC)]
            xh_k = [xhv[:, k * 512:(k + 1) * 512] for k in range(KC)]
            xhf_k = [cwb[:, k * 512:(k + 1) * 512] for k in range(KC)]
            xl_k = [xlr[:, (k % 4) * 512:(k % 4 + 1) * 512] for k in range(KC)]
            xrdy = x_ready[0]

            sxh = p1_sxh if s == 0 else pend_sxh

            def xl_split(k, slot_wait):
                return dve(lambda v, d=xl_k[k], a=xk[k], b=xhf_k[k]:
                           v.tensor_tensor(out=d, in0=a, in1=b,
                                           op=OP.subtract),
                           waits=[sxh[k], slot_wait])

            kbanks = []
            kbw = []
            for j in range(KC):
                b, w = pool.acquire()
                kbanks.append(b)
                kbw.append(w)
            wkl_ld = [None] * KC
            gAB = [None] * KC
            for k in range(KC):
                sxl = (p1_kxl[k] if s == 0
                       else xl_split(k, xl_busy[k % 4]))
                pairs = ([(wAr[:, k * 1024 + j * 128:k * 1024 + (j + 1) * 128],
                           xh_k[k]) for j in range(KC)] +
                         [(wAr[:, k * 1024 + j * 128:k * 1024 + (j + 1) * 128],
                           xl_k[k]) for j in range(KC)])
                outs = [kbanks[j][:, :] for j in range(KC)] * 2
                w0 = [sxl, wkh_ok[k]] + (sum(kbw, []) if k == 0 else [])
                if k == 0:
                    w0.append(abuf_free)
                gAB[k] = mm_group(
                    pairs=pairs, out_slices=outs,
                    starts=[k == 0] * KC + [False] * KC,
                    stops=[False] * (2 * KC), waits=w0)
                xl_busy[k % 4] = gAB[k]
                if s == 0:
                    p1_fgAB[k].sv = gAB[k]
                wkl_ld[k] = dma("sync", s_wa[k // 4],
                                wAr[:, k * 1024:(k + 1) * 1024],
                                wkl_d[k * 128:(k + 1) * 128, :], waits=[gAB[k]])
            gC = [None] * KC
            gC7 = [None] * KC
            for k in range(KC):
                if k < KC - 1:
                    gC[k] = mm_group(
                        pairs=[(wAr[:, k * 1024 + j * 128:
                                    k * 1024 + (j + 1) * 128],
                                xh_k[k]) for j in range(KC)],
                        out_slices=[kbanks[j][:, :] for j in range(KC)],
                        starts=[False] * KC, stops=[False] * KC,
                        waits=[wkl_ld[3 if k < 4 else 7]])
                else:
                    for j in range(KC):
                        gC7[j] = mm_group(
                            pairs=[(wAr[:, k * 1024 + j * 128:
                                        k * 1024 + (j + 1) * 128],
                                    xh_k[k])],
                            out_slices=[kbanks[j][:, :]],
                            starts=[False], stops=[True],
                            waits=[wkl_ld[7]] if j == 0 else [])
                    gC[k] = gC7[KC - 1]
            if s == NT1 - 1:
                kA_last = gC[KC - 1]
            kevac = []
            for j in range(KC):
                dst = abuf[:, j * 512:(j + 1) * 512]
                ev = act(lambda sc, b=kbanks[j], d=dst: sc.activation(
                    out=d, in_=b[:, :], func=AF.Identity),
                    waits=[gC7[j], abuf_free])
                kevac.append(ev)
                pool.release(kbanks[j], [ev])
            emit_wqo_chunks(4)
            if s < NT1 - 1:
                wkh_ld = [None] * KC
                for k in range(KC):
                    wkh_ld[k] = dma("sync", s_wa[k // 4],
                                    wAr[:, k * 1024:(k + 1) * 1024],
                                    wkh_d[k * 128:(k + 1) * 128, :],
                                    waits=[gC[k]])
                for k in range(KC):
                    wkh_ok[k] = wkh_ld[3 if k < 4 else 7]

            vbanks = []
            vbw = []
            for vi in range(KC):
                b, w = pool.acquire()
                vbanks.append(b)
                vbw.append(w)
            wvl_ld = [None] * KC
            vAB = [None] * KC
            for k in range(KC):
                sxl = (p1_vxl[k] if s == 0
                       else xl_split(k, xl_busy[k % 4]))
                if k == 3:
                    vsplit3 = sxl
                if k == KC - 1:
                    last_vsplit = sxl
                pairs = []
                outs = []
                for xp in (xh_k, xl_k):
                    for tc in range(4):
                        for jh in range(2):
                            pairs.append((
                                xp[k][:, tc * 128:(tc + 1) * 128],
                                wBr[:, k * 1024 + jh * 512:
                                    k * 1024 + jh * 512 + 512]))
                            outs.append(vbanks[tc * 2 + jh][:, :])
                w0 = [sxl, wvh_ok[k]] + (sum(vbw, []) if k == 0 else [])
                if k == 0:
                    w0.append(vpad_free)
                vAB[k] = mm_group(
                    pairs=pairs, out_slices=outs,
                    starts=[k == 0] * KC + [False] * KC,
                    stops=[False] * (2 * KC), waits=w0)
                xl_busy[k % 4] = vAB[k]
                if s == 0 and k < 4:
                    p1_fvAB[k].sv = vAB[k]
                wvl_ld[k] = dma("sync", s_gv2[k // 4],
                                wBr[:, k * 1024:(k + 1) * 1024],
                                wvl_d[k * 128:(k + 1) * 128, :], waits=[vAB[k]])
            vC = [None] * KC
            vC7 = [None] * KC
            for k in range(KC):
                if k < KC - 1:
                    pairs = []
                    outs = []
                    for tc in range(4):
                        for jh in range(2):
                            pairs.append((
                                xh_k[k][:, tc * 128:(tc + 1) * 128],
                                wBr[:, k * 1024 + jh * 512:
                                    k * 1024 + jh * 512 + 512]))
                            outs.append(vbanks[tc * 2 + jh][:, :])
                    vC[k] = mm_group(
                        pairs=pairs, out_slices=outs,
                        starts=[False] * KC, stops=[False] * KC,
                        waits=[wvl_ld[3 if k < 4 else 7]])
                    xh_free_k[k] = vC[k]
                else:
                    for vi in range(KC):
                        tc, jh = vi // 2, vi % 2
                        vC7[vi] = mm_group(
                            pairs=[(xh_k[k][:, tc * 128:(tc + 1) * 128],
                                    wBr[:, k * 1024 + jh * 512:
                                        k * 1024 + jh * 512 + 512])],
                            out_slices=[vbanks[vi][:, :]],
                            starts=[False], stops=[True],
                            waits=[wvl_ld[7]] if vi == 0 else [])
                    vC[k] = vC7[KC - 1]
                    xh_free_k[k] = vC7[KC - 1]
            if s == NT1 - 1:
                vB_last = vC[KC - 1]
            vevac = []
            for tc in range(4):
                vp = bigA[:, 4096 + tc * 1040:4096 + (tc + 1) * 1040]
                vp3 = vp.rearrange("p (a b) -> p a b", a=16)
                for jh in range(2):
                    bank = vbanks[tc * 2 + jh]
                    src3 = bank[:, :].rearrange("p (a b) -> p a b", a=8)
                    dst3 = vp3[:, jh * 8:(jh + 1) * 8, 0:64]
                    ev = dve(lambda v, s3=src3, d3=dst3: v.tensor_copy(
                        out=d3, in_=s3), waits=[vC7[tc * 2 + jh]])
                    vevac.append(ev)
                    pool.release(bank, [ev])
            emit_wqo_chunks(4)
            xbuf_free = [[vsplit3], [last_vsplit]]
            load_x_tile(s + 1)
            if s < NT1 - 1:
                wvh_ld = [None] * KC
                for k in range(KC):
                    wvh_ld[k] = dma("sync", s_gv2[k // 4],
                                    wBr[:, k * 1024:(k + 1) * 1024],
                                    wvh_d[k * 128:(k + 1) * 128, :],
                                    waits=[vC[k]])
                for k in range(KC):
                    wvh_ok[k] = wvh_ld[3 if k < 4 else 7]

            kfevac = []
            for hp in range(HPR):
                bank, bw = pool.acquire()
                g = mm_group(
                    pairs=[(abuf[:, hp * 512 + c * 128:hp * 512 + (c + 1) * 128],
                            proj2[:, :]) for c in range(4)],
                    out_slices=[bank[:, c * 128:(c + 1) * 128] for c in range(4)],
                    starts=[c == 0 for c in range(4)],
                    stops=[c == 3 for c in range(4)],
                    waits=[kevac[hp], setup_done, w_pj_ms] + bw +
                          ([kf_free] if hp == 0 else []))
                if s == NT1 - 1 and hp == HPR - 1:
                    kf_last = g
                dst = bigA[:, hp * 512:(hp + 1) * 512]
                ev = dve(lambda v, b=bank, d=dst: v.tensor_scalar(
                    out=d, in0=b[:, :], scalar1=0.0, scalar2=EPS_ATT,
                    op0=OP.max, op1=OP.add), waits=[g])
                kfevac.append(ev)
                pool.release(bank, [ev])
            emit_wqo_chunks(4)

            kvb = []
            kvw = []
            for b in range(3):
                bk, w = pool.acquire()
                kvb.append(bk)
                kvw.append(w)
            kv_last_b = [None] * 3
            for tc in range(4):
                for hp in range(HPR):
                    bidx = hp // 3
                    slot = hp % 3
                    g = mm_group(
                        pairs=[(bigA[:, hp * 512 + tc * 128:hp * 512 + (tc + 1) * 128],
                                bigA[:, 4096 + tc * 1040 + hp * 130:
                                     4096 + tc * 1040 + (hp + 1) * 130])],
                        out_slices=[kvb[bidx][:, slot * 130:(slot + 1) * 130]],
                        starts=[tc == 0 and slot == 0],
                        stops=[tc == 3 and hp in (2, 5, 7)],
                        waits=[kfevac[hp], vevac[tc * 2 + hp // 4]] +
                              (kvw[bidx] if (tc == 0 and slot == 0) else []))
                    kv_last_b[bidx] = g
            if s + 1 < NT1:
                pend_sxh = [dve(lambda v, d=xh_k[k], s_=xk[k]: v.tensor_copy(
                    out=d, in_=s_), waits=[x_half[k // 4], xh_free_k[k]])
                    for k in range(KC)]
            for b in range(3):
                off = b * 390
                w_ = 390 if b < 2 else 260
                ev = dve(lambda v, bk=kvb[b], off=off, w_=w_: v.tensor_tensor(
                    out=kvacc[:, off:off + w_], in0=bk[:, 0:w_],
                    in1=kvacc[:, off:off + w_], op=OP.add),
                    waits=[kv_last_b[b], kvacc_busy])
                pool.release(kvb[b], [ev])
                kvacc_busy = ev
                last_kv = ev
            emit_wqo_chunks(4)
            kf_free = kv_last_b[2]
            vpad_free = kv_last_b[2]
            abuf_free = kv_last_b[2]

        def emit_pend_splits_a(free_list):
            sh = [dve(lambda v, d=xhv[:, k * 512:(k + 1) * 512],
                      s_=xbuf[:, k * 512:(k + 1) * 512]: v.tensor_copy(
                          out=d, in_=s_),
                      waits=[x_half[k // 4], free_list[k]])
                  for k in range(KC)]
            sl = []
            for k in range(4):
                e_ = dve(lambda v, d=xlr[:, k * 512:(k + 1) * 512],
                         a=xbuf[:, k * 512:(k + 1) * 512],
                         b=cwb[:, k * 512:(k + 1) * 512]: v.tensor_tensor(
                             out=d, in0=a, in1=b, op=OP.subtract),
                         waits=[sh[k], xl_busy[k]])
                sl.append(e_)
                xl_busy[k] = e_
            return sh, sl

        def emit_pend_splits_b(sh, sl):
            fq = [FutMark() for _ in range(4)]
            for k in range(4, KC):
                e_ = dve(lambda v, d=xlr[:, (k % 4) * 512:(k % 4 + 1) * 512],
                         a=xbuf[:, k * 512:(k + 1) * 512],
                         b=cwb[:, k * 512:(k + 1) * 512]: v.tensor_tensor(
                             out=d, in0=a, in1=b, op=OP.subtract),
                         waits=[sh[k], fq[k - 4]])
                sl.append(e_)
                xl_busy[k % 4] = e_
            return sl, fq

        pend_sxh2, _sl = emit_pend_splits_a(xh_free_k)
        pend_sxl2, pend_fq = emit_pend_splits_b(pend_sxh2, _sl)

        ev_kv = [last_kv] * HPR
        w_ks_ms = dve(lambda v: v.memset(ksum2[:, :], 0.0))
        kvbd_done = None
        for hp in range(HPR):
            base = hp * 130
            kvbd_done = dve(lambda v, hp=hp, base=base: v.tensor_copy(
                out=kvbd[0:64, hp * 128:hp * 128 + 64],
                in_=kvsb[0:64, base:base + 64]), waits=[ev_kv[hp], w_setup_dve])
            kvbd_done = dve(lambda v, hp=hp, base=base: v.tensor_copy(
                out=kvbd[64:128, hp * 128 + 64:hp * 128 + 128],
                in_=kvsb[64:128, base + 65:base + 129]), waits=[ev_kv[hp]])
            kvbd_done = dve(lambda v, hp=hp, base=base: v.tensor_copy(
                out=ksum2[0:64, hp * 2:hp * 2 + 1],
                in_=kvsb[0:64, base + 64:base + 65]), waits=[ev_kv[hp], w_ks_ms])
            kvbd_done = dve(lambda v, hp=hp, base=base: v.tensor_copy(
                out=ksum2[64:128, hp * 2 + 1:hp * 2 + 2],
                in_=kvsb[64:128, base + 64:base + 65]), waits=[ev_kv[hp]])
        for hp in range(HPR):
            kvbd_done = dve(lambda v, hp=hp: v.tensor_copy(
                out=ksum16[:, hp * 16 + 2 * hp:hp * 16 + 2 * hp + 2],
                in_=ksum2[:, 2 * hp:2 * hp + 2]), waits=[kvbd_done])

        if debug == "kv":
            st = dma("scalar", s_out, dbg_h[:, :], kvsb[:, :], waits=[ev_kv[-1]])

            def fin(sc):
                return sc.nop()
            plan.emit("scalar", [st], fin, [])
            _run(block, plan)
            return nc

        waH_busy = [kA_last, kA_last]
        wbH_busy = [vB_last, vB_last]
        abuf_busy_j = [kf_last] * KC
        cwb_free_k = list(xh_free_k)
        scores_busy = (s_dve, 0)
        cwTsb_busy = (s_cb, 0)
        rdenb_busy = (s_dve, 0)
        meanb_busy = (s_dve, 0)
        sq_busy = [(s_mm, 0), (s_mm, 0)]
        rbuf_busy = [(s_gp, 0), (s_gp, 0), (s_gp, 0), (s_gp, 0)]
        r2s_free = kvbd_done
        dsb_busy = (s_dve, 0)
        out_done = []

        LN = {"mean": lnst[0:1, 0:512], "tmp": lnst[0:1, 512:1024]}

        for t in range(NT2):
            ti = NT1 + t
            xk = [xbuf[:, k * 512:(k + 1) * 512] for k in range(KC)]
            xrdy = x_ready[0]

            wq_done = []
            for k in range(KC):
                wq_done.append(dma(
                    "gpsimd", s_wq if k < 4 else s_cb,
                    wAr[:, k * 1024:(k + 1) * 1024],
                    wqh_d[k * 128:(k + 1) * 128, :],
                    waits=([waH_busy[k // 4]] +
                           ([wq32_done] if t == 0 else []))
                    if k % 4 == 0 else ()))
            wA_wh = [wq_done[3], wq_done[7]]

            wo_done = []
            for k in range(KC):
                wo_done.append(dma(
                    "gpsimd", s_wo if k < 4 else s_mr,
                    wBr[:, k * 1024:(k + 1) * 1024],
                    woh_d[k * 128:(k + 1) * 128, :],
                    waits=([wbH_busy[k // 4]] +
                           ([wq32_done] if t == 0 else []))
                    if k % 4 == 0 else ()))
            wB_wh = [wo_done[3], wo_done[7]]

            sxh = pend_sxh2

            def xl_split2(k, slot_wait, src_k):
                return dve(lambda v, d=xl_k[k], a=src_k[k], b=xhf_k[k]:
                           v.tensor_tensor(out=d, in0=a, in1=b,
                                           op=OP.subtract),
                           waits=[sxh[k], slot_wait])

            qbanks = []
            qbw = []
            for j in range(KC):
                b, w = pool.acquire()
                qbanks.append(b)
                qbw.append(w)
            wql_ld = [None] * KC
            qAB = [None] * KC
            for k in range(KC):
                sxl = pend_sxl2[k]
                pairs = ([(wAr[:, k * 1024 + j * 128:k * 1024 + (j + 1) * 128],
                           xh_k[k]) for j in range(KC)] +
                         [(wAr[:, k * 1024 + j * 128:k * 1024 + (j + 1) * 128],
                           xl_k[k]) for j in range(KC)])
                qAB[k] = mm_group(
                    pairs=pairs,
                    out_slices=[qbanks[j][:, :] for j in range(KC)] * 2,
                    starts=[k == 0] * KC + [False] * KC,
                    stops=[False] * (2 * KC),
                    waits=[sxl, wA_wh[k // 4]] +
                          (sum(qbw, []) if k == 0 else []))
                xl_busy[k % 4] = qAB[k]
                if k < 4:
                    pend_fq[k].sv = qAB[k]
                wql_ld[k] = dma("gpsimd", s_wq if k < 4 else s_cb,
                                wAr[:, k * 1024:(k + 1) * 1024],
                                wql_d[k * 128:(k + 1) * 128, :],
                                waits=[qAB[k]] +
                                      ([wq32_done] if t == 0 else []))
            qg7 = [None] * KC
            qp3 = [None] * KC
            for k in range(KC):
                if k < KC - 1:
                    qp3[k] = mm_group(
                        pairs=[(wAr[:, k * 1024 + j * 128:
                                    k * 1024 + (j + 1) * 128],
                                xh_k[k]) for j in range(KC)],
                        out_slices=[qbanks[j][:, :] for j in range(KC)],
                        starts=[False] * KC, stops=[False] * KC,
                        waits=[wql_ld[3 if k < 4 else 7]])
                else:
                    for j in range(KC):
                        qg7[j] = mm_group(
                            pairs=[(wAr[:, k * 1024 + j * 128:
                                        k * 1024 + (j + 1) * 128],
                                    xh_k[k])],
                            out_slices=[qbanks[j][:, :]],
                            starts=[False], stops=[True],
                            waits=[wql_ld[7]] if j == 0 else [])
                    qp3[k] = qg7[KC - 1]
            q_groups = []
            for j in range(KC):
                dst = abuf[:, j * 512:(j + 1) * 512]
                ev = act(lambda sc, b=qbanks[j], d=dst: sc.activation(
                    out=d, in_=b[:, :], func=AF.Identity),
                    waits=[qg7[j], abuf_busy_j[j]])
                pool.release(qbanks[j], [ev])
                q_groups.append((qg7[j], ev))
            waH_busy = [qp3[3], qg7[KC - 1]]

            uph_ld = {}

            def emit_uph(pr, gates, extra0=()):
                m = None
                for k in range(KC):
                    for h in range(2):
                        pe_ = 2 * pr + h
                        m = dma("sync", s_up[k // 4],
                                wAr[:, h * 4096 + k * 512:
                                    h * 4096 + (k + 1) * 512],
                                uph_d[pe_, k * 128:(k + 1) * 128, :],
                                waits=([gates[k]] + list(extra0 if k == 0
                                                         else ()))
                                if h == 0 else ())
                    if k == 3:
                        uph_ld[(pr, 0)] = m
                uph_ld[(pr, 1)] = m

            emit_uph(0, [qp3[4 + k // 2] for k in range(KC)],
                     extra0=([wqo_split_done] if t == 0 else ()))

            qf_ev = []
            for hp in range(HPR):
                bank, bw = pool.acquire()
                g = mm_group(
                    pairs=[(proj2[:, :], abuf[:, hp * 512:(hp + 1) * 512])],
                    out_slices=[bank[:, :]], starts=[True], stops=[True],
                    waits=[q_groups[hp][1]] + bw)
                dst = abuf[:, hp * 512:(hp + 1) * 512]
                ev = dve(lambda v, b=bank, d=dst: v.tensor_scalar(
                    out=d, in0=b[:, :], scalar1=0.0, scalar2=EPS_ATT,
                    op0=OP.max, op1=OP.add), waits=[g])
                qf_ev.append(ev)
                pool.release(bank, [ev])

            o_ev = []
            bank_den, bwd = pool.acquire()
            g_den = None
            for hp in range(HPR):
                qf_ap = abuf[:, hp * 512:(hp + 1) * 512]
                g_den = mm_group(
                    pairs=[(ksum16[:, hp * 16:(hp + 1) * 16], qf_ap)],
                    out_slices=[bank_den[0:16, :]],
                    starts=[hp == 0], stops=[hp == HPR - 1],
                    waits=[qf_ev[hp], kvbd_done] + (bwd if hp == 0 else []))
            e1 = dve(lambda v, b=bank_den: v.tensor_scalar(
                out=dsb[:, :], in0=b[0:16, :], scalar1=EPS_ATT, scalar2=None,
                op0=OP.add), waits=[g_den, dsb_busy])
            pool.release(bank_den, [e1])
            e2 = dve(lambda v: v.reciprocal(out=rd16[:, :], in_=dsb[:, :]),
                     waits=[e1])
            dsb_busy = e2
            num_banks = {}
            for hp in range(HPR):
                qf_ap = abuf[:, hp * 512:(hp + 1) * 512]
                bank_n, bwn = pool.acquire()
                g_num = mm_group(
                    pairs=[(kvbd[:, hp * 128:(hp + 1) * 128], qf_ap)],
                    out_slices=[bank_n[:, :]], starts=[True], stops=[True],
                    waits=[qf_ev[hp], kvbd_done] + bwn)
                bank_b, bwb = pool.acquire()
                g_bc = mm_group(
                    pairs=[(sel16[:, hp * 128:(hp + 1) * 128], rd16[:, :])],
                    out_slices=[bank_b[:, :]], starts=[True], stops=[True],
                    waits=[e2, setup_done] + bwb)
                rb = rdenb[:, 0:512]
                e3 = dve(lambda v, bb=bank_b, rb=rb: v.tensor_copy(
                    out=rb, in_=bb[:, :]), waits=[g_bc, rdenb_busy])
                pool.release(bank_b, [e3])
                ev = dve(lambda v, b=bank_n, rb=rb, d=qf_ap:
                         v.tensor_tensor(out=d, in0=b[:, :], in1=rb,
                                         op=OP.mult), waits=[g_num, e3])
                o_ev.append(ev)
                rdenb_busy = ev
                pool.release(bank_n, [ev])

            if debug == "o" and t == 0:
                st = None
                for j in range(KC):
                    st = dma("scalar", s_out, dbg_h[:, j * 512:(j + 1) * 512],
                             abuf[:, j * 512:(j + 1) * 512], waits=[o_ev[j]])
                plan.emit("scalar", [st], lambda sc: sc.nop(), [])
                _run(block, plan)
                return nc

            oh_k = xh_k
            sxo = []
            for k in range(KC):
                s1 = dve(lambda v, d=oh_k[k], s_=abuf[:, k * 512:(k + 1) * 512]:
                         v.tensor_copy(out=d, in_=s_),
                         waits=[o_ev[k], qp3[k]])
                sxo.append(s1)

            def ol_split(k, slot_wait):
                return dve(lambda v, d=xl_k[k],
                           a=abuf[:, k * 512:(k + 1) * 512], b=xhf_k[k]:
                           v.tensor_tensor(out=d, in0=a, in1=b,
                                           op=OP.subtract),
                           waits=[sxo[k], slot_wait])

            abanks = []
            abw = []
            for j in range(KC):
                b, w = pool.acquire()
                abanks.append(b)
                abw.append(w)
            wol_ld = [None] * KC
            aAB = [None] * KC
            for k in range(KC):
                sol = ol_split(k, xl_busy[k % 4])
                pairs = ([(wBr[:, k * 1024 + j * 128:k * 1024 + (j + 1) * 128],
                           oh_k[k]) for j in range(KC)] +
                         [(wBr[:, k * 1024 + j * 128:k * 1024 + (j + 1) * 128],
                           xl_k[k]) for j in range(KC)])
                aAB[k] = mm_group(
                    pairs=pairs,
                    out_slices=[abanks[j][:, :] for j in range(KC)] * 2,
                    starts=[k == 0] * KC, stops=[False] * (2 * KC),
                    waits=[sol, wB_wh[k // 4]] +
                          (sum(abw, []) if k == 0 else []))
                xl_busy[k % 4] = aAB[k]
                wol_ld[k] = dma("gpsimd", s_wo if k < 4 else s_mr,
                                wBr[:, k * 1024:(k + 1) * 1024],
                                wol_d[k * 128:(k + 1) * 128, :],
                                waits=[aAB[k]] +
                                      ([wq32_done] if t == 0 else []))
            ag7 = [None] * KC
            ap3 = [None] * KC
            for k in range(KC):
                if k < KC - 1:
                    ap3[k] = mm_group(
                        pairs=[(wBr[:, k * 1024 + j * 128:
                                    k * 1024 + (j + 1) * 128],
                                oh_k[k]) for j in range(KC)],
                        out_slices=[abanks[j][:, :] for j in range(KC)],
                        starts=[False] * KC, stops=[False] * KC,
                        waits=[wol_ld[3 if k < 4 else 7]])
                else:
                    for j in range(KC):
                        ag7[j] = mm_group(
                            pairs=[(wBr[:, k * 1024 + j * 128:
                                        k * 1024 + (j + 1) * 128],
                                    oh_k[k])],
                            out_slices=[abanks[j][:, :]],
                            starts=[False], stops=[True],
                            waits=[wol_ld[7]] if j == 0 else [])
                    ap3[k] = ag7[KC - 1]
            cwb_free_k = list(ap3)
            t1_ev = []
            for j in range(KC):
                dst = xk[j]
                ev = dve(lambda v, b=abanks[j], d=dst, j=j:
                         v.scalar_tensor_tensor(
                             out=d, in0=b[:, :], scalar=bocol[:, j:j + 1],
                             in1=d, op0=OP.add, op1=OP.add),
                         waits=[ag7[j], setup_done])
                t1_ev.append(ev)
                pool.release(abanks[j], [ev])
            wbH_busy = [ap3[3], ag7[KC - 1]]

            if debug == "t1" and t == 0:
                st = None
                for j in range(KC):
                    st = dma("scalar", s_out, dbg_h[:, j * 512:(j + 1) * 512],
                             xk[j], waits=[t1_ev[j]])
                plan.emit("scalar", [st], lambda sc: sc.nop(), [])
                _run(block, plan)
                return nc

            def layer_norm(src_chunks, t_ev_list, dst_chunks=None,
                           norm_eng=None):
                nonlocal sq_busy
                dst_chunks = dst_chunks or src_chunks
                bank1, bw1 = pool.acquire()
                bank2, bw2 = pool.acquire()
                sq_ev = [None] * KC
                g1 = g2 = None
                for j in range(KC):
                    sq = sqbuf[:, (j % 2) * 512:(j % 2 + 1) * 512]
                    sq_ev[j] = act(lambda sc, s_=src_chunks[j], q=sq: sc.activation(
                        out=q, in_=s_, func=AF.Square),
                        waits=[t_ev_list[j], sq_busy[j % 2]])
                    g1 = mm_group(
                        pairs=[(ones1[:, :], src_chunks[j])],
                        out_slices=[bank1[0:1, :]],
                        starts=[j == 0], stops=[j == KC - 1],
                        waits=[t_ev_list[j]] + (bw1 if j == 0 else []),
                        inc_group=(j == KC - 1))
                    g2 = mm_group(
                        pairs=[(ones1[:, :], sq)],
                        out_slices=[bank2[0:1, :]],
                        starts=[j == 0], stops=[j == KC - 1],
                        waits=[sq_ev[j]] + (bw2 if j == 0 else []),
                        inc_group=(j == KC - 1))
                    sq_busy[j % 2] = g2
                c = 1.0 / 1024.0
                e1 = dve(lambda v, b=bank1: v.tensor_scalar(
                    out=LN["mean"], in0=b[0:1, :], scalar1=c, scalar2=None,
                    op0=OP.mult), waits=[g1])
                e2 = dve(lambda v, b=bank2: v.tensor_scalar(
                    out=LN["tmp"], in0=b[0:1, :], scalar1=c, scalar2=None,
                    op0=OP.mult), waits=[g2])
                pool.release(bank1, [e1])
                pool.release(bank2, [e2])
                nonlocal rdenb_busy
                bank_m, bwm = pool.acquire()
                bank_r, bwr = pool.acquire()
                g_m = mm_group(
                    pairs=[(onesrow[:, :], LN["mean"])],
                    out_slices=[bank_m[:, :]], starts=[True], stops=[True],
                    waits=[e1, w_setup_dve] + bwm)
                e3 = dve(lambda v: v.tensor_tensor(
                    out=LN["mean"], in0=LN["mean"], in1=LN["mean"], op=OP.mult),
                    waits=[e1, g_m])
                e4 = dve(lambda v: v.tensor_tensor(
                    out=LN["tmp"], in0=LN["tmp"], in1=LN["mean"], op=OP.subtract),
                    waits=[e2, e3])
                e5 = dve(lambda v: v.tensor_scalar(
                    out=LN["tmp"], in0=LN["tmp"], scalar1=EPS_LN, scalar2=None,
                    op0=OP.add), waits=[e4])
                e6 = act(lambda sc: sc.activation(
                    out=LN["tmp"], in_=LN["tmp"], func=AF.Sqrt), waits=[e5])
                g_r = mm_group(
                    pairs=[(onesrow[:, :], LN["tmp"])],
                    out_slices=[bank_r[:, :]], starts=[True], stops=[True],
                    waits=[e6] + bwr)
                nonlocal meanb_busy
                e_mb = dve(lambda v, b=bank_m: v.tensor_copy(
                    out=meanb[:, :], in_=b[:, :]), waits=[g_m, meanb_busy])
                pool.release(bank_m, [e_mb])
                rsb = rdenb[:, 0:512]
                e7 = dve(lambda v, b=bank_r, rsb=rsb: v.reciprocal(
                    out=rsb, in_=b[:, :]), waits=[g_r, rdenb_busy])
                pool.release(bank_r, [e7])
                norm_ev = []
                n1_ev = []
                nfn = gps if norm_eng == "gpsimd" else dve
                for j in range(KC):
                    n1 = nfn(lambda v, s_=src_chunks[j], d=dst_chunks[j]:
                             v.tensor_tensor(
                                 out=d, in0=s_, in1=meanb[:, :],
                                 op=OP.subtract),
                             waits=[e_mb, t_ev_list[j]])
                    n2 = nfn(lambda v, d=dst_chunks[j], rsb=rsb:
                             v.tensor_tensor(out=d, in0=d, in1=rsb,
                                             op=OP.mult), waits=[e7, n1])
                    norm_ev.append(n2)
                    n1_ev.append(n1)
                meanb_busy = n1_ev[-1]
                rdenb_busy = norm_ev[-1]
                return norm_ev, n1_ev

            x1_ev, _ = layer_norm(xk, t1_ev)

            if debug == "ln" and t == 0:
                st = dma("scalar", s_out, dbg_h[0:1, 0:1024], lnst,
                         waits=[x1_ev[-1]])
                plan.emit("scalar", [st], lambda sc: sc.nop(), [])
                _run(block, plan)
                return nc

            if debug == "x1" and t == 0:
                st = None
                for j in range(KC):
                    st = dma("scalar", s_out, dbg_h[:, j * 512:(j + 1) * 512],
                             xk[j], waits=[x1_ev[j]])

                def fin(sc):
                    return sc.nop()
                plan.emit("scalar", [st], fin, [])
                _run(block, plan)
                return nc

            bank_lg, bwl = pool.acquire()
            g_lg = None
            for k in range(KC):
                g_lg = mm_group(
                    pairs=[(xk[k][:, tc * 128:(tc + 1) * 128],
                            gwts[:, k * 8:(k + 1) * 8]) for tc in range(4)],
                    out_slices=[bank_lg[:, tc * 8:(tc + 1) * 8]
                                for tc in range(4)],
                    starts=[k == 0 and tc == 0 for tc in range(4)],
                    stops=[k == KC - 1 and tc == 3 for tc in range(4)],
                    waits=[x1_ev[k], setup_done] + (bwl if k == 0 else []))
            sig = act(lambda sc, b=bank_lg: sc.activation(
                out=scoresb[:, :], in_=b[:, 0:32], func=AF.Sigmoid),
                waits=[g_lg, scores_busy])
            pool.release(bank_lg, [sig])

            P3 = lambda c0, c1, w: rsc[:, c0:c1].rearrange(
                "p (t c) -> p t c", t=4)
            sc3 = scoresb[:, :].rearrange("p (t c) -> p t c", t=4)
            e = dve(lambda v: v.tensor_tensor(
                out=rsc[:, 0:32], in0=scoresb[:, :], in1=gbrep[:, :],
                op=OP.add), waits=[sig, setup_done, scores_busy])
            sfc3 = P3(0, 32, 8)
            ps3 = P3(32, 80, 12)
            pr = [(0, 1), (2, 3), (0, 2), (1, 3), (0, 3), (1, 2)]
            for gidx in range(2):
                for p_, (a_, b_) in enumerate(pr):
                    c0, c1, co = 4 * gidx + a_, 4 * gidx + b_, 6 * gidx + p_
                    e = dve(lambda v, c0=c0, c1=c1, co=co: v.tensor_tensor(
                        out=ps3[:, :, co:co + 1], in0=sfc3[:, :, c0:c0 + 1],
                        in1=sfc3[:, :, c1:c1 + 1], op=OP.add), waits=[e])
            e = dve(lambda v: v.reduce_max(
                out=rsc[:, 80:88],
                in_=rsc[:, 32:80].rearrange("p (t g k) -> p (t g) k", t=4,
                                            g=2), axis=AX), waits=[e])
            gs3 = P3(80, 88, 2)
            e = dve(lambda v: v.tensor_tensor(
                out=rsc[:, 88:92], in0=gs3[:, :, 0:1].rearrange("p t c -> p (t c)"),
                in1=gs3[:, :, 1:2].rearrange("p t c -> p (t c)"),
                op=OP.is_ge), waits=[e])
            e = dve(lambda v: v.tensor_scalar(
                out=rsc[:, 92:96], in0=rsc[:, 88:92], scalar1=-1.0,
                scalar2=1.0, op0=OP.mult, op1=OP.add), waits=[e])
            m0b = P3(88, 92, 1)
            m1b = P3(92, 96, 1)
            msk3 = P3(96, 128, 8)
            e = dve(lambda v: v.tensor_tensor(
                out=msk3[:, :, 0:4], in0=sfc3[:, :, 0:4],
                in1=m0b.broadcast_to([PART, 4, 4]), op=OP.mult), waits=[e])
            e = dve(lambda v: v.tensor_tensor(
                out=msk3[:, :, 4:8], in0=sfc3[:, :, 4:8],
                in1=m1b.broadcast_to([PART, 4, 4]), op=OP.mult), waits=[e])
            e = dve(lambda v: v.reduce_max(
                out=rsc[:, 128:132], in_=msk3, axis=AX), waits=[e])
            mx1b = P3(128, 132, 1)
            e = dve(lambda v: v.tensor_tensor(
                out=P3(132, 164, 8), in0=msk3,
                in1=mx1b.broadcast_to([PART, 4, 8]), op=OP.is_equal),
                waits=[e])
            e = dve(lambda v: v.scalar_tensor_tensor(
                out=rsc[:, 164:196], in0=rsc[:, 132:164], scalar=-1e30,
                in1=rsc[:, 96:128], op0=OP.mult, op1=OP.add), waits=[e])
            e = dve(lambda v: v.reduce_max(
                out=rsc[:, 196:200], in_=P3(164, 196, 8), axis=AX), waits=[e])
            mx2b = P3(196, 200, 1)
            e = dve(lambda v: v.tensor_tensor(
                out=P3(200, 232, 8), in0=msk3,
                in1=mx2b.broadcast_to([PART, 4, 8]), op=OP.is_ge), waits=[e])
            e = dve(lambda v: v.tensor_tensor(
                out=rsc[:, 232:264], in0=rsc[:, 200:232], in1=scoresb[:, :],
                op=OP.mult), waits=[e])
            e = dve(lambda v: v.reduce_sum(
                out=rsc[:, 264:268], in_=P3(232, 264, 8), axis=AX), waits=[e])
            e = dve(lambda v: v.tensor_scalar(
                out=rsc[:, 264:268], in0=rsc[:, 264:268], scalar1=1e-20,
                scalar2=None, op0=OP.add), waits=[e])
            e = dve(lambda v: v.reciprocal(
                out=rsc[:, 268:272], in_=rsc[:, 264:268]), waits=[e])
            rv3 = P3(268, 272, 1)
            e = dve(lambda v: v.scalar_tensor_tensor(
                out=cwt[:, 0:32].rearrange("p (t c) -> p t c", t=4),
                in0=P3(232, 264, 8), scalar=2.5,
                in1=rv3.broadcast_to([PART, 4, 8]),
                op0=OP.mult, op1=OP.mult), waits=[e])
            cw_ev = [e] * 4
            scores_busy = cw_ev[-1]

            def emit_cwT():
                nonlocal cwTsb_busy
                bank_cw, bwc = pool.acquire()
                g_cw = None
                for tc in range(4):
                    g_cw = mm_group(
                        pairs=[(cwt[:, tc * 8:(tc + 1) * 8], identsb[:, :])],
                        out_slices=[bank_cw[0:8, tc * 128:(tc + 1) * 128]],
                        starts=[tc == 0], stops=[tc == 3],
                        waits=[cw_ev[tc], setup_done] +
                              (bwc if tc == 0 else []),
                        inc_group=(tc == 3))
                e_cwT = dve(lambda v, b=bank_cw: v.tensor_copy(
                    out=cwTsb[:, :], in_=b[0:8, :]), waits=[g_cw, cwTsb_busy])
                pool.release(bank_cw, [e_cwT])
                return e_cwT

            cwslots = [rdenb[:, 0:512], meanb[:, 0:512]]
            cw_busy = [rdenb_busy, meanb_busy]
            rbslots = [rbuf[:, 0:512], rbuf[:, 512:1024],
                       sqbuf[:, 0:512], sqbuf[:, 512:1024]]
            sxh1 = []
            for k in range(KC):
                sxh1.append(gps(lambda g_, d=xh_k[k], s_=xk[k]:
                                g_.tensor_copy(out=d, in_=s_),
                                waits=[x1_ev[k]]))
            stat = lambda h, k, i: wAr[:, h * 4096 + k * 512 + i * 128:
                                       h * 4096 + k * 512 + (i + 1) * 128]
            dn_evac = []
            g0_evac_last = None
            last_upP3 = None
            for g_i in range(2):
                pes = list(range(g_i * GSZ, (g_i + 1) * GSZ))
                rfree = r2s_free
                r2s_ready = {}
                for pl in range(3):
                    pr = g_i * 3 + pl
                    qA, qB = 2 * pl, 2 * pl + 1
                    peA, peB = pes[qA], pes[qB]
                    prev_rsp = rsp_next if pr > 0 else None
                    rsp_next = None
                    rsp = [None] * KC
                    for k in range(4):
                        rsp[k] = (prev_rsp[k] if prev_rsp is not None else
                                  gps(lambda g_, d=xl_k[k], a=xk[k],
                                      b=xhf_k[k]: g_.tensor_tensor(
                                          out=d, in0=a, in1=b,
                                          op=OP.subtract),
                                      waits=[sxh1[k], xl_busy[k % 4]]))
                    banks8 = []
                    bw8 = []
                    for b in range(8):
                        bk, w = pool.acquire()
                        banks8.append(bk)
                        bw8.append(w)
                    upAB = [None] * KC
                    upl_m03 = upl_m47 = None
                    for k in range(KC):
                        pairs = ([(stat(h, k, i), xh_k[k])
                                  for h in range(2) for i in range(4)] +
                                 [(stat(h, k, i), xl_k[k])
                                  for h in range(2) for i in range(4)])
                        upAB[k] = mm_group(
                            pairs=pairs,
                            out_slices=[banks8[h * 4 + i][:, :]
                                        for h in range(2) for i in range(4)] * 2,
                            starts=[k == 0] * 8 + [False] * 8,
                            stops=[False] * 16,
                            waits=[rsp[k], sxh1[k], uph_ld[(pr, k // 4)]] +
                                  (sum(bw8, []) if k == 0 else []))
                        xl_busy[k % 4] = upAB[k]
                        ul = None
                        for h, pe_ in enumerate((peA, peB)):
                            ul = dma("gpsimd", s_ul[k // 4],
                                     wAr[:, h * 4096 + k * 512:
                                         h * 4096 + (k + 1) * 512],
                                     upl_d[pe_, k * 128:(k + 1) * 128, :],
                                     waits=[upAB[k]] if h == 0 else ())
                        if k == 3:
                            upl_m03 = ul
                        if k == KC - 1:
                            upl_m47 = ul
                        if k < 4:
                            rsp[k + 4] = gps(
                                lambda g_, d=xl_k[k + 4], a=xk[k + 4],
                                b=xhf_k[k + 4]: g_.tensor_tensor(
                                    out=d, in0=a, in1=b, op=OP.subtract),
                                waits=[sxh1[k + 4], upAB[k]])
                    if pr + 1 < 6:
                        rsp_next = [gps(lambda g_, d=xl_k[k], a=xk[k],
                                        b=xhf_k[k]: g_.tensor_tensor(
                                            out=d, in0=a, in1=b,
                                            op=OP.subtract),
                                        waits=[sxh1[k], xl_busy[k % 4]])
                                    for k in range(4)]
                    upP3 = [None] * KC
                    g7 = [None] * 8
                    for k in range(4):
                        upP3[k] = mm_group(
                            pairs=[(stat(h, k, i), xh_k[k])
                                   for h in range(2) for i in range(4)],
                            out_slices=[banks8[h * 4 + i][:, :]
                                        for h in range(2) for i in range(4)],
                            starts=[False] * 8, stops=[False] * 8,
                            waits=[upl_m03] if k == 0 else [])
                    for b in range(8):
                        g7[b] = mm_group(
                            pairs=[(stat(b // 4, k, b % 4), xh_k[k])
                                   for k in range(4, KC)],
                            out_slices=[banks8[b][:, :]] * 4,
                            starts=[False] * 4,
                            stops=[False, False, False, True],
                            waits=[upl_m47] if b == 0 else [])
                    for k in range(4, KC):
                        upP3[k] = g7[7]
                    last_upP3 = upP3
                    a1s = [None] * 8
                    slots8 = []
                    for b in range(8):
                        q_ = (qA, qB)[b // 4]
                        slots8.append(bigA[:, (q_ * 4 + b % 4) * 512:
                                           (q_ * 4 + b % 4 + 1) * 512])
                    for b in range(8):
                        a1 = dve(lambda v, bk=banks8[b], sl=slots8[b]:
                                 v.tensor_scalar(out=sl, in0=bk[:, :],
                                                 scalar1=0.0, scalar2=None,
                                                 op0=OP.max),
                                 waits=[g7[b], rfree])
                        a1s[b] = a1
                        pool.release(banks8[b], [a1])
                    if g_i == 0 and pl == 0:
                        e_cwT = emit_cwT()
                    ec_m = {}
                    for h, pe_ in enumerate((peA, peB)):
                        if pe_ < 8:
                            bank_cw2, bwc2 = pool.acquire()
                            gg = mm_group(
                                pairs=[(selcw[:, pe_ * 128:(pe_ + 1) * 128],
                                        cwTsb[:, :])],
                                out_slices=[bank_cw2[:, :]], starts=[True],
                                stops=[True],
                                waits=[e_cwT, setup_done, cwTsb_busy] + bwc2)
                            cwTsb_busy = gg
                            ecv = dve(lambda v, b=bank_cw2, d=cwslots[h]:
                                      v.tensor_copy(out=d, in_=b[:, :]),
                                      waits=[gg, cw_busy[h]])
                            pool.release(bank_cw2, [ecv])
                            ec_m[h] = ecv
                    for b in range(8):
                        q_ = (qA, qB)[b // 4]
                        pe_ = (peA, peB)[b // 4]
                        slot = slots8[b]
                        r = rbslots[b % 4]
                        if pe_ < 8:
                            d1 = dve(lambda v, sl=slot, r=r,
                                     c=cwslots[b // 4]: v.tensor_tensor(
                                         out=r, in0=sl, in1=c, op=OP.mult),
                                     waits=[a1s[b], ec_m[b // 4],
                                            rbuf_busy[b % 4]])
                            cw_busy[b // 4] = d1
                            g1 = gps(lambda g_, sl=slot, r=r: g_.tensor_tensor(
                                out=sl, in0=sl, in1=r, op=OP.mult),
                                waits=[d1])
                            rbuf_busy[b % 4] = g1
                        else:
                            g1 = gps(lambda g_, sl=slot: g_.tensor_tensor(
                                out=sl, in0=sl, in1=sl, op=OP.mult),
                                waits=[a1s[b]])
                        r2s_ready[(q_, b % 4)] = g1
                    if pr + 1 < 6:
                        emit_uph(pr + 1, upP3)
                waH_busy = [last_upP3[KC - 1], last_upP3[KC - 1]]

                if debug == "upa" and t == 0 and g_i == 0:
                    st = None
                    for sl8 in range(KC):
                        st = dma("scalar", s_out,
                                 dbg_h[:, sl8 * 512:(sl8 + 1) * 512],
                                 bigA[:, sl8 * 512:(sl8 + 1) * 512],
                                 waits=[r2s_ready[(sl8 // 4, sl8 % 4)]])
                    plan.emit("scalar", [st], lambda sc: sc.nop(), [])
                    _run(block, plan)
                    return nc

                banks_j = []
                bws_j = []
                for j in range(KC):
                    bank, bw = pool.acquire()
                    banks_j.append(bank)
                    bws_j.append(bw)
                dn_loaded = {}
                dn_last_mm = {}

                def emit_dn_load(qi):
                    pe = pes[qi]
                    h = qi % 2
                    w = [wbH_busy[h]] if qi < 2 else [dn_last_mm[qi - 2]]
                    ld = None
                    for i in range(4):
                        ld = dma("scalar", s_wb[h],
                                 wbufB[:, h * 4096 + i * 1024:h * 4096 + (i + 1) * 1024],
                                 dn_all[pe, i * 128:(i + 1) * 128, :],
                                 waits=w if i == 0 else ())
                    dn_loaded[qi] = ld

                emit_dn_load(0)
                emit_dn_load(1)
                dn_last = {}
                for qi, pe in enumerate(pes):
                    h = qi % 2
                    for j in range(KC):
                        g = mm_group(
                            pairs=[(wbufB[:, h * 4096 + i * 1024 + j * 128:
                                          h * 4096 + i * 1024 + (j + 1) * 128],
                                    bigA[:, (qi * 4 + i) * 512:
                                         (qi * 4 + i + 1) * 512])
                                   for i in range(4)],
                            out_slices=[banks_j[j][:, :]] * 4,
                            starts=[(qi == 0 and i == 0) for i in range(4)],
                            stops=[(qi == GSZ - 1 and i == 3) for i in range(4)],
                            waits=[dn_loaded[qi], r2s_ready[(qi, 3)]] +
                                  [r2s_ready[(qi, i)] for i in range(3)] +
                                  (bws_j[j] if qi == 0 else []))
                        if qi == GSZ - 1:
                            dn_last[j] = g
                    dn_last_mm[qi] = (s_mm, s_mm.n)
                    if qi + 2 < GSZ:
                        emit_dn_load(qi + 2)
                wbH_busy = [dn_last_mm[GSZ - 2], dn_last_mm[GSZ - 1]]
                evs = []
                for j in range(KC):
                    if g_i == 0:
                        ev = dve(lambda v, b=banks_j[j], j=j, xkj=xk[j]:
                                 v.tensor_tensor(
                                     out=abuf[:, j * 512:(j + 1) * 512],
                                     in0=b[:, :], in1=xkj, op=OP.add),
                                 waits=[dn_last[j]])
                    else:
                        ev = dve(lambda v, b=banks_j[j], j=j: v.tensor_tensor(
                            out=abuf[:, j * 512:(j + 1) * 512], in0=b[:, :],
                            in1=abuf[:, j * 512:(j + 1) * 512], op=OP.add),
                            waits=[dn_last[j]])
                    evs.append(ev)
                    pool.release(banks_j[j], [ev])
                if g_i == 0:
                    g0_evac_last = evs[-1]
                dn_evac = evs
                r2s_free = (s_mm, s_mm.n)

                if debug == "dng0" and t == 0 and g_i == 0:
                    st = None
                    for j8 in range(KC):
                        st = dma("scalar", s_out,
                                 dbg_h[:, j8 * 512:(j8 + 1) * 512],
                                 abuf[:, j8 * 512:(j8 + 1) * 512],
                                 waits=[evs[j8]])
                    plan.emit("scalar", [st], lambda sc: sc.nop(), [])
                    _run(block, plan)
                    return nc

            cwb_free_k = list(last_upP3)
            rdenb_busy = cw_busy[0]
            meanb_busy = cw_busy[1]
            sq_busy = [rbuf_busy[2], rbuf_busy[3]]

            xbuf_free = [[(s_mm, s_mm.n), g0_evac_last]] * 2
            if ti + 1 < NT1 + NT2:
                load_x_tile(ti + 1)

            if debug == "ffn" and t == 0:
                st = None
                for j in range(KC):
                    st = dma("scalar", s_out, dbg_h[:, j * 512:(j + 1) * 512],
                             abuf[:, j * 512:(j + 1) * 512], waits=[dn_evac[j]])
                plan.emit("scalar", [st], lambda sc: sc.nop(), [])
                _run(block, plan)
                return nc

            if t + 1 < NT2:
                pend_sxh2, _sl = emit_pend_splits_a(cwb_free_k)

            t2_chunks = [abuf[:, j * 512:(j + 1) * 512] for j in range(KC)]
            x2_ev, n1s = layer_norm(t2_chunks, dn_evac)

            st = None
            for j in range(KC):
                st = dma("scalar", s_out, y_h[j * 128:(j + 1) * 128,
                                              t * 512:(t + 1) * 512],
                         abuf[:, j * 512:(j + 1) * 512], waits=[x2_ev[j]])
            out_done.append(st)
            abuf_busy_j = [st] * KC
            if t + 1 < NT2:
                pend_sxl2, pend_fq = emit_pend_splits_b(pend_sxh2, _sl)

        def fin(sc):
            return sc.nop()
        plan.emit("scalar", [out_done[-1]], fin, [])

        _run(block, plan)
    return nc


def _run(block, plan):
    def runner(name):
        def go(eng):
            last = {}
            for waits, op, incs in plan.steps[name]:
                for w in waits:
                    s, v = w.sv if isinstance(w, FutMark) else w
                    if v > 0 and last.get(s.name, -1) < v:
                        eng.wait_ge(s.h, v)
                        last[s.name] = v
                ins = op(eng)
                for s, k in incs:
                    ins.then_inc(s.h, k)
        return go

    block.sync(runner("sync"))
    block.scalar(runner("scalar"))
    block.vector(runner("vector"))
    block.tensor(runner("tensor"))
    block.gpsimd(runner("gpsimd"))


def _prep_inputs(L0_attn_x, L0_attn_Wq, L0_attn_Wk, L0_attn_Wv, L0_attn_proj,
                 L0_attn_Wo, L0_attn_bo, L0_ffn_gate_weight, L0_ffn_gate_bias,
                 L0_ffn_up_weights, L0_ffn_down_weights, L0_ffn_shared_up_weight,
                 L0_ffn_shared_down_weight):
    f = lambda a: np.ascontiguousarray(np.asarray(a, dtype=np.float32))
    x = f(L0_attn_x)
    wq, wk, wv, wo = f(L0_attn_Wq), f(L0_attn_Wk), f(L0_attn_Wv), f(L0_attn_Wo)
    proj = f(L0_attn_proj)
    bo = f(L0_attn_bo).reshape(1024, 1)
    gw = f(L0_ffn_gate_weight)
    gb = f(L0_ffn_gate_bias).reshape(1, 8)
    upw = f(L0_ffn_up_weights)
    dnw = f(L0_ffn_down_weights)
    sup = f(L0_ffn_shared_up_weight)
    sdn = f(L0_ffn_shared_down_weight)

    upT = np.transpose(upw, (0, 2, 1))
    supT = np.ascontiguousarray(sup.T)
    up_all = np.stack([*upT] + [supT[:, p * 512:(p + 1) * 512] for p in range(4)])
    dnT = np.transpose(dnw, (0, 2, 1))
    sdnT = np.ascontiguousarray(sdn.T)
    dn_all = np.stack([*dnT] + [sdnT[p * 512:(p + 1) * 512, :] for p in range(4)])
    gwT = np.ascontiguousarray(gw.T)
    ident = np.eye(128, dtype=np.float32)
    sel16 = np.zeros((16, 1024), dtype=np.float32)
    for hp in range(8):
        sel16[2 * hp, hp * 128:hp * 128 + 64] = 1.0
        sel16[2 * hp + 1, hp * 128 + 64:hp * 128 + 128] = 1.0
    selcw = np.zeros((8, 1024), dtype=np.float32)
    for e in range(8):
        selcw[e, e * 128:(e + 1) * 128] = 1.0

    in_maps = []
    for core in range(8):
        b, h = core // 2, core % 2
        xT = np.ascontiguousarray(x[b].T)
        in_maps.append({
            "x_full": xT,
            "x_own": np.ascontiguousarray(xT[:, h * 2048:(h + 1) * 2048]),
            "wq": wq, "wk": wk, "wv": wv, "wo": wo,
            "proj": proj, "bo": bo,
            "up_all": np.ascontiguousarray(up_all),
            "dn_all": np.ascontiguousarray(dn_all),
            "gwt": gwT, "gb": gb, "ident": ident, "sel16": sel16,
            "selcw": selcw,
        })
    return in_maps


def kernel(**inputs):
    in_maps = _prep_inputs(**inputs)
    nc = build()
    res = run_bass_kernel_spmd(nc, in_maps, core_ids=list(range(8)))
    ys = [res.results[c]["y"] for c in range(8)]
    out = np.stack([
        np.concatenate([ys[2 * b], ys[2 * b + 1]], axis=1).T for b in range(4)
    ]).astype(np.float32)
    return np.ascontiguousarray(out)


# version 42
# speedup vs baseline: 1.1960x; 1.0692x over previous
from contextlib import ExitStack

import numpy as np
import concourse.bass as bass
import concourse.mybir as mybir
from concourse.bass_utils import run_bass_kernel_spmd

F32 = mybir.dt.float32
F32R = mybir.dt.float32r
AF = mybir.ActivationFunctionType
OP = mybir.AluOpType
AX = mybir.AxisListType.X

PART = 128
TT = 512
NT1 = 8
NT2 = 4
KC = 8
HPR = 8
NPE = 12
GSZ = 6
D = 1024
EPS_ATT = 1e-6
EPS_LN = 1e-5

ENGS = ("sync", "scalar", "vector", "tensor", "gpsimd")


class PSem:
    __slots__ = ("name", "n", "h")

    def __init__(self, name):
        self.name, self.n, self.h = name, 0, None


class FutMark:
    __slots__ = ("sv",)

    def __init__(self):
        self.sv = None


class Plan:
    def __init__(self):
        self.steps = {e: [] for e in ENGS}
        self.sems = []
        self.last_compute = {e: None for e in ENGS}

    def sem(self, name):
        s = PSem(name)
        self.sems.append(s)
        return s

    def emit(self, eng, waits, op, incs, is_dma=False):
        waits = list(waits)
        if not is_dma and eng in ("scalar", "vector", "gpsimd"):
            lc = self.last_compute[eng]
            if lc is not None:
                waits.append(lc)
        w = tuple(x for x in waits
                  if isinstance(x, FutMark) or x[1] > 0)
        for s, k in incs:
            s.n += k
        if not is_dma and incs:
            s0, _ = incs[0]
            self.last_compute[eng] = (s0, s0.n)
        self.steps[eng].append((w, op, tuple(incs)))


class Pool:

    def __init__(self, banks):
        self.free = list(banks)
        self.rel = {}

    def acquire(self):
        b = self.free.pop(0)
        return b, list(self.rel.get(id(b), ()))

    def release(self, b, waits):
        self.rel[id(b)] = tuple(waits)
        self.free.append(b)


_LAST_PLAN = None


def build(debug=None):
    global _LAST_PLAN
    nc = bass.Bass("TRN2", target_bir_lowering=False)

    x_full = nc.dram_tensor("x_full", [D, 4096], F32, kind="ExternalInput").ap()
    x_own = nc.dram_tensor("x_own", [D, 2048], F32, kind="ExternalInput").ap()
    wq = nc.dram_tensor("wq", [D, D], F32, kind="ExternalInput").ap()
    wk = nc.dram_tensor("wk", [D, D], F32, kind="ExternalInput").ap()
    wv = nc.dram_tensor("wv", [D, D], F32, kind="ExternalInput").ap()
    wo = nc.dram_tensor("wo", [D, D], F32, kind="ExternalInput").ap()
    proj_h = nc.dram_tensor("proj", [64, 64], F32, kind="ExternalInput").ap()
    bo_h = nc.dram_tensor("bo", [D, 1], F32, kind="ExternalInput").ap()
    up_all = nc.dram_tensor("up_all", [NPE, D, 512], F32, kind="ExternalInput").ap()
    dn_all = nc.dram_tensor("dn_all", [NPE, 512, D], F32, kind="ExternalInput").ap()
    gwt_h = nc.dram_tensor("gwt", [D, 8], F32, kind="ExternalInput").ap()
    gb_h = nc.dram_tensor("gb", [1, 8], F32, kind="ExternalInput").ap()
    ident_h = nc.dram_tensor("ident", [128, 128], F32, kind="ExternalInput").ap()
    sel16_h = nc.dram_tensor("sel16", [16, 1024], F32, kind="ExternalInput").ap()
    selcw_h = nc.dram_tensor("selcw", [8, 1024], F32, kind="ExternalInput").ap()
    y_h = nc.dram_tensor("y", [D, 2048], F32, kind="ExternalOutput").ap()
    scr_d = nc.dram_tensor("scr", [16, 512], F32, kind="Internal").ap()
    wkh_d = nc.dram_tensor("wkh", [D, D], F32R, kind="Internal").ap()
    wkl_d = nc.dram_tensor("wkl", [D, D], F32R, kind="Internal").ap()
    wvh_d = nc.dram_tensor("wvh", [D, D], F32R, kind="Internal").ap()
    wvl_d = nc.dram_tensor("wvl", [D, D], F32R, kind="Internal").ap()
    uph_d = nc.dram_tensor("uph", [NPE, D, 512], F32R, kind="Internal").ap()
    upl_d = nc.dram_tensor("upl", [NPE, D, 512], F32R, kind="Internal").ap()
    wqh_d = nc.dram_tensor("wqh", [D, D], F32R, kind="Internal").ap()
    wql_d = nc.dram_tensor("wql", [D, D], F32R, kind="Internal").ap()
    woh_d = nc.dram_tensor("woh", [D, D], F32R, kind="Internal").ap()
    wol_d = nc.dram_tensor("wol", [D, D], F32R, kind="Internal").ap()
    dbg_h = None
    if debug == "kv":
        dbg_h = nc.dram_tensor("dbg", [PART, 1040], F32, kind="ExternalOutput").ap()
    elif debug in ("x1", "o", "t1", "t2", "ln", "cwb", "ffn", "upa", "dng0"):
        dbg_h = nc.dram_tensor("dbg", [PART, 4096], F32, kind="ExternalOutput").ap()

    plan = Plan()
    _LAST_PLAN = plan
    s_x = [plan.sem("s_x0"), plan.sem("s_x1")]
    s_wa = [plan.sem("s_wa0"), plan.sem("s_wa1")]
    s_wb = [plan.sem("s_wb0"), plan.sem("s_wb1")]
    s_cb = plan.sem("s_cb")
    s_wq = plan.sem("s_wq")
    s_wo = plan.sem("s_wo")
    s_spl = plan.sem("s_spl")
    s_gv = [plan.sem("s_gv0"), plan.sem("s_gv1")]
    s_gv2 = [plan.sem("s_gv2a"), plan.sem("s_gv2b")]
    s_qh = [plan.sem("s_qh0"), plan.sem("s_qh1")]
    s_up = [plan.sem("s_up0"), plan.sem("s_up1")]
    s_ul = [plan.sem("s_ul0"), plan.sem("s_ul1")]
    s_sp2 = plan.sem("s_sp2")
    s_bc = [plan.sem("s_bc0"), plan.sem("s_bc1"),
            plan.sem("s_bc2"), plan.sem("s_bc3")]
    s_mr = plan.sem("s_mr")
    s_sm = plan.sem("s_sm")
    s_out = plan.sem("s_out")
    s_scw = plan.sem("s_scw")
    s_mm = plan.sem("s_mm")
    s_dve = plan.sem("s_dve")
    s_act = plan.sem("s_act")
    s_gp = plan.sem("s_gp")

    with ExitStack() as stack:
        ec = stack.enter_context
        xbuf_t = ec(nc.sbuf_tensor("xbuf", [PART, 4096], F32))
        abuf_t = ec(nc.sbuf_tensor("abuf", [PART, 4096], F32))
        bigA_t = ec(nc.sbuf_tensor("bigA", [PART, 12288], F32))
        xlb_t = ec(nc.sbuf_tensor("xlb", [PART, 2048], F32))
        selcw_t = ec(nc.sbuf_tensor("selcwsb", [8, 1024], F32))
        offA = nc.sbuf_base
        wbufA_t = ec(nc.sbuf_tensor("wbufA", [PART, 8192], F32))
        wbufB_t = ec(nc.sbuf_tensor("wbufB", [PART, 8192], F32))
        cwb_t = ec(nc.sbuf_tensor("cwb", [PART, 4096], F32))
        rdenb_t = ec(nc.sbuf_tensor("rdenb", [PART, 512], F32))
        meanb_t = ec(nc.sbuf_tensor("meanb", [PART, 512], F32))
        kvacc_t = ec(nc.sbuf_tensor("kvacc", [PART, 1040], F32))
        sel2_t = ec(nc.sbuf_tensor("sel2", [2, 128], F32))
        ones_row_t = ec(nc.sbuf_tensor("onesrow", [1, 128], F32))
        sqbuf_t = ec(nc.sbuf_tensor("sqbuf", [PART, 1024], F32))
        rbuf_t = ec(nc.sbuf_tensor("rbuf", [PART, 1024], F32))
        proj2_t = ec(nc.sbuf_tensor("proj2", [PART, 128], F32))
        ident_t = ec(nc.sbuf_tensor("identsb", [PART, 128], F32))
        kvbd_t = ec(nc.sbuf_tensor("kvbd", [PART, 1024], F32))
        ksum2_t = ec(nc.sbuf_tensor("ksum2", [PART, 16], F32))
        bocol_t = ec(nc.sbuf_tensor("bocol", [PART, 8], F32))
        gbrep_t = ec(nc.sbuf_tensor("gbrep", [PART, 32], F32))
        gwts_t = ec(nc.sbuf_tensor("gwts", [PART, 64], F32))
        ones1_t = ec(nc.sbuf_tensor("ones1", [PART, 1], F32))
        cwt_t = ec(nc.sbuf_tensor("cwt", [PART, 32], F32))
        scores_t = ec(nc.sbuf_tensor("scoresb", [PART, 32], F32))
        rsc_t = ec(nc.sbuf_tensor("rsc", [PART, 272], F32))
        cwTsb_t = ec(nc.sbuf_tensor("cwTsb", [8, 512], F32))
        dsb_t = ec(nc.sbuf_tensor("dsb", [16, 512], F32))
        rd16_t = ec(nc.sbuf_tensor("rd16", [16, 512], F32))
        sel16_t = ec(nc.sbuf_tensor("sel16sb", [16, 1024], F32))
        ksum16_t = ec(nc.sbuf_tensor("ksum16", [PART, 128], F32))
        psts = [ec(nc.psum_tensor("ps%d" % i, [PART, 512], F32)) for i in range(8)]
        hs = [ec(nc.semaphore("m%d" % i)) for i in range(len(plan.sems))]
        block = ec(nc.Block())
        for s, h in zip(plan.sems, hs):
            s.h = h

        xbuf = xbuf_t.ap()
        abuf = abuf_t.ap()
        bigA = bigA_t.ap()
        wbufA = wbufA_t.ap()
        wbufB = wbufB_t.ap()
        cwb = cwb_t.ap()
        rdenb = rdenb_t.ap()
        meanb = meanb_t.ap()
        kvacc = kvacc_t.ap()
        xhv = cwb_t.ap().bitcast(F32R)
        wAr = nc.alloc_sbuf_tensor_at(
            "wbufAr", [PART, 8192], F32R, offset=offA).ap()
        wBr = nc.alloc_sbuf_tensor_at(
            "wbufBr", [PART, 8192], F32R, offset=offA + 32768).ap()
        xlr = xlb_t.ap().bitcast(F32R)
        xlf = xlb_t.ap()
        selcw = selcw_t.ap()
        splh2v = [(bigA_t.ap()[:, 8256:9280],
                   bigA_t.ap().bitcast(F32R)[:, 8256:9280]),
                  (bigA_t.ap()[:, 0:1024],
                   bigA_t.ap().bitcast(F32R)[:, 0:1024])]
        splh, splh_r = splh2v[0]
        sel2 = sel2_t.ap()
        onesrow = ones_row_t.ap()
        sqbuf = sqbuf_t.ap()
        rbuf = rbuf_t.ap()
        proj2 = proj2_t.ap()
        identsb = ident_t.ap()
        kvbd = kvbd_t.ap()
        ksum2 = ksum2_t.ap()
        bocol = bocol_t.ap()
        gbrep = gbrep_t.ap()
        gwts = gwts_t.ap()
        kvsb = kvacc
        ones1 = ones1_t.ap()
        cwt = cwt_t.ap()
        scoresb = scores_t.ap()
        rsc = rsc_t.ap()
        cwTsb = cwTsb_t.ap()
        dsb = dsb_t.ap()
        rd16 = rd16_t.ap()
        sel16 = sel16_t.ap()
        ksum16 = ksum16_t.ap()
        lnst = rbuf_t.ap()[0:1, 0:1024]
        PS = [p.ap() for p in psts]

        def mm_group(pairs, out_slices, starts, stops, waits, inc_group=True,
                     per_mm_waits=None):
            seq = list(zip(pairs, out_slices, starts, stops,
                           per_mm_waits or [None] * len(pairs)))

            def op(t):
                ins = None
                last = {}
                for (lhsT, rhs), out, st, sp, pw in seq:
                    if pw is not None:
                        s_, v_ = pw.sv if isinstance(pw, FutMark) else pw
                        if v_ > 0 and last.get(s_.name, -1) < v_:
                            t.wait_ge(s_.h, v_)
                            last[s_.name] = v_
                    ins = t.matmul(out, lhsT, rhs, start=st, stop=sp)
                return ins
            plan.emit("tensor", waits, op, [(s_mm, 1)])
            return (s_mm, s_mm.n)

        def dma(eng, sem, out, in_, waits=()):
            def op(e):
                return e.dma_start(out=out, in_=in_)
            plan.emit(eng, waits, op, [(sem, 16)], is_dma=True)
            return (sem, sem.n)

        def dve(fn, waits=(), sem=s_dve):
            plan.emit("vector", waits, fn, [(sem, 1)])
            return (sem, sem.n)

        def act(fn, waits=()):
            plan.emit("scalar", waits, fn, [(s_act, 1)])
            return (s_act, s_act.n)

        def gps(fn, waits=()):
            plan.emit("gpsimd", waits, fn, [(s_gp, 1)])
            return (s_gp, s_gp.n)

        pool = Pool([PS[i] for i in range(8)])

        w_pj_ms = dve(lambda v: v.memset(proj2[:, :], 0.0))
        dma("scalar", s_sm, proj2[0:64, 0:64], proj_h[:, :], waits=[w_pj_ms])
        dma("scalar", s_sm, proj2[64:128, 64:128], proj_h[:, :], waits=[w_pj_ms])
        dma("scalar", s_sm, identsb[:, :], ident_h[:, :])
        dma("scalar", s_sm, sel16[:, :], sel16_h[:, :])
        dma("scalar", s_sm, selcw[:, :], selcw_h[:, :])
        for k in range(KC):
            dma("scalar", s_sm, gwts[:, k * 8:(k + 1) * 8],
                gwt_h[k * 128:(k + 1) * 128, :])
        for r4 in range(4):
            dma("scalar", s_sm, gbrep[:, r4 * 8:(r4 + 1) * 8],
                gb_h[0:1, :].broadcast_to([PART, 8]))
        for j in range(KC):
            dma("scalar", s_sm, bocol[:, j:j + 1], bo_h[j * 128:(j + 1) * 128, 0:1])
        setup_done = (s_sm, s_sm.n)
        dve(lambda v: v.memset(ones1[:, :], 1.0))
        dve(lambda v: v.memset(kvbd[:, :], 0.0))
        dve(lambda v: v.memset(sel2[:, :], 0.0))
        dve(lambda v: v.memset(sel2[0:1, 0:64], 1.0))
        dve(lambda v: v.memset(sel2[0:2, 64:128], 1.0))
        dve(lambda v: v.memset(sel2[0:1, 64:128], 0.0))
        dve(lambda v: v.memset(ksum16[:, :], 0.0))
        kvacc_ms = dve(lambda v: v.memset(kvacc[:, :], 0.0))
        dve(lambda v: v.memset(bigA[:, 4096:8256], 1.0))
        w_setup_dve = dve(lambda v: v.memset(onesrow[:, :], 1.0))

        stg = [bigA[:, 9280 + i * 1024:9280 + (i + 1) * 1024] for i in range(2)]
        stgr = [bigA_t.ap().bitcast(F32R)[:, 9280 + i * 1024:
                                          9280 + (i + 1) * 1024]
                for i in range(2)]
        wkh_ok = [None] * KC
        wvh_ok = [None] * KC

        xbuf_free = [[], []]
        x_ready = [None]
        x_half = [None, None]

        def load_x_tile(ti):
            done = None
            hmarks = [None, None]
            for k in range(KC):
                src = (x_full[k * 128:(k + 1) * 128, ti * TT:ti * TT + TT]
                       if ti < NT1 else
                       x_own[k * 128:(k + 1) * 128, (ti - NT1) * TT:
                             (ti - NT1) * TT + TT])
                done = dma("scalar", s_x[k // 4],
                           xbuf[:, k * 512:(k + 1) * 512],
                           src, waits=list(xbuf_free[k // 4])
                           if k % 4 == 0 else ())
                hmarks[k // 4] = done
            x_ready[0] = done
            x_half[0], x_half[1] = hmarks
            return done

        load_x_tile(0)

        p1_sxh = [dve(lambda v, d=xhv[:, k * 512:(k + 1) * 512],
                      s_=xbuf[:, k * 512:(k + 1) * 512]: v.tensor_copy(
                          out=d, in_=s_), waits=[x_half[k // 4]])
                  for k in range(KC)]
        p1_fgAB = [FutMark() for _ in range(KC)]
        p1_fvAB = [FutMark() for _ in range(4)]
        p1_kxl = [None] * KC
        p1_vxl = [None] * KC

        def emit_p1_kxl(k):
            w2 = [p1_sxh[k]] + ([p1_fgAB[k - 4]] if k >= 4 else [])
            p1_kxl[k] = dve(
                lambda v, d=xlr[:, (k % 4) * 512:(k % 4 + 1) * 512],
                a=xbuf[:, k * 512:(k + 1) * 512],
                b=cwb[:, k * 512:(k + 1) * 512]: v.tensor_tensor(
                    out=d, in0=a, in1=b, op=OP.subtract), waits=w2)

        def emit_p1_vxl(k):
            w2 = [p1_sxh[k],
                  p1_fgAB[k + 4] if k < 4 else p1_fvAB[k - 4]]
            p1_vxl[k] = dve(
                lambda v, d=xlr[:, (k % 4) * 512:(k % 4 + 1) * 512],
                a=xbuf[:, k * 512:(k + 1) * 512],
                b=cwb[:, k * 512:(k + 1) * 512]: v.tensor_tensor(
                    out=d, in0=a, in1=b, op=OP.subtract), waits=w2)

        for k in range(4):
            emit_p1_kxl(k)

        splh_free = []
        stg_free = [[], []]
        splh_free2 = [[], []]
        for wsel in range(2):
            wsrc, wbr_, hd, ld_, oks = (
                (wk, wAr, wkh_d, wkl_d, wkh_ok) if wsel == 0 else
                (wv, wBr, wvh_d, wvl_d, wvh_ok))
            for k in range(KC):
                c0, c1 = k * 1024, (k + 1) * 1024
                sg = k % 2
                sp, sp_r = splh2v[sg]
                ld0 = dma("sync", s_bc[sg], stg[sg],
                          wsrc[k * 128:(k + 1) * 128, :],
                          waits=list(stg_free[sg]))
                s1 = dve(lambda v, sg=sg, sp_r=sp_r: v.tensor_copy(
                    out=sp_r[:, 0:1024], in_=stg[sg]),
                    waits=[ld0] + list(splh_free2[sg]))
                s2 = dve(lambda v, c0=c0, c1=c1, sg=sg, wbr_=wbr_, sp=sp:
                         v.tensor_tensor(out=wbr_[:, c0:c1], in0=stg[sg],
                                         in1=sp[:, 0:1024], op=OP.subtract),
                         waits=[s1])
                stg_free[sg] = [s2]
                d3 = dma("scalar", s_spl, ld_[k * 128:(k + 1) * 128, :],
                         wbr_[:, c0:c1], waits=[s2])
                d5 = dma("scalar", s_spl, hd[k * 128:(k + 1) * 128, :],
                         sp_r[:, 0:1024], waits=())
                s4 = dve(lambda v, c0=c0, c1=c1, wbr_=wbr_, sp_r=sp_r:
                         v.tensor_copy(
                             out=wbr_[:, c0:c1], in_=sp_r[:, 0:1024]),
                         waits=[d5])
                oks[k] = s4
                splh_free2[sg] = [s4]
            if wsel == 0:
                for k2 in range(4, KC):
                    emit_p1_kxl(k2)
            else:
                for k2 in range(KC):
                    emit_p1_vxl(k2)
        splh_free = sum(splh_free2, [])
        wqo_pend = [(w_[k * 128:(k + 1) * 128, hf * 512:(hf + 1) * 512],
                     hd[k * 128:(k + 1) * 128, hf * 512:(hf + 1) * 512],
                     ld_[k * 128:(k + 1) * 128, hf * 512:(hf + 1) * 512])
                    for w_, hd, ld_ in ((wq, wqh_d, wql_d),
                                        (wo, woh_d, wol_d))
                    for k in range(KC) for hf in range(2)]
        wqo_pend += [(up_all[pe, k * 128:(k + 1) * 128, :],
                      uph_d[pe, k * 128:(k + 1) * 128, :],
                      upl_d[pe, k * 128:(k + 1) * 128, :])
                     for pe in range(NPE) for k in range(KC)]
        wqo_done_m = FutMark()
        wq32_done_m = FutMark()
        bigr = bigA_t.ap().bitcast(F32R)
        w_in = [bigA[:, 9280 + s * 512:9280 + (s + 1) * 512] for s in range(4)]
        w_inr = [bigr[:, 9280 + s * 512:9280 + (s + 1) * 512] for s in range(4)]
        w_hi = [splh[:, hh * 512:(hh + 1) * 512] for hh in range(2)]
        w_hir = [splh_r[:, hh * 512:(hh + 1) * 512] for hh in range(2)]
        w_in_free = [list(stg_free[0]), list(stg_free[0]),
                     list(stg_free[1]), list(stg_free[1])]
        w_hi_free = [list(splh_free), list(splh_free)]
        wqo_c = [0]
        w_d5_prev = [[]]

        def emit_wqo_chunks(n):
            for _ in range(n):
                if not wqo_pend:
                    return
                src_ap, hd_ap, ld_ap = wqo_pend.pop(0)
                c = wqo_c[0]
                wqo_c[0] += 1
                sl = c % 4
                hh = c % 2
                ld0 = dma("scalar", s_bc[sl], w_in[sl], src_ap,
                          waits=w_in_free[sl])
                s1 = gps(lambda g_, sl=sl, hh=hh: g_.tensor_copy(
                    out=w_hir[hh], in_=w_in[sl]),
                    waits=[ld0] + w_hi_free[hh])
                s2 = gps(lambda g_, sl=sl, hh=hh: g_.tensor_tensor(
                    out=w_inr[sl], in0=w_in[sl], in1=w_hi[hh],
                    op=OP.subtract), waits=[s1])
                d3 = dma("gpsimd", s_sp2, ld_ap, w_inr[sl],
                         waits=[s2] + w_d5_prev[0])
                d5 = dma("gpsimd", s_sp2, hd_ap, w_hir[hh], waits=())
                w_d5_prev[0] = [d5]
                w_in_free[sl] = [d5]
                w_hi_free[hh] = [d5]
                if wqo_c[0] == 32 and wq32_done_m.sv is None:
                    wq32_done_m.sv = (s_sp2, s_sp2.n)
            if not wqo_pend and wqo_done_m.sv is None:
                wqo_done_m.sv = (s_sp2, s_sp2.n)
        wqo_split_done = wqo_done_m
        wq32_done = wq32_done_m



        last_kv = None
        kf_free = (s_mm, 0)
        abuf_free = (s_mm, 0)
        vpad_free = (s_mm, 0)
        xh_free_k = [(s_mm, 0)] * KC
        xl_busy = [(s_mm, 0)] * 4
        kvacc_busy = kvacc_ms

        wqo_sched = [18, 18, 18, 18, 18, 18, 14, 6]
        for s in range(NT1):
            nw4, rw4 = divmod(wqo_sched[s], 4)
            wqo_part = [nw4 + (1 if i_ < rw4 else 0) for i_ in range(4)]
            xk = [xbuf[:, k * 512:(k + 1) * 512] for k in range(KC)]
            xh_k = [xhv[:, k * 512:(k + 1) * 512] for k in range(KC)]
            xhf_k = [cwb[:, k * 512:(k + 1) * 512] for k in range(KC)]
            xl_k = [xlr[:, (k % 4) * 512:(k % 4 + 1) * 512] for k in range(KC)]
            xrdy = x_ready[0]

            sxh = p1_sxh if s == 0 else pend_sxh

            def xl_split(k, slot_wait):
                return dve(lambda v, d=xl_k[k], a=xk[k], b=xhf_k[k]:
                           v.tensor_tensor(out=d, in0=a, in1=b,
                                           op=OP.subtract),
                           waits=[sxh[k], slot_wait])

            kbanks = []
            kbw = []
            for j in range(KC):
                b, w = pool.acquire()
                kbanks.append(b)
                kbw.append(w)
            wkl_ld = [None] * KC
            gAB = [None] * KC
            for k in range(KC):
                sxl = (p1_kxl[k] if s == 0
                       else xl_split(k, xl_busy[k % 4]))
                pairs = ([(wAr[:, k * 1024 + j * 128:k * 1024 + (j + 1) * 128],
                           xh_k[k]) for j in range(KC)] +
                         [(wAr[:, k * 1024 + j * 128:k * 1024 + (j + 1) * 128],
                           xl_k[k]) for j in range(KC)])
                outs = [kbanks[j][:, :] for j in range(KC)] * 2
                w0 = [sxl, wkh_ok[k]] + (sum(kbw, []) if k == 0 else [])
                if k == 0:
                    w0.append(abuf_free)
                gAB[k] = mm_group(
                    pairs=pairs, out_slices=outs,
                    starts=[k == 0] * KC + [False] * KC,
                    stops=[False] * (2 * KC), waits=w0)
                xl_busy[k % 4] = gAB[k]
                if s == 0:
                    p1_fgAB[k].sv = gAB[k]
                wkl_ld[k] = dma("sync", s_wa[k // 4],
                                wAr[:, k * 1024:(k + 1) * 1024],
                                wkl_d[k * 128:(k + 1) * 128, :], waits=[gAB[k]])
            gC = [None] * KC
            gC7 = [None] * KC
            for k in range(KC):
                if k < KC - 1:
                    gC[k] = mm_group(
                        pairs=[(wAr[:, k * 1024 + j * 128:
                                    k * 1024 + (j + 1) * 128],
                                xh_k[k]) for j in range(KC)],
                        out_slices=[kbanks[j][:, :] for j in range(KC)],
                        starts=[False] * KC, stops=[False] * KC,
                        waits=[wkl_ld[3 if k < 4 else 7]])
                else:
                    for j in range(KC):
                        gC7[j] = mm_group(
                            pairs=[(wAr[:, k * 1024 + j * 128:
                                        k * 1024 + (j + 1) * 128],
                                    xh_k[k])],
                            out_slices=[kbanks[j][:, :]],
                            starts=[False], stops=[True],
                            waits=[wkl_ld[7]] if j == 0 else [])
                    gC[k] = gC7[KC - 1]
            if s == NT1 - 1:
                kA_last = gC[KC - 1]
            kevac = []
            for j in range(KC):
                dst = abuf[:, j * 512:(j + 1) * 512]
                ev = act(lambda sc, b=kbanks[j], d=dst: sc.activation(
                    out=d, in_=b[:, :], func=AF.Identity),
                    waits=[gC7[j], abuf_free])
                kevac.append(ev)
                pool.release(kbanks[j], [ev])
            emit_wqo_chunks(wqo_part[0])
            if s < NT1 - 1:
                wkh_ld = [None] * KC
                for k in range(KC):
                    wkh_ld[k] = dma("sync", s_wa[k // 4],
                                    wAr[:, k * 1024:(k + 1) * 1024],
                                    wkh_d[k * 128:(k + 1) * 128, :],
                                    waits=[gC[k]])
                for k in range(KC):
                    wkh_ok[k] = wkh_ld[3 if k < 4 else 7]

            vbanks = []
            vbw = []
            for vi in range(KC):
                b, w = pool.acquire()
                vbanks.append(b)
                vbw.append(w)
            wvl_ld = [None] * KC
            vAB = [None] * KC
            for k in range(KC):
                sxl = (p1_vxl[k] if s == 0
                       else xl_split(k, xl_busy[k % 4]))
                if k == 3:
                    vsplit3 = sxl
                if k == KC - 1:
                    last_vsplit = sxl
                pairs = []
                outs = []
                for xp in (xh_k, xl_k):
                    for tc in range(4):
                        for jh in range(2):
                            pairs.append((
                                xp[k][:, tc * 128:(tc + 1) * 128],
                                wBr[:, k * 1024 + jh * 512:
                                    k * 1024 + jh * 512 + 512]))
                            outs.append(vbanks[tc * 2 + jh][:, :])
                w0 = [sxl, wvh_ok[k]] + (sum(vbw, []) if k == 0 else [])
                if k == 0:
                    w0.append(vpad_free)
                vAB[k] = mm_group(
                    pairs=pairs, out_slices=outs,
                    starts=[k == 0] * KC + [False] * KC,
                    stops=[False] * (2 * KC), waits=w0)
                xl_busy[k % 4] = vAB[k]
                if s == 0 and k < 4:
                    p1_fvAB[k].sv = vAB[k]
                wvl_ld[k] = dma("sync", s_gv2[k // 4],
                                wBr[:, k * 1024:(k + 1) * 1024],
                                wvl_d[k * 128:(k + 1) * 128, :], waits=[vAB[k]])
            vC = [None] * KC
            vC7 = [None] * KC
            for k in range(KC):
                if k < KC - 1:
                    pairs = []
                    outs = []
                    for tc in range(4):
                        for jh in range(2):
                            pairs.append((
                                xh_k[k][:, tc * 128:(tc + 1) * 128],
                                wBr[:, k * 1024 + jh * 512:
                                    k * 1024 + jh * 512 + 512]))
                            outs.append(vbanks[tc * 2 + jh][:, :])
                    vC[k] = mm_group(
                        pairs=pairs, out_slices=outs,
                        starts=[False] * KC, stops=[False] * KC,
                        waits=[wvl_ld[3 if k < 4 else 7]])
                    xh_free_k[k] = vC[k]
                else:
                    for vi in range(KC):
                        tc, jh = vi // 2, vi % 2
                        vC7[vi] = mm_group(
                            pairs=[(xh_k[k][:, tc * 128:(tc + 1) * 128],
                                    wBr[:, k * 1024 + jh * 512:
                                        k * 1024 + jh * 512 + 512])],
                            out_slices=[vbanks[vi][:, :]],
                            starts=[False], stops=[True],
                            waits=[wvl_ld[7]] if vi == 0 else [])
                    vC[k] = vC7[KC - 1]
                    xh_free_k[k] = vC7[KC - 1]
            if s == NT1 - 1:
                vB_last = vC[KC - 1]
            vevac = []
            for tc in range(4):
                vp = bigA[:, 4096 + tc * 1040:4096 + (tc + 1) * 1040]
                vp3 = vp.rearrange("p (a b) -> p a b", a=16)
                for jh in range(2):
                    bank = vbanks[tc * 2 + jh]
                    src3 = bank[:, :].rearrange("p (a b) -> p a b", a=8)
                    dst3 = vp3[:, jh * 8:(jh + 1) * 8, 0:64]
                    ev = dve(lambda v, s3=src3, d3=dst3: v.tensor_copy(
                        out=d3, in_=s3), waits=[vC7[tc * 2 + jh]])
                    vevac.append(ev)
                    pool.release(bank, [ev])
            emit_wqo_chunks(wqo_part[1])
            xbuf_free = [[vsplit3], [last_vsplit]]
            load_x_tile(s + 1)
            if s < NT1 - 1:
                wvh_ld = [None] * KC
                for k in range(KC):
                    wvh_ld[k] = dma("sync", s_gv2[k // 4],
                                    wBr[:, k * 1024:(k + 1) * 1024],
                                    wvh_d[k * 128:(k + 1) * 128, :],
                                    waits=[vC[k]])
                for k in range(KC):
                    wvh_ok[k] = wvh_ld[3 if k < 4 else 7]

            kfevac = []
            for hp in range(HPR):
                bank, bw = pool.acquire()
                g = mm_group(
                    pairs=[(abuf[:, hp * 512 + c * 128:hp * 512 + (c + 1) * 128],
                            proj2[:, :]) for c in range(4)],
                    out_slices=[bank[:, c * 128:(c + 1) * 128] for c in range(4)],
                    starts=[c == 0 for c in range(4)],
                    stops=[c == 3 for c in range(4)],
                    waits=[kevac[hp], setup_done, w_pj_ms] + bw +
                          ([kf_free] if hp == 0 else []))
                if s == NT1 - 1 and hp == HPR - 1:
                    kf_last = g
                dst = bigA[:, hp * 512:(hp + 1) * 512]
                ev = dve(lambda v, b=bank, d=dst: v.tensor_scalar(
                    out=d, in0=b[:, :], scalar1=0.0, scalar2=EPS_ATT,
                    op0=OP.max, op1=OP.add), waits=[g])
                kfevac.append(ev)
                pool.release(bank, [ev])
            emit_wqo_chunks(wqo_part[2])

            kvb = []
            kvw = []
            for b in range(3):
                bk, w = pool.acquire()
                kvb.append(bk)
                kvw.append(w)
            kv_last_b = [None] * 3
            for tc in range(4):
                for hp in range(HPR):
                    bidx = hp // 3
                    slot = hp % 3
                    g = mm_group(
                        pairs=[(bigA[:, hp * 512 + tc * 128:hp * 512 + (tc + 1) * 128],
                                bigA[:, 4096 + tc * 1040 + hp * 130:
                                     4096 + tc * 1040 + (hp + 1) * 130])],
                        out_slices=[kvb[bidx][:, slot * 130:(slot + 1) * 130]],
                        starts=[tc == 0 and slot == 0],
                        stops=[tc == 3 and hp in (2, 5, 7)],
                        waits=[kfevac[hp], vevac[tc * 2 + hp // 4]] +
                              (kvw[bidx] if (tc == 0 and slot == 0) else []))
                    kv_last_b[bidx] = g
            if s + 1 < NT1:
                pend_sxh = [dve(lambda v, d=xh_k[k], s_=xk[k]: v.tensor_copy(
                    out=d, in_=s_), waits=[x_half[k // 4], xh_free_k[k]])
                    for k in range(KC)]
            for b in range(3):
                off = b * 390
                w_ = 390 if b < 2 else 260
                ev = dve(lambda v, bk=kvb[b], off=off, w_=w_: v.tensor_tensor(
                    out=kvacc[:, off:off + w_], in0=bk[:, 0:w_],
                    in1=kvacc[:, off:off + w_], op=OP.add),
                    waits=[kv_last_b[b], kvacc_busy])
                pool.release(kvb[b], [ev])
                kvacc_busy = ev
                last_kv = ev
            emit_wqo_chunks(wqo_part[3])
            kf_free = kv_last_b[2]
            vpad_free = kv_last_b[2]
            abuf_free = kv_last_b[2]

        def emit_pend_splits_a(free_list):
            sh = [dve(lambda v, d=xhv[:, k * 512:(k + 1) * 512],
                      s_=xbuf[:, k * 512:(k + 1) * 512]: v.tensor_copy(
                          out=d, in_=s_),
                      waits=[x_half[k // 4], free_list[k]])
                  for k in range(KC)]
            sl = []
            for k in range(4):
                e_ = dve(lambda v, d=xlr[:, k * 512:(k + 1) * 512],
                         a=xbuf[:, k * 512:(k + 1) * 512],
                         b=cwb[:, k * 512:(k + 1) * 512]: v.tensor_tensor(
                             out=d, in0=a, in1=b, op=OP.subtract),
                         waits=[sh[k], xl_busy[k]])
                sl.append(e_)
                xl_busy[k] = e_
            return sh, sl

        def emit_pend_splits_b(sh, sl):
            fq = [FutMark() for _ in range(4)]
            for k in range(4, KC):
                e_ = dve(lambda v, d=xlr[:, (k % 4) * 512:(k % 4 + 1) * 512],
                         a=xbuf[:, k * 512:(k + 1) * 512],
                         b=cwb[:, k * 512:(k + 1) * 512]: v.tensor_tensor(
                             out=d, in0=a, in1=b, op=OP.subtract),
                         waits=[sh[k], fq[k - 4]])
                sl.append(e_)
                xl_busy[k % 4] = e_
            return sl, fq

        pend_sxh2, _sl = emit_pend_splits_a(xh_free_k)
        pend_sxl2, pend_fq = emit_pend_splits_b(pend_sxh2, _sl)

        ev_kv = [last_kv] * HPR
        w_ks_ms = dve(lambda v: v.memset(ksum2[:, :], 0.0))
        kvbd_done = None
        for hp in range(HPR):
            base = hp * 130
            kvbd_done = dve(lambda v, hp=hp, base=base: v.tensor_copy(
                out=kvbd[0:64, hp * 128:hp * 128 + 64],
                in_=kvsb[0:64, base:base + 64]), waits=[ev_kv[hp], w_setup_dve])
            kvbd_done = dve(lambda v, hp=hp, base=base: v.tensor_copy(
                out=kvbd[64:128, hp * 128 + 64:hp * 128 + 128],
                in_=kvsb[64:128, base + 65:base + 129]), waits=[ev_kv[hp]])
            kvbd_done = dve(lambda v, hp=hp, base=base: v.tensor_copy(
                out=ksum2[0:64, hp * 2:hp * 2 + 1],
                in_=kvsb[0:64, base + 64:base + 65]), waits=[ev_kv[hp], w_ks_ms])
            kvbd_done = dve(lambda v, hp=hp, base=base: v.tensor_copy(
                out=ksum2[64:128, hp * 2 + 1:hp * 2 + 2],
                in_=kvsb[64:128, base + 64:base + 65]), waits=[ev_kv[hp]])
        for hp in range(HPR):
            kvbd_done = dve(lambda v, hp=hp: v.tensor_copy(
                out=ksum16[:, hp * 16 + 2 * hp:hp * 16 + 2 * hp + 2],
                in_=ksum2[:, 2 * hp:2 * hp + 2]), waits=[kvbd_done])

        if debug == "kv":
            st = dma("scalar", s_out, dbg_h[:, :], kvsb[:, :], waits=[ev_kv[-1]])

            def fin(sc):
                return sc.nop()
            plan.emit("scalar", [st], fin, [])
            _run(block, plan)
            return nc

        waH_busy = [kA_last, kA_last]
        wbH_busy = [vB_last, vB_last]
        abuf_busy_j = [kf_last] * KC
        cwb_free_k = list(xh_free_k)
        scores_busy = (s_dve, 0)
        cwTsb_busy = (s_cb, 0)
        rdenb_busy = (s_dve, 0)
        meanb_busy = (s_dve, 0)
        sq_busy = [(s_mm, 0), (s_mm, 0)]
        rbuf_busy = [(s_gp, 0), (s_gp, 0), (s_gp, 0), (s_gp, 0)]
        r2s_free = kvbd_done
        dsb_busy = (s_dve, 0)
        out_done = []

        LN = {"mean": lnst[0:1, 0:512], "tmp": lnst[0:1, 512:1024]}

        for t in range(NT2):
            ti = NT1 + t
            xk = [xbuf[:, k * 512:(k + 1) * 512] for k in range(KC)]
            xrdy = x_ready[0]

            wq_done = []
            for k in range(KC):
                wq_done.append(dma(
                    "sync", s_qh[k // 4],
                    wAr[:, k * 1024:(k + 1) * 1024],
                    wqh_d[k * 128:(k + 1) * 128, :],
                    waits=([waH_busy[k // 4]] +
                           ([wq32_done] if t == 0 else []))
                    if k % 4 == 0 else ()))
            wA_wh = [wq_done[3], wq_done[7]]

            wo_done = []
            for k in range(KC):
                wo_done.append(dma(
                    "sync", s_gv[k // 4],
                    wBr[:, k * 1024:(k + 1) * 1024],
                    woh_d[k * 128:(k + 1) * 128, :],
                    waits=([wbH_busy[k // 4]] +
                           ([wq32_done] if t == 0 else []))
                    if k % 4 == 0 else ()))
            wB_wh = [wo_done[3], wo_done[7]]

            sxh = pend_sxh2

            def xl_split2(k, slot_wait, src_k):
                return dve(lambda v, d=xl_k[k], a=src_k[k], b=xhf_k[k]:
                           v.tensor_tensor(out=d, in0=a, in1=b,
                                           op=OP.subtract),
                           waits=[sxh[k], slot_wait])

            qbanks = []
            qbw = []
            for j in range(KC):
                b, w = pool.acquire()
                qbanks.append(b)
                qbw.append(w)
            wql_ld = [None] * KC
            qAB = [None] * KC
            for k in range(KC):
                sxl = pend_sxl2[k]
                pairs = ([(wAr[:, k * 1024 + j * 128:k * 1024 + (j + 1) * 128],
                           xh_k[k]) for j in range(KC)] +
                         [(wAr[:, k * 1024 + j * 128:k * 1024 + (j + 1) * 128],
                           xl_k[k]) for j in range(KC)])
                qAB[k] = mm_group(
                    pairs=pairs,
                    out_slices=[qbanks[j][:, :] for j in range(KC)] * 2,
                    starts=[k == 0] * KC + [False] * KC,
                    stops=[False] * (2 * KC),
                    waits=[sxl, wA_wh[k // 4]] +
                          (sum(qbw, []) if k == 0 else []))
                xl_busy[k % 4] = qAB[k]
                if k < 4:
                    pend_fq[k].sv = qAB[k]
                wql_ld[k] = dma("gpsimd", s_wq if k < 4 else s_cb,
                                wAr[:, k * 1024:(k + 1) * 1024],
                                wql_d[k * 128:(k + 1) * 128, :],
                                waits=[qAB[k]] +
                                      ([wq32_done] if t == 0 else []))
            qg7 = [None] * KC
            qp3 = [None] * KC
            for k in range(KC):
                if k < KC - 1:
                    qp3[k] = mm_group(
                        pairs=[(wAr[:, k * 1024 + j * 128:
                                    k * 1024 + (j + 1) * 128],
                                xh_k[k]) for j in range(KC)],
                        out_slices=[qbanks[j][:, :] for j in range(KC)],
                        starts=[False] * KC, stops=[False] * KC,
                        waits=[wql_ld[3 if k < 4 else 7]])
                else:
                    for j in range(KC):
                        qg7[j] = mm_group(
                            pairs=[(wAr[:, k * 1024 + j * 128:
                                        k * 1024 + (j + 1) * 128],
                                    xh_k[k])],
                            out_slices=[qbanks[j][:, :]],
                            starts=[False], stops=[True],
                            waits=[wql_ld[7]] if j == 0 else [])
                    qp3[k] = qg7[KC - 1]
            q_groups = []
            for j in range(KC):
                dst = abuf[:, j * 512:(j + 1) * 512]
                ev = act(lambda sc, b=qbanks[j], d=dst: sc.activation(
                    out=d, in_=b[:, :], func=AF.Identity),
                    waits=[qg7[j], abuf_busy_j[j]])
                pool.release(qbanks[j], [ev])
                q_groups.append((qg7[j], ev))
            waH_busy = [qp3[3], qg7[KC - 1]]

            uph_ld = {}

            def emit_uph(pr, gates, extra0=()):
                m = None
                for k in range(KC):
                    for h in range(2):
                        pe_ = 2 * pr + h
                        m = dma("sync", s_up[k // 4],
                                wAr[:, h * 4096 + k * 512:
                                    h * 4096 + (k + 1) * 512],
                                uph_d[pe_, k * 128:(k + 1) * 128, :],
                                waits=([gates[k]] + list(extra0 if k == 0
                                                         else ()))
                                if h == 0 else ())
                    if k == 3:
                        uph_ld[(pr, 0)] = m
                uph_ld[(pr, 1)] = m

            emit_uph(0, [qp3[4 + k // 2] for k in range(KC)],
                     extra0=([wqo_split_done] if t == 0 else ()))

            qf_ev = []
            for hp in range(HPR):
                bank, bw = pool.acquire()
                g = mm_group(
                    pairs=[(proj2[:, :], abuf[:, hp * 512:(hp + 1) * 512])],
                    out_slices=[bank[:, :]], starts=[True], stops=[True],
                    waits=[q_groups[hp][1]] + bw)
                dst = abuf[:, hp * 512:(hp + 1) * 512]
                ev = dve(lambda v, b=bank, d=dst: v.tensor_scalar(
                    out=d, in0=b[:, :], scalar1=0.0, scalar2=EPS_ATT,
                    op0=OP.max, op1=OP.add), waits=[g])
                qf_ev.append(ev)
                pool.release(bank, [ev])

            o_ev = []
            bank_den, bwd = pool.acquire()
            g_den = None
            for hp in range(HPR):
                qf_ap = abuf[:, hp * 512:(hp + 1) * 512]
                g_den = mm_group(
                    pairs=[(ksum16[:, hp * 16:(hp + 1) * 16], qf_ap)],
                    out_slices=[bank_den[0:16, :]],
                    starts=[hp == 0], stops=[hp == HPR - 1],
                    waits=[qf_ev[hp], kvbd_done] + (bwd if hp == 0 else []))
            e1 = dve(lambda v, b=bank_den: v.tensor_scalar(
                out=dsb[:, :], in0=b[0:16, :], scalar1=EPS_ATT, scalar2=None,
                op0=OP.add), waits=[g_den, dsb_busy])
            pool.release(bank_den, [e1])
            e2 = dve(lambda v: v.reciprocal(out=rd16[:, :], in_=dsb[:, :]),
                     waits=[e1])
            dsb_busy = e2
            num_banks = {}
            for hp in range(HPR):
                qf_ap = abuf[:, hp * 512:(hp + 1) * 512]
                bank_n, bwn = pool.acquire()
                g_num = mm_group(
                    pairs=[(kvbd[:, hp * 128:(hp + 1) * 128], qf_ap)],
                    out_slices=[bank_n[:, :]], starts=[True], stops=[True],
                    waits=[qf_ev[hp], kvbd_done] + bwn)
                bank_b, bwb = pool.acquire()
                g_bc = mm_group(
                    pairs=[(sel16[:, hp * 128:(hp + 1) * 128], rd16[:, :])],
                    out_slices=[bank_b[:, :]], starts=[True], stops=[True],
                    waits=[e2, setup_done] + bwb)
                rb = rdenb[:, 0:512]
                e3 = dve(lambda v, bb=bank_b, rb=rb: v.tensor_copy(
                    out=rb, in_=bb[:, :]), waits=[g_bc, rdenb_busy])
                pool.release(bank_b, [e3])
                ev = dve(lambda v, b=bank_n, rb=rb, d=qf_ap:
                         v.tensor_tensor(out=d, in0=b[:, :], in1=rb,
                                         op=OP.mult), waits=[g_num, e3])
                o_ev.append(ev)
                rdenb_busy = ev
                pool.release(bank_n, [ev])

            if debug == "o" and t == 0:
                st = None
                for j in range(KC):
                    st = dma("scalar", s_out, dbg_h[:, j * 512:(j + 1) * 512],
                             abuf[:, j * 512:(j + 1) * 512], waits=[o_ev[j]])
                plan.emit("scalar", [st], lambda sc: sc.nop(), [])
                _run(block, plan)
                return nc

            oh_k = xh_k
            sxo = []
            for k in range(KC):
                s1 = dve(lambda v, d=oh_k[k], s_=abuf[:, k * 512:(k + 1) * 512]:
                         v.tensor_copy(out=d, in_=s_),
                         waits=[o_ev[k], qp3[k]])
                sxo.append(s1)

            def ol_split(k, slot_wait):
                return dve(lambda v, d=xl_k[k],
                           a=abuf[:, k * 512:(k + 1) * 512], b=xhf_k[k]:
                           v.tensor_tensor(out=d, in0=a, in1=b,
                                           op=OP.subtract),
                           waits=[sxo[k], slot_wait])

            abanks = []
            abw = []
            for j in range(KC):
                b, w = pool.acquire()
                abanks.append(b)
                abw.append(w)
            wol_ld = [None] * KC
            aAB = [None] * KC
            for k in range(KC):
                sol = ol_split(k, xl_busy[k % 4])
                pairs = ([(wBr[:, k * 1024 + j * 128:k * 1024 + (j + 1) * 128],
                           oh_k[k]) for j in range(KC)] +
                         [(wBr[:, k * 1024 + j * 128:k * 1024 + (j + 1) * 128],
                           xl_k[k]) for j in range(KC)])
                aAB[k] = mm_group(
                    pairs=pairs,
                    out_slices=[abanks[j][:, :] for j in range(KC)] * 2,
                    starts=[k == 0] * KC, stops=[False] * (2 * KC),
                    waits=[sol, wB_wh[k // 4]] +
                          (sum(abw, []) if k == 0 else []))
                xl_busy[k % 4] = aAB[k]
                wol_ld[k] = dma("gpsimd", s_wo if k < 4 else s_mr,
                                wBr[:, k * 1024:(k + 1) * 1024],
                                wol_d[k * 128:(k + 1) * 128, :],
                                waits=[aAB[k]] +
                                      ([wq32_done] if t == 0 else []))
            ag7 = [None] * KC
            ap3 = [None] * KC
            for k in range(KC):
                if k < KC - 1:
                    ap3[k] = mm_group(
                        pairs=[(wBr[:, k * 1024 + j * 128:
                                    k * 1024 + (j + 1) * 128],
                                oh_k[k]) for j in range(KC)],
                        out_slices=[abanks[j][:, :] for j in range(KC)],
                        starts=[False] * KC, stops=[False] * KC,
                        waits=[wol_ld[3 if k < 4 else 7]])
                else:
                    for j in range(KC):
                        ag7[j] = mm_group(
                            pairs=[(wBr[:, k * 1024 + j * 128:
                                        k * 1024 + (j + 1) * 128],
                                    oh_k[k])],
                            out_slices=[abanks[j][:, :]],
                            starts=[False], stops=[True],
                            waits=[wol_ld[7]] if j == 0 else [])
                    ap3[k] = ag7[KC - 1]
            cwb_free_k = list(ap3)
            t1_ev = []
            for j in range(KC):
                dst = xk[j]
                ev = dve(lambda v, b=abanks[j], d=dst, j=j:
                         v.scalar_tensor_tensor(
                             out=d, in0=b[:, :], scalar=bocol[:, j:j + 1],
                             in1=d, op0=OP.add, op1=OP.add),
                         waits=[ag7[j], setup_done])
                t1_ev.append(ev)
                pool.release(abanks[j], [ev])
            wbH_busy = [ap3[3], ag7[KC - 1]]

            if debug == "t1" and t == 0:
                st = None
                for j in range(KC):
                    st = dma("scalar", s_out, dbg_h[:, j * 512:(j + 1) * 512],
                             xk[j], waits=[t1_ev[j]])
                plan.emit("scalar", [st], lambda sc: sc.nop(), [])
                _run(block, plan)
                return nc

            def layer_norm(src_chunks, t_ev_list, dst_chunks=None,
                           norm_eng=None):
                nonlocal sq_busy
                dst_chunks = dst_chunks or src_chunks
                bank1, bw1 = pool.acquire()
                bank2, bw2 = pool.acquire()
                sq_ev = [None] * KC
                g1 = g2 = None
                for j in range(KC):
                    sq = sqbuf[:, (j % 2) * 512:(j % 2 + 1) * 512]
                    sq_ev[j] = act(lambda sc, s_=src_chunks[j], q=sq: sc.activation(
                        out=q, in_=s_, func=AF.Square),
                        waits=[t_ev_list[j], sq_busy[j % 2]])
                    g1 = mm_group(
                        pairs=[(ones1[:, :], src_chunks[j])],
                        out_slices=[bank1[0:1, :]],
                        starts=[j == 0], stops=[j == KC - 1],
                        waits=[t_ev_list[j]] + (bw1 if j == 0 else []),
                        inc_group=(j == KC - 1))
                    g2 = mm_group(
                        pairs=[(ones1[:, :], sq)],
                        out_slices=[bank2[0:1, :]],
                        starts=[j == 0], stops=[j == KC - 1],
                        waits=[sq_ev[j]] + (bw2 if j == 0 else []),
                        inc_group=(j == KC - 1))
                    sq_busy[j % 2] = g2
                c = 1.0 / 1024.0
                e1 = dve(lambda v, b=bank1: v.tensor_scalar(
                    out=LN["mean"], in0=b[0:1, :], scalar1=c, scalar2=None,
                    op0=OP.mult), waits=[g1])
                e2 = dve(lambda v, b=bank2: v.tensor_scalar(
                    out=LN["tmp"], in0=b[0:1, :], scalar1=c, scalar2=None,
                    op0=OP.mult), waits=[g2])
                pool.release(bank1, [e1])
                pool.release(bank2, [e2])
                nonlocal rdenb_busy
                bank_m, bwm = pool.acquire()
                bank_r, bwr = pool.acquire()
                g_m = mm_group(
                    pairs=[(onesrow[:, :], LN["mean"])],
                    out_slices=[bank_m[:, :]], starts=[True], stops=[True],
                    waits=[e1, w_setup_dve] + bwm)
                e3 = dve(lambda v: v.tensor_tensor(
                    out=LN["mean"], in0=LN["mean"], in1=LN["mean"], op=OP.mult),
                    waits=[e1, g_m])
                e4 = dve(lambda v: v.tensor_tensor(
                    out=LN["tmp"], in0=LN["tmp"], in1=LN["mean"], op=OP.subtract),
                    waits=[e2, e3])
                e5 = dve(lambda v: v.tensor_scalar(
                    out=LN["tmp"], in0=LN["tmp"], scalar1=EPS_LN, scalar2=None,
                    op0=OP.add), waits=[e4])
                e6 = act(lambda sc: sc.activation(
                    out=LN["tmp"], in_=LN["tmp"], func=AF.Sqrt), waits=[e5])
                g_r = mm_group(
                    pairs=[(onesrow[:, :], LN["tmp"])],
                    out_slices=[bank_r[:, :]], starts=[True], stops=[True],
                    waits=[e6] + bwr)
                nonlocal meanb_busy
                e_mb = dve(lambda v, b=bank_m: v.tensor_copy(
                    out=meanb[:, :], in_=b[:, :]), waits=[g_m, meanb_busy])
                pool.release(bank_m, [e_mb])
                rsb = rdenb[:, 0:512]
                e7 = dve(lambda v, b=bank_r, rsb=rsb: v.reciprocal(
                    out=rsb, in_=b[:, :]), waits=[g_r, rdenb_busy])
                pool.release(bank_r, [e7])
                norm_ev = []
                n1_ev = []
                nfn = gps if norm_eng == "gpsimd" else dve
                for j in range(KC):
                    n1 = nfn(lambda v, s_=src_chunks[j], d=dst_chunks[j]:
                             v.tensor_tensor(
                                 out=d, in0=s_, in1=meanb[:, :],
                                 op=OP.subtract),
                             waits=[e_mb, t_ev_list[j]])
                    n2 = nfn(lambda v, d=dst_chunks[j], rsb=rsb:
                             v.tensor_tensor(out=d, in0=d, in1=rsb,
                                             op=OP.mult), waits=[e7, n1])
                    norm_ev.append(n2)
                    n1_ev.append(n1)
                meanb_busy = n1_ev[-1]
                rdenb_busy = norm_ev[-1]
                return norm_ev, n1_ev

            x1_ev, _ = layer_norm(xk, t1_ev)

            if debug == "ln" and t == 0:
                st = dma("scalar", s_out, dbg_h[0:1, 0:1024], lnst,
                         waits=[x1_ev[-1]])
                plan.emit("scalar", [st], lambda sc: sc.nop(), [])
                _run(block, plan)
                return nc

            if debug == "x1" and t == 0:
                st = None
                for j in range(KC):
                    st = dma("scalar", s_out, dbg_h[:, j * 512:(j + 1) * 512],
                             xk[j], waits=[x1_ev[j]])

                def fin(sc):
                    return sc.nop()
                plan.emit("scalar", [st], fin, [])
                _run(block, plan)
                return nc

            bank_lg, bwl = pool.acquire()
            g_lg = None
            for k in range(KC):
                g_lg = mm_group(
                    pairs=[(xk[k][:, tc * 128:(tc + 1) * 128],
                            gwts[:, k * 8:(k + 1) * 8]) for tc in range(4)],
                    out_slices=[bank_lg[:, tc * 8:(tc + 1) * 8]
                                for tc in range(4)],
                    starts=[k == 0 and tc == 0 for tc in range(4)],
                    stops=[k == KC - 1 and tc == 3 for tc in range(4)],
                    waits=[x1_ev[k], setup_done] + (bwl if k == 0 else []))
            sig = act(lambda sc, b=bank_lg: sc.activation(
                out=scoresb[:, :], in_=b[:, 0:32], func=AF.Sigmoid),
                waits=[g_lg, scores_busy])
            pool.release(bank_lg, [sig])

            P3 = lambda c0, c1, w: rsc[:, c0:c1].rearrange(
                "p (t c) -> p t c", t=4)
            sc3 = scoresb[:, :].rearrange("p (t c) -> p t c", t=4)
            e = dve(lambda v: v.tensor_tensor(
                out=rsc[:, 0:32], in0=scoresb[:, :], in1=gbrep[:, :],
                op=OP.add), waits=[sig, setup_done, scores_busy])
            sfc3 = P3(0, 32, 8)
            ps3 = P3(32, 80, 12)
            pr = [(0, 1), (2, 3), (0, 2), (1, 3), (0, 3), (1, 2)]
            for gidx in range(2):
                for p_, (a_, b_) in enumerate(pr):
                    c0, c1, co = 4 * gidx + a_, 4 * gidx + b_, 6 * gidx + p_
                    e = dve(lambda v, c0=c0, c1=c1, co=co: v.tensor_tensor(
                        out=ps3[:, :, co:co + 1], in0=sfc3[:, :, c0:c0 + 1],
                        in1=sfc3[:, :, c1:c1 + 1], op=OP.add), waits=[e])
            e = dve(lambda v: v.reduce_max(
                out=rsc[:, 80:88],
                in_=rsc[:, 32:80].rearrange("p (t g k) -> p (t g) k", t=4,
                                            g=2), axis=AX), waits=[e])
            gs3 = P3(80, 88, 2)
            e = dve(lambda v: v.tensor_tensor(
                out=rsc[:, 88:92], in0=gs3[:, :, 0:1].rearrange("p t c -> p (t c)"),
                in1=gs3[:, :, 1:2].rearrange("p t c -> p (t c)"),
                op=OP.is_ge), waits=[e])
            e = dve(lambda v: v.tensor_scalar(
                out=rsc[:, 92:96], in0=rsc[:, 88:92], scalar1=-1.0,
                scalar2=1.0, op0=OP.mult, op1=OP.add), waits=[e])
            m0b = P3(88, 92, 1)
            m1b = P3(92, 96, 1)
            msk3 = P3(96, 128, 8)
            e = dve(lambda v: v.tensor_tensor(
                out=msk3[:, :, 0:4], in0=sfc3[:, :, 0:4],
                in1=m0b.broadcast_to([PART, 4, 4]), op=OP.mult), waits=[e])
            e = dve(lambda v: v.tensor_tensor(
                out=msk3[:, :, 4:8], in0=sfc3[:, :, 4:8],
                in1=m1b.broadcast_to([PART, 4, 4]), op=OP.mult), waits=[e])
            e = dve(lambda v: v.reduce_max(
                out=rsc[:, 128:132], in_=msk3, axis=AX), waits=[e])
            mx1b = P3(128, 132, 1)
            e = dve(lambda v: v.tensor_tensor(
                out=P3(132, 164, 8), in0=msk3,
                in1=mx1b.broadcast_to([PART, 4, 8]), op=OP.is_equal),
                waits=[e])
            e = dve(lambda v: v.scalar_tensor_tensor(
                out=rsc[:, 164:196], in0=rsc[:, 132:164], scalar=-1e30,
                in1=rsc[:, 96:128], op0=OP.mult, op1=OP.add), waits=[e])
            e = dve(lambda v: v.reduce_max(
                out=rsc[:, 196:200], in_=P3(164, 196, 8), axis=AX), waits=[e])
            mx2b = P3(196, 200, 1)
            e = dve(lambda v: v.tensor_tensor(
                out=P3(200, 232, 8), in0=msk3,
                in1=mx2b.broadcast_to([PART, 4, 8]), op=OP.is_ge), waits=[e])
            e = dve(lambda v: v.tensor_tensor(
                out=rsc[:, 232:264], in0=rsc[:, 200:232], in1=scoresb[:, :],
                op=OP.mult), waits=[e])
            e = dve(lambda v: v.reduce_sum(
                out=rsc[:, 264:268], in_=P3(232, 264, 8), axis=AX), waits=[e])
            e = dve(lambda v: v.tensor_scalar(
                out=rsc[:, 264:268], in0=rsc[:, 264:268], scalar1=1e-20,
                scalar2=None, op0=OP.add), waits=[e])
            e = dve(lambda v: v.reciprocal(
                out=rsc[:, 268:272], in_=rsc[:, 264:268]), waits=[e])
            rv3 = P3(268, 272, 1)
            e = dve(lambda v: v.scalar_tensor_tensor(
                out=cwt[:, 0:32].rearrange("p (t c) -> p t c", t=4),
                in0=P3(232, 264, 8), scalar=2.5,
                in1=rv3.broadcast_to([PART, 4, 8]),
                op0=OP.mult, op1=OP.mult), waits=[e])
            cw_ev = [e] * 4
            scores_busy = cw_ev[-1]

            def emit_cwT():
                nonlocal cwTsb_busy
                bank_cw, bwc = pool.acquire()
                g_cw = None
                for tc in range(4):
                    g_cw = mm_group(
                        pairs=[(cwt[:, tc * 8:(tc + 1) * 8], identsb[:, :])],
                        out_slices=[bank_cw[0:8, tc * 128:(tc + 1) * 128]],
                        starts=[tc == 0], stops=[tc == 3],
                        waits=[cw_ev[tc], setup_done] +
                              (bwc if tc == 0 else []),
                        inc_group=(tc == 3))
                e_cwT = dve(lambda v, b=bank_cw: v.tensor_copy(
                    out=cwTsb[:, :], in_=b[0:8, :]), waits=[g_cw, cwTsb_busy])
                pool.release(bank_cw, [e_cwT])
                return e_cwT

            cwslots = [rdenb[:, 0:512], meanb[:, 0:512]]
            cw_busy = [rdenb_busy, meanb_busy]
            rbslots = [rbuf[:, 0:512], rbuf[:, 512:1024],
                       sqbuf[:, 0:512], sqbuf[:, 512:1024]]
            sxh1 = []
            for k in range(KC):
                sxh1.append(gps(lambda g_, d=xh_k[k], s_=xk[k]:
                                g_.tensor_copy(out=d, in_=s_),
                                waits=[x1_ev[k]]))
            stat = lambda h, k, i: wAr[:, h * 4096 + k * 512 + i * 128:
                                       h * 4096 + k * 512 + (i + 1) * 128]
            dn_evac = []
            g0_evac_last = None
            last_upP3 = None
            for g_i in range(2):
                pes = list(range(g_i * GSZ, (g_i + 1) * GSZ))
                rfree = r2s_free
                r2s_ready = {}
                for pl in range(3):
                    pr = g_i * 3 + pl
                    qA, qB = 2 * pl, 2 * pl + 1
                    peA, peB = pes[qA], pes[qB]
                    prev_rsp = rsp_next if pr > 0 else None
                    rsp_next = None
                    rsp = [None] * KC
                    for k in range(4):
                        rsp[k] = (prev_rsp[k] if prev_rsp is not None else
                                  gps(lambda g_, d=xl_k[k], a=xk[k],
                                      b=xhf_k[k]: g_.tensor_tensor(
                                          out=d, in0=a, in1=b,
                                          op=OP.subtract),
                                      waits=[sxh1[k], xl_busy[k % 4]]))
                    banks8 = []
                    bw8 = []
                    for b in range(8):
                        bk, w = pool.acquire()
                        banks8.append(bk)
                        bw8.append(w)
                    upAB = [None] * KC
                    upl_m03 = upl_m47 = None
                    for k in range(KC):
                        pairs = ([(stat(h, k, i), xh_k[k])
                                  for h in range(2) for i in range(4)] +
                                 [(stat(h, k, i), xl_k[k])
                                  for h in range(2) for i in range(4)])
                        upAB[k] = mm_group(
                            pairs=pairs,
                            out_slices=[banks8[h * 4 + i][:, :]
                                        for h in range(2) for i in range(4)] * 2,
                            starts=[k == 0] * 8 + [False] * 8,
                            stops=[False] * 16,
                            waits=[rsp[k], sxh1[k], uph_ld[(pr, k // 4)]] +
                                  (sum(bw8, []) if k == 0 else []))
                        xl_busy[k % 4] = upAB[k]
                        ul = None
                        for h, pe_ in enumerate((peA, peB)):
                            ul = dma("gpsimd", s_ul[k // 4],
                                     wAr[:, h * 4096 + k * 512:
                                         h * 4096 + (k + 1) * 512],
                                     upl_d[pe_, k * 128:(k + 1) * 128, :],
                                     waits=[upAB[k]] if h == 0 else ())
                        if k == 3:
                            upl_m03 = ul
                        if k == KC - 1:
                            upl_m47 = ul
                        if k < 4:
                            rsp[k + 4] = gps(
                                lambda g_, d=xl_k[k + 4], a=xk[k + 4],
                                b=xhf_k[k + 4]: g_.tensor_tensor(
                                    out=d, in0=a, in1=b, op=OP.subtract),
                                waits=[sxh1[k + 4], upAB[k]])
                    if pr + 1 < 6:
                        rsp_next = [gps(lambda g_, d=xl_k[k], a=xk[k],
                                        b=xhf_k[k]: g_.tensor_tensor(
                                            out=d, in0=a, in1=b,
                                            op=OP.subtract),
                                        waits=[sxh1[k], xl_busy[k % 4]])
                                    for k in range(4)]
                    upP3 = [None] * KC
                    g7 = [None] * 8
                    for k in range(4):
                        upP3[k] = mm_group(
                            pairs=[(stat(h, k, i), xh_k[k])
                                   for h in range(2) for i in range(4)],
                            out_slices=[banks8[h * 4 + i][:, :]
                                        for h in range(2) for i in range(4)],
                            starts=[False] * 8, stops=[False] * 8,
                            waits=[upl_m03] if k == 0 else [])
                    for b in range(8):
                        g7[b] = mm_group(
                            pairs=[(stat(b // 4, k, b % 4), xh_k[k])
                                   for k in range(4, KC)],
                            out_slices=[banks8[b][:, :]] * 4,
                            starts=[False] * 4,
                            stops=[False, False, False, True],
                            waits=[upl_m47] if b == 0 else [])
                    for k in range(4, KC):
                        upP3[k] = g7[7]
                    last_upP3 = upP3
                    a1s = [None] * 8
                    slots8 = []
                    for b in range(8):
                        q_ = (qA, qB)[b // 4]
                        slots8.append(bigA[:, (q_ * 4 + b % 4) * 512:
                                           (q_ * 4 + b % 4 + 1) * 512])
                    for b in range(8):
                        a1 = dve(lambda v, bk=banks8[b], sl=slots8[b]:
                                 v.tensor_scalar(out=sl, in0=bk[:, :],
                                                 scalar1=0.0, scalar2=None,
                                                 op0=OP.max),
                                 waits=[g7[b], rfree])
                        a1s[b] = a1
                        pool.release(banks8[b], [a1])
                    if g_i == 0 and pl == 0:
                        e_cwT = emit_cwT()
                    ec_m = {}
                    for h, pe_ in enumerate((peA, peB)):
                        if pe_ < 8:
                            bank_cw2, bwc2 = pool.acquire()
                            gg = mm_group(
                                pairs=[(selcw[:, pe_ * 128:(pe_ + 1) * 128],
                                        cwTsb[:, :])],
                                out_slices=[bank_cw2[:, :]], starts=[True],
                                stops=[True],
                                waits=[e_cwT, setup_done, cwTsb_busy] + bwc2)
                            cwTsb_busy = gg
                            ecv = dve(lambda v, b=bank_cw2, d=cwslots[h]:
                                      v.tensor_copy(out=d, in_=b[:, :]),
                                      waits=[gg, cw_busy[h]])
                            pool.release(bank_cw2, [ecv])
                            ec_m[h] = ecv
                    for b in range(8):
                        q_ = (qA, qB)[b // 4]
                        pe_ = (peA, peB)[b // 4]
                        slot = slots8[b]
                        r = rbslots[b % 4]
                        if pe_ < 8:
                            d1 = dve(lambda v, sl=slot, r=r,
                                     c=cwslots[b // 4]: v.tensor_tensor(
                                         out=r, in0=sl, in1=c, op=OP.mult),
                                     waits=[a1s[b], ec_m[b // 4],
                                            rbuf_busy[b % 4]])
                            cw_busy[b // 4] = d1
                            g1 = gps(lambda g_, sl=slot, r=r: g_.tensor_tensor(
                                out=sl, in0=sl, in1=r, op=OP.mult),
                                waits=[d1])
                            rbuf_busy[b % 4] = g1
                        else:
                            g1 = gps(lambda g_, sl=slot: g_.tensor_tensor(
                                out=sl, in0=sl, in1=sl, op=OP.mult),
                                waits=[a1s[b]])
                        r2s_ready[(q_, b % 4)] = g1
                    if pr + 1 < 6:
                        emit_uph(pr + 1, upP3)
                waH_busy = [last_upP3[KC - 1], last_upP3[KC - 1]]

                if debug == "upa" and t == 0 and g_i == 0:
                    st = None
                    for sl8 in range(KC):
                        st = dma("scalar", s_out,
                                 dbg_h[:, sl8 * 512:(sl8 + 1) * 512],
                                 bigA[:, sl8 * 512:(sl8 + 1) * 512],
                                 waits=[r2s_ready[(sl8 // 4, sl8 % 4)]])
                    plan.emit("scalar", [st], lambda sc: sc.nop(), [])
                    _run(block, plan)
                    return nc

                banks_j = []
                bws_j = []
                for j in range(KC):
                    bank, bw = pool.acquire()
                    banks_j.append(bank)
                    bws_j.append(bw)
                dn_loaded = {}
                dn_last_mm = {}

                def emit_dn_load(qi):
                    pe = pes[qi]
                    h = qi % 2
                    w = [wbH_busy[h]] if qi < 2 else [dn_last_mm[qi - 2]]
                    ld = None
                    for i in range(4):
                        ld = dma("scalar", s_wb[h],
                                 wbufB[:, h * 4096 + i * 1024:h * 4096 + (i + 1) * 1024],
                                 dn_all[pe, i * 128:(i + 1) * 128, :],
                                 waits=w if i == 0 else ())
                    dn_loaded[qi] = ld

                emit_dn_load(0)
                emit_dn_load(1)
                dn_last = {}
                for qi, pe in enumerate(pes):
                    h = qi % 2
                    for j in range(KC):
                        g = mm_group(
                            pairs=[(wbufB[:, h * 4096 + i * 1024 + j * 128:
                                          h * 4096 + i * 1024 + (j + 1) * 128],
                                    bigA[:, (qi * 4 + i) * 512:
                                         (qi * 4 + i + 1) * 512])
                                   for i in range(4)],
                            out_slices=[banks_j[j][:, :]] * 4,
                            starts=[(qi == 0 and i == 0) for i in range(4)],
                            stops=[(qi == GSZ - 1 and i == 3) for i in range(4)],
                            waits=[dn_loaded[qi], r2s_ready[(qi, 3)]] +
                                  [r2s_ready[(qi, i)] for i in range(3)] +
                                  (bws_j[j] if qi == 0 else []))
                        if qi == GSZ - 1:
                            dn_last[j] = g
                    dn_last_mm[qi] = (s_mm, s_mm.n)
                    if qi + 2 < GSZ:
                        emit_dn_load(qi + 2)
                wbH_busy = [dn_last_mm[GSZ - 2], dn_last_mm[GSZ - 1]]
                evs = []
                for j in range(KC):
                    if g_i == 0:
                        ev = dve(lambda v, b=banks_j[j], j=j, xkj=xk[j]:
                                 v.tensor_tensor(
                                     out=abuf[:, j * 512:(j + 1) * 512],
                                     in0=b[:, :], in1=xkj, op=OP.add),
                                 waits=[dn_last[j]])
                    else:
                        ev = dve(lambda v, b=banks_j[j], j=j: v.tensor_tensor(
                            out=abuf[:, j * 512:(j + 1) * 512], in0=b[:, :],
                            in1=abuf[:, j * 512:(j + 1) * 512], op=OP.add),
                            waits=[dn_last[j]])
                    evs.append(ev)
                    pool.release(banks_j[j], [ev])
                if g_i == 0:
                    g0_evac_last = evs[-1]
                dn_evac = evs
                r2s_free = (s_mm, s_mm.n)

                if debug == "dng0" and t == 0 and g_i == 0:
                    st = None
                    for j8 in range(KC):
                        st = dma("scalar", s_out,
                                 dbg_h[:, j8 * 512:(j8 + 1) * 512],
                                 abuf[:, j8 * 512:(j8 + 1) * 512],
                                 waits=[evs[j8]])
                    plan.emit("scalar", [st], lambda sc: sc.nop(), [])
                    _run(block, plan)
                    return nc

            cwb_free_k = list(last_upP3)
            rdenb_busy = cw_busy[0]
            meanb_busy = cw_busy[1]
            sq_busy = [rbuf_busy[2], rbuf_busy[3]]

            xbuf_free = [[(s_mm, s_mm.n), g0_evac_last]] * 2
            if ti + 1 < NT1 + NT2:
                load_x_tile(ti + 1)

            if debug == "ffn" and t == 0:
                st = None
                for j in range(KC):
                    st = dma("scalar", s_out, dbg_h[:, j * 512:(j + 1) * 512],
                             abuf[:, j * 512:(j + 1) * 512], waits=[dn_evac[j]])
                plan.emit("scalar", [st], lambda sc: sc.nop(), [])
                _run(block, plan)
                return nc

            if t + 1 < NT2:
                pend_sxh2, _sl = emit_pend_splits_a(cwb_free_k)

            t2_chunks = [abuf[:, j * 512:(j + 1) * 512] for j in range(KC)]
            x2_ev, n1s = layer_norm(t2_chunks, dn_evac)

            st = None
            for j in range(KC):
                st = dma("scalar", s_out, y_h[j * 128:(j + 1) * 128,
                                              t * 512:(t + 1) * 512],
                         abuf[:, j * 512:(j + 1) * 512], waits=[x2_ev[j]])
            out_done.append(st)
            abuf_busy_j = [st] * KC
            if t + 1 < NT2:
                pend_sxl2, pend_fq = emit_pend_splits_b(pend_sxh2, _sl)

        def fin(sc):
            return sc.nop()
        plan.emit("scalar", [out_done[-1]], fin, [])

        _run(block, plan)
    return nc


def _run(block, plan):
    def runner(name):
        def go(eng):
            last = {}
            for waits, op, incs in plan.steps[name]:
                for w in waits:
                    s, v = w.sv if isinstance(w, FutMark) else w
                    if v > 0 and last.get(s.name, -1) < v:
                        eng.wait_ge(s.h, v)
                        last[s.name] = v
                ins = op(eng)
                for s, k in incs:
                    ins.then_inc(s.h, k)
        return go

    block.sync(runner("sync"))
    block.scalar(runner("scalar"))
    block.vector(runner("vector"))
    block.tensor(runner("tensor"))
    block.gpsimd(runner("gpsimd"))


def _prep_inputs(L0_attn_x, L0_attn_Wq, L0_attn_Wk, L0_attn_Wv, L0_attn_proj,
                 L0_attn_Wo, L0_attn_bo, L0_ffn_gate_weight, L0_ffn_gate_bias,
                 L0_ffn_up_weights, L0_ffn_down_weights, L0_ffn_shared_up_weight,
                 L0_ffn_shared_down_weight):
    f = lambda a: np.ascontiguousarray(np.asarray(a, dtype=np.float32))
    x = f(L0_attn_x)
    wq, wk, wv, wo = f(L0_attn_Wq), f(L0_attn_Wk), f(L0_attn_Wv), f(L0_attn_Wo)
    proj = f(L0_attn_proj)
    bo = f(L0_attn_bo).reshape(1024, 1)
    gw = f(L0_ffn_gate_weight)
    gb = f(L0_ffn_gate_bias).reshape(1, 8)
    upw = f(L0_ffn_up_weights)
    dnw = f(L0_ffn_down_weights)
    sup = f(L0_ffn_shared_up_weight)
    sdn = f(L0_ffn_shared_down_weight)

    upT = np.transpose(upw, (0, 2, 1))
    supT = np.ascontiguousarray(sup.T)
    up_all = np.stack([*upT] + [supT[:, p * 512:(p + 1) * 512] for p in range(4)])
    dnT = np.transpose(dnw, (0, 2, 1))
    sdnT = np.ascontiguousarray(sdn.T)
    dn_all = np.stack([*dnT] + [sdnT[p * 512:(p + 1) * 512, :] for p in range(4)])
    gwT = np.ascontiguousarray(gw.T)
    ident = np.eye(128, dtype=np.float32)
    sel16 = np.zeros((16, 1024), dtype=np.float32)
    for hp in range(8):
        sel16[2 * hp, hp * 128:hp * 128 + 64] = 1.0
        sel16[2 * hp + 1, hp * 128 + 64:hp * 128 + 128] = 1.0
    selcw = np.zeros((8, 1024), dtype=np.float32)
    for e in range(8):
        selcw[e, e * 128:(e + 1) * 128] = 1.0

    in_maps = []
    for core in range(8):
        b, h = core // 2, core % 2
        xT = np.ascontiguousarray(x[b].T)
        in_maps.append({
            "x_full": xT,
            "x_own": np.ascontiguousarray(xT[:, h * 2048:(h + 1) * 2048]),
            "wq": wq, "wk": wk, "wv": wv, "wo": wo,
            "proj": proj, "bo": bo,
            "up_all": np.ascontiguousarray(up_all),
            "dn_all": np.ascontiguousarray(dn_all),
            "gwt": gwT, "gb": gb, "ident": ident, "sel16": sel16,
            "selcw": selcw,
        })
    return in_maps


def kernel(**inputs):
    in_maps = _prep_inputs(**inputs)
    nc = build()
    res = run_bass_kernel_spmd(nc, in_maps, core_ids=list(range(8)))
    ys = [res.results[c]["y"] for c in range(8)]
    out = np.stack([
        np.concatenate([ys[2 * b], ys[2 * b + 1]], axis=1).T for b in range(4)
    ]).astype(np.float32)
    return np.ascontiguousarray(out)
